# version 09
# baseline (speedup 1.0000x reference)
import numpy as np
import ml_dtypes
import contextlib
import concourse.bass as bass
import concourse.mybir as mybir
from concourse.bass_utils import run_bass_kernel_spmd

F32 = mybir.dt.float32
BF16 = mybir.dt.bfloat16
I32 = mybir.dt.int32
AX = mybir.AxisListType.X
OP = mybir.AluOpType
AF = mybir.ActivationFunctionType
BF = ml_dtypes.bfloat16

D = 2048
T = 2048
OWN = 1024
NH = 16
DH = 128
II = 8192
NKC = D // 128
NIC = II // 128
NT = T // 128
NQ = OWN // 128
SM_SCALE = float(1.0 / np.sqrt(DH))
MASK_RAW = float(-1e10 / SM_SCALE)
LN_EPS = 1e-5
RMS_EPS = 1e-6
QEPS = 1e-5

ENGINES = ("te", "ve", "se", "ge", "sy")


class Prog:

    def __init__(self, nc):
        self.nc = nc
        self.stream = {e: [] for e in ENGINES}
        self.sem_ctr = {}
        self.sem_handle = {}
        self.phase_wait = {e: [] for e in ENGINES}

    def make_sems(self, stack, names):
        self._stack = stack
        for n in names:
            self.sem_handle[n] = stack.enter_context(self.nc.semaphore(n))
            self.sem_ctr[n] = 0

    def _sem(self, name):
        if name not in self.sem_handle:
            self.sem_handle[name] = self._stack.enter_context(
                self.nc.semaphore(name))
            self.sem_ctr[name] = 0
        return name

    ENG_SEM = {"te": "T", "ve": "V", "se": "A", "ge": "G"}

    def op(self, eng, fn, after=(), inc="auto", amt=1, key=None):
        waits = list(self.phase_wait[eng]) + [t for t in after if t is not None]
        self.phase_wait[eng] = []
        tok = None
        if inc == "auto":
            inc = self.ENG_SEM.get(eng)
            assert inc is not None, "sync ops must pass inc='L'/'S' + key"
        if inc is not None:
            if inc in ("L", "S"):
                assert key is not None, "DMA op needs a buffer key"
                inc = self._sem(f"{inc}_{key}")
                amt = 16
            self.sem_ctr[inc] += amt
            tok = (inc, self.sem_ctr[inc],
                   "dma" if inc.startswith(("L_", "S_")) else eng)
        self.stream[eng].append((waits, fn, inc, amt))
        return tok

    def store_totals(self):
        return [(n, self.sem_ctr[n], "dma") for n in self.sem_ctr
                if n.startswith("S_")]

    def barrier(self, tokens):
        toks = [t for t in tokens if t is not None]
        for e in ENGINES:
            self.phase_wait[e].extend(toks)

    def play(self, block):
        secs = {"te": block.tensor, "ve": block.vector, "se": block.scalar,
                "ge": block.gpsimd, "sy": block.sync}
        for ename in ENGINES:
            ops = self.stream[ename]
            if not ops:
                continue

            def body(e, ops=ops, ename=ename):
                waited = {}
                for waits, fn, inc, amt in ops:
                    for (sname, val, peng) in waits:
                        if waited.get(sname, 0) >= val:
                            continue
                        e.wait_ge(self.sem_handle[sname], val)
                        waited[sname] = val
                    if fn is None:
                        continue
                    inst = fn(e)
                    if inc is not None:
                        inst.then_inc(self.sem_handle[inc], amt)
            secs[ename](body)


def build_program(debug=()):
    nc = bass.Bass("TRN2", target_bir_lowering=False,
                   detect_race_conditions=False,
                   dynamic_dma_scratch_size=8192)

    def din(name, shape, dt):
        return nc.dram_tensor(name, shape, dt, kind="ExternalInput").ap()

    def dout(name, shape, dt):
        return nc.dram_tensor(name, shape, dt, kind="ExternalOutput").ap()

    dbgset = set(debug)

    def dint(name, shape, dt):
        kind = "ExternalOutput" if name in dbgset else "Internal"
        return nc.dram_tensor(name, shape, dt, kind=kind).ap()

    io = dict(
        x_seq=din("x_seq", [T, D], F32),
        mtri=din("mtri_in", [128, 128], BF16),
        m2c=din("m2c_in", [128, 1], F32),
        wq=din("wq", [D, D], F32), wk=din("wk", [D, D], F32),
        wv=din("wv", [D, D], F32),
        bq=din("bq", [D], F32), bk=din("bk", [D], F32), bv=din("bv", [D], F32),
        wo=din("wo", [D, D], F32), bo=din("bo", [D], F32),
        wg=din("wg", [II, D], F32), wu=din("wu", [II, D], F32),
        wd=din("wd", [D, II], F32), lnw=din("lnw", [II], F32),
        identd=din("ident_in", [128, 128], BF16),
        onesd=din("ones_in", [128, 128], BF16),
        out=dout("out", [OWN, D], F32),
        kT_hbm=dint("kT_hbm", [D, T], BF16),
        vT_hbm=dint("vT_hbm", [D, T], BF16),
        qT_hbm=dint("qT_hbm", [D, OWN], BF16),
        aoT_hbm=dint("aoT_hbm", [D, OWN], BF16),
        y1_hbm=dint("y1_hbm", [NKC, NQ, 128, 128], BF16),
        xnew_hbm=dint("xnew_hbm", [OWN, D], F32),
        y2_hbm=dint("y2_hbm", [NKC, NQ, 128, 128], BF16),
    )
    io["debug"] = set(debug)
    io["dbg"] = {}
    io["dout"] = dout

    with contextlib.ExitStack() as ctx:
        _build(nc, ctx, io)
    return nc


def _build(nc, ctx, io):
    p = Prog(nc)
    p.make_sems(ctx, ["L", "S", "T", "V", "A", "G"])

    def sb(name, shape, dt):
        return ctx.enter_context(nc.sbuf_tensor(name, shape, dt))

    def psum(name, shape, dt):
        return ctx.enter_context(nc.psum_tensor(name, shape, dt))

    ident = sb("ident", [128, 128], BF16)
    ones = sb("ones", [128, 128], BF16)
    mtri_b = sb("mtri_b", [128, 128], BF16)
    m2b = sb("m2b", [128, 1], F32)
    bqc = sb("bqc", [128, NKC], F32)
    bkc = sb("bkc", [128, NKC], F32)
    bvc = sb("bvc", [128, NKC], F32)
    lnwc = sb("lnwc", [128, NIC], F32)
    ssacc = sb("ssacc", [128, 1024], F32)
    biga = sb("biga", [128, NIC * OWN], BF16)
    qx2T = sb("qx2T", [128, NKC * OWN], BF16)
    cmean = sb("cmean", [128, 8], F32)
    thc = sb("thc", [128, 8], F32)
    dqb = sb("dqb", [128, 1024], F32)

    def lnT(k):
        return biga[:, k * T:(k + 1) * T]

    def attnO(k):
        return biga[:, k * OWN:(k + 1) * OWN]

    def hfm(j):
        return biga[:, j * OWN:(j + 1) * OWN]

    def qx2Tc(k):
        return qx2T[:, k * OWN:(k + 1) * OWN]

    psA = psum("psA", [128, 2048], F32)
    psB = psum("psB", [128, 1024], F32)
    psC = psum("psC", [128, 1024], BF16)
    psD = psum("psD", [128, 1024], BF16)

    def bank(i):
        if i < 4:
            return psA[:, i * 512:(i + 1) * 512]
        assert i < 6
        return psB[:, (i - 4) * 512:(i - 3) * 512]

    def bank_bf(i):
        return (psC if i == 6 else psD)[:]

    x = io["x_seq"]

    t_id = p.op("sy", lambda e: e.dma_start(out=ident[:], in_=io["identd"]),
                inc="L", amt=16, key="ident")
    p.op("sy", lambda e: e.dma_start(out=ones[:], in_=io["onesd"]),
         inc="L", amt=16, key="ones")
    t_consts = [t_id]
    for nm2, (dst, src) in (("bqc", (bqc, io["bq"])), ("bkc", (bkc, io["bk"])),
                            ("bvc", (bvc, io["bv"]))):
        t_consts.append(p.op("sy", lambda e, d=dst, s=src: e.dma_start(
            out=d[:], in_=s.rearrange("(c p) -> p c", p=128),
            allow_slow_non_contiguous=True), inc="L", amt=16, key=nm2))
    t_consts.append(p.op("sy", lambda e: e.dma_start(
        out=lnwc[:], in_=io["lnw"].rearrange("(c p) -> p c", p=128),
        allow_slow_non_contiguous=True), inc="L", amt=16, key="lnwc"))
    t_mtri = p.op("sy", lambda e: e.dma_start(out=mtri_b[:], in_=io["mtri"]),
                  inc="L", amt=16, key="mtri")
    t_m2 = p.op("sy", lambda e: e.dma_start(out=m2b[:], in_=io["m2c"]),
                inc="L", amt=16, key="m2b")

    ph1 = contextlib.ExitStack()

    def sb1(name, shape, dt):
        return ph1.enter_context(nc.sbuf_tensor(name, shape, dt))

    xbuf = [sb1(f"xbuf{i}", [128, D], F32) for i in range(2)]
    xsq = sb1("xsq", [128, D], BF16)
    lntile = [sb1(f"lntile{i}", [128, D], BF16) for i in range(2)]
    st1 = sb1("st1", [128, 16], F32)
    xload = [None, None]
    lnfree = [None, None]
    stat_free = [None, None]
    sqrt_free = [None, None]
    bank_free = {6: None, 7: None}
    t_lnT_done = None

    for tt in range(NT):
        bi = tt % 2
        c0 = 8 * bi
        tl = p.op("sy", lambda e, b=bi, t=tt: e.dma_start(
            out=xbuf[b][:], in_=x[t * 128:(t + 1) * 128, :]),
            after=[xload[bi]] + t_consts, inc="L", amt=16, key=f"xbuf{bi}")
        tv1 = p.op("ve", lambda e, b=bi, c=c0: e.reduce_sum(
            out=st1[:, c:c + 1], in_=xbuf[b][:], axis=AX), after=[tl])
        ta1 = p.op("se", lambda e, b=bi, c=c0: e.activation(
            out=xsq[:], in_=xbuf[b][:], func=AF.Square,
            accum_out=st1[:, c + 1:c + 2]),
            after=[tl, stat_free[bi]])
        tmean = p.op("ve", lambda e, c=c0: e.tensor_scalar(
            out=st1[:, c + 2:c + 3], in0=st1[:, c:c + 1], scalar1=1.0 / D,
            scalar2=None, op0=OP.mult), after=[tv1])
        tm2 = p.op("ve", lambda e, c=c0: e.tensor_tensor(
            out=st1[:, c + 3:c + 4], in0=st1[:, c + 2:c + 3],
            in1=st1[:, c + 2:c + 3], op=OP.mult), after=[tmean])
        tvar = p.op("ve", lambda e, c=c0: e.scalar_tensor_tensor(
            out=st1[:, c + 4:c + 5], in0=st1[:, c + 1:c + 2], scalar=1.0 / D,
            in1=st1[:, c + 3:c + 4], op0=OP.mult, op1=OP.subtract),
            after=[ta1, tm2])
        stat_free[bi] = tvar
        teps = p.op("ve", lambda e, c=c0: e.tensor_scalar(
            out=st1[:, c + 5:c + 6], in0=st1[:, c + 4:c + 5], scalar1=LN_EPS,
            scalar2=None, op0=OP.add), after=[tvar])
        tv2 = p.op("ve", lambda e, c=c0: e.reciprocal(
            out=st1[:, c + 6:c + 7], in_=st1[:, c + 5:c + 6]), after=[teps])
        ta2 = p.op("se", lambda e, c=c0: e.activation(
            out=st1[:, c + 7:c + 8], in_=st1[:, c + 6:c + 7], func=AF.Sqrt),
            after=[tv2, sqrt_free[bi]])
        tv3 = p.op("ve", lambda e, b=bi, c=c0: e.tensor_scalar(
            out=lntile[b][:], in0=xbuf[b][:], scalar1=st1[:, c + 2:c + 3],
            scalar2=st1[:, c + 7:c + 8], op0=OP.subtract, op1=OP.mult),
            after=[ta2, lnfree[bi]])
        xload[bi] = tv3
        sqrt_free[bi] = tv3
        last_tr = None
        for kg in range(4):
            pb = 6 + (kg % 2)
            tts = None
            for k4 in range(4):
                k = kg * 4 + k4
                tts = p.op("te", lambda e, b=bi, kk=k, k4=k4, pb=pb: e.transpose(
                    out=bank_bf(pb)[:, k4 * 128:(k4 + 1) * 128],
                    in_=lntile[b][:, kk * 128:(kk + 1) * 128],
                    identity=ident[:]),
                    after=([tv3, t_id, bank_free[pb]] if k4 == 0 else []),
                    inc="T" if k4 == 3 else None)
            base = kg * 4 * T + tt * 128
            dst3 = biga[:, base:base + 4 * T].rearrange(
                "p (c q) -> p c q", c=4)[:, :, 0:128]
            tev = p.op("ve", lambda e, d=dst3, pb=pb: e.tensor_copy(
                out=d, in_=bank_bf(pb)[:, 0:512]), after=[tts], inc="V")
            bank_free[pb] = tev
            last_tr = tts
            t_lnT_done = tev
        lnfree[bi] = last_tr

    p.barrier([t_lnT_done, bank_free[6], bank_free[7]])

    if "lnT_dbg" in io["debug"]:
        lnd = io["dout"]("lnT_dbg", [D, T], BF16)
        for k in range(NKC):
            p.op("sy", lambda e, k=k: e.dma_start(
                out=lnd[k * 128:(k + 1) * 128, :], in_=lnT(k)), inc="S", amt=16, key="dbg")

    ph1.close()

    ph2 = contextlib.ExitStack()

    def sb2(name, shape, dt):
        return ph2.enter_context(nc.sbuf_tensor(name, shape, dt))

    wcol = [sb2(f"wcol{i}", [128, D], F32) for i in range(2)]
    wbf = [sb2(f"wbf{i}", [128, D], BF16) for i in range(2)]
    stage = [sb2(f"stage{i}", [128, T], BF16) for i in range(2)]
    wload = [None, None]
    wbf_free = [None, None]
    stage_free = [None, None]
    grp_free = {"A": None, "B": None}
    grp_free2 = {"A": None, "B": None}
    t_p2_done = []

    projs = [
        ("q", io["wq"], bqc, io["qT_hbm"], OWN),
        ("k", io["wk"], bkc, io["kT_hbm"], T),
        ("v", io["wv"], bvc, io["vT_hbm"], T),
    ]
    ji = 0
    for (pname, wdram, bcol, ohbm, tspan) in projs:
        nb = tspan // 512
        for j in range(NKC):
            par = ji % 2
            ji += 1
            wsrc = wdram[:, j * 128:(j + 1) * 128].rearrange(
                "(kc p) c -> p kc c", p=128)
            tl = p.op("sy", lambda e, par=par, s=wsrc: e.dma_start(
                out=wcol[par][:].rearrange("p (kc c) -> p kc c", kc=NKC),
                in_=s), after=[wload[par]], inc="L", amt=16, key=f"wcol{par}")
            tc = p.op("ge", lambda e, par=par: e.tensor_copy(
                out=wbf[par][:], in_=wcol[par][:]),
                after=[tl, wbf_free[par]], inc="G")
            wload[par] = tc
            if nb == 4:
                bset, gk = [0, 1, 2, 3], "A"
            else:
                bset, gk = ([0, 1], "A") if par == 0 else ([4, 5], "B")
            tmm = None
            for k in range(NKC):
                for tb in range(nb):
                    tmm = p.op("te", lambda e, par=par, k=k, tb=tb, bset=bset: e.matmul(
                        out=bank(bset[tb]),
                        lhsT=wbf[par][:, k * 128:(k + 1) * 128],
                        rhs=lnT(k)[:, tb * 512:(tb + 1) * 512],
                        start=(k == 0), stop=(k == NKC - 1)),
                        after=([tc, grp_free[gk], grp_free2[gk]]
                               if k == 0 and tb == 0 else
                               [tc] if k == 0 else []),
                        inc="T" if (k == NKC - 1 and tb == nb - 1) else None)
            wbf_free[par] = tmm
            tevs2 = []
            for tb in range(nb):
                eng = "ve" if (tb % 2 == 0) else "se"
                if eng == "ve":
                    tevs2.append(p.op("ve", lambda e, par=par, tb=tb, bset=bset, j=j, bcol=bcol: e.tensor_scalar(
                        out=stage[par][:, tb * 512:(tb + 1) * 512],
                        in0=bank(bset[tb]), scalar1=bcol[:, j:j + 1],
                        scalar2=None, op0=OP.add),
                        after=[tmm, stage_free[par]], inc="V"))
                else:
                    tevs2.append(p.op("se", lambda e, par=par, tb=tb, bset=bset, j=j, bcol=bcol: e.activation(
                        out=stage[par][:, tb * 512:(tb + 1) * 512],
                        in_=bank(bset[tb]), func=AF.Identity,
                        bias=bcol[:, j:j + 1]),
                        after=[tmm, stage_free[par]], inc="A"))
            grp_free[gk] = tevs2[-2] if len(tevs2) >= 2 else tevs2[-1]
            grp_free2[gk] = tevs2[-1]
            tst = p.op("sy", lambda e, par=par, j=j, ohbm=ohbm, tspan=tspan: e.dma_start(
                out=ohbm[j * 128:(j + 1) * 128, :],
                in_=stage[par][:, 0:tspan]),
                after=tevs2[-2:], inc="S", amt=16, key=f"stage{par}")
            stage_free[par] = tst
            t_p2_done.append(tst)

    p.barrier(t_p2_done[-2:])

    ph2.close()

    ph3 = contextlib.ExitStack()

    def sb3(name, shape, dt):
        return ph3.enter_context(nc.sbuf_tensor(name, shape, dt))

    kh = [sb3(f"kh{i}", [128, T], BF16) for i in range(2)]
    vTh0 = sb3("vTh0", [128, T], BF16)
    vTh = [vTh0, vTh0]
    qh = [sb3(f"qh{i}", [128, OWN], BF16) for i in range(2)]
    vtok = sb3("vtok", [128, T], BF16)
    pbuf = sb3("pbuf", [128, T], BF16)
    pT = sb3("pT", [128, NT * 512], BF16)
    smax = sb3("smax", [128, 6], F32)
    ssum = sb3("ssum", [128, 4], F32)

    pT3 = pT[:].rearrange("p (kt q) -> p kt q", kt=NT)
    if "vtok_dbg" in io["debug"]:
        io["dbg"]["vtok_dbg"] = io["dout"]("vtok_dbg", [128, T], BF16)
        io["dbg"]["vtok2_dbg"] = io["dout"]("vtok2_dbg", [128, T], BF16)
        io["dbg"]["pT_dbg"] = io["dout"]("pT_dbg", [128, 2 * NT * 512], BF16)
    t_kh_used = [None, None]
    t_vTh_used = [None, None]
    t_qh_used = [None, None]
    t_vtok_evac = None
    t_pv = [None, None]
    t_pv_guard = None
    t_oevac_prev = None
    t_exp_prev = None
    t_soft_prev = None
    t_ptr_last_prev = None
    bank_free3 = {6: None, 7: None}

    for h in range(NH):
        par = h % 2
        tkl = p.op("sy", lambda e, par=par, h=h: e.dma_start(
            out=kh[par][:], in_=io["kT_hbm"][h * 128:(h + 1) * 128, :]),
            after=[t_kh_used[par]], inc="L", key=f"kh{par}")
        tvl = p.op("sy", lambda e, par=par, h=h: e.dma_start(
            out=vTh[par][:], in_=io["vT_hbm"][h * 128:(h + 1) * 128, :]),
            after=[t_vTh_used[0], t_vTh_used[1]], inc="L", key="vTh")
        tql = p.op("sy", lambda e, par=par, h=h: e.dma_start(
            out=qh[par][:], in_=io["qT_hbm"][h * 128:(h + 1) * 128, :]),
            after=[t_qh_used[par]], inc="L", key=f"qh{par}")
        t_vtr = None
        for kg in range(4):
            pb = 6 + (kg % 2)
            tts = None
            for k4 in range(4):
                kt = kg * 4 + k4
                tts = p.op("te", lambda e, par=par, kt=kt, k4=k4, pb=pb: e.transpose(
                    out=bank_bf(pb)[:, k4 * 128:(k4 + 1) * 128],
                    in_=vTh[par][:, kt * 128:(kt + 1) * 128],
                    identity=ident[:]),
                    after=([tvl, bank_free3[pb]] if k4 == 0 else []),
                    inc="T" if k4 == 3 else None)
            base = kg * 4 * 128
            if kg % 2 == 0:
                tev = p.op("ve", lambda e, base=base, pb=pb: e.tensor_copy(
                    out=vtok[:, base:base + 512], in_=bank_bf(pb)[:, 0:512]),
                    after=[tts, t_pv[0], t_pv[1]] if kg == 0 else [tts])
            else:
                tev = p.op("se", lambda e, base=base, pb=pb: e.activation(
                    out=vtok[:, base:base + 512], in_=bank_bf(pb)[:, 0:512],
                    func=AF.Identity),
                    after=[tts, t_pv[0], t_pv[1]] if kg == 1 else [tts])
            bank_free3[pb] = tev
            t_vtr = tev
        t_vTh_used[0] = t_vtr
        t_vTh_used[1] = t_vtr
        t_vtok_evac = t_vtr
        if h == 0 and "vtok_dbg" in io["debug"]:
            p.op("sy", lambda e: e.dma_start(
                out=io["dbg"]["vtok_dbg"], in_=vtok[:]),
                after=[t_vtr], inc="S", key="dbg")

        t_qk_last = None
        for qb in range(2):
            t_ptr_evacs = []
            for q4 in range(4):
                qt = qb * 4 + q4
                vis1 = list(range(qt + 1))
                nv1 = qt + 1
                tqk = None
                first = True
                for b in range(4):
                    if b < 2:
                        lo = b * 4
                        if lo > qt:
                            continue
                        w = (min(qt, lo + 3) - lo + 1) * 128
                        has_diag = lo <= qt <= lo + 3
                        deps = [tkl, tql, t_exp_prev] if first else []
                        if first and h == 0 and qt == 0:
                            deps = deps + [t_mtri, t_m2]
                        tq_ = p.op("te", lambda e, par=par, qt=qt, b=b, lo=lo, w=w, has_diag=has_diag: e.matmul(
                            out=psA[:, lo * 128:lo * 128 + w],
                            lhsT=qh[par][:, qt * 128:(qt + 1) * 128],
                            rhs=kh[par][:, lo * 128:lo * 128 + w],
                            start=True, stop=not has_diag),
                            after=deps,
                            inc=None if has_diag else "T")
                        if has_diag:
                            tqk = p.op("te", lambda e, qt=qt: e.matmul(
                                out=psA[:, qt * 128:(qt + 1) * 128],
                                lhsT=ident[:], rhs=mtri_b[:],
                                start=False, stop=True), inc="T")
                        else:
                            tqk = tq_
                        first = False
                    else:
                        tqk = p.op("te", lambda e, par=par, qt=qt, b=b: e.matmul(
                            out=bank(b),
                            lhsT=qh[par][:, qt * 128:(qt + 1) * 128],
                            rhs=kh[par][:, b * 512:(b + 1) * 512],
                            start=True, stop=True), inc="T" if b == 3 else None)
                t_qk_last = tqk
                ea = nv1 * 128
                tmxA = p.op("ve", lambda e, ea=ea: e.reduce_max(
                    out=smax[:, 0:1], in_=psA[:, 0:ea], axis=AX),
                    after=[tqk, t_soft_prev])
                tmxB = p.op("ve", lambda e: e.reduce_max(
                    out=smax[:, 2:3], in_=psA[:, 1024:2048], axis=AX))
                tmxC = p.op("ve", lambda e: e.tensor_tensor(
                    out=smax[:, 0:1], in0=smax[:, 0:1], in1=smax[:, 2:3],
                    op=OP.max), after=[tmxA, tmxB])
                tnb = p.op("ve", lambda e: e.tensor_scalar(
                    out=smax[:, 1:2], in0=smax[:, 0:1], scalar1=-SM_SCALE,
                    scalar2=None, op0=OP.mult), after=[tmxC])
                tnb2a = p.op("ve", lambda e: e.tensor_tensor(
                    out=smax[:, 4:5], in0=m2b[:], in1=smax[:, 0:1],
                    op=OP.subtract), after=[tmxC])
                tnb2 = p.op("ve", lambda e: e.tensor_scalar(
                    out=smax[:, 3:4], in0=smax[:, 4:5], scalar1=SM_SCALE,
                    scalar2=None, op0=OP.mult), after=[tnb2a])
                t_expA = p.op("se", lambda e, ea=ea: e.activation(
                    out=pbuf[:, 0:ea], in_=psA[:, 0:ea], func=AF.Exp,
                    bias=smax[:, 1:2], scale=SM_SCALE,
                    accum_out=ssum[:, 0:1]),
                    after=[tnb, t_ptr_last_prev, t_soft_prev])
                t_exp = p.op("se", lambda e: e.activation(
                    out=pbuf[:, 1024:2048], in_=psA[:, 1024:2048],
                    func=AF.Exp, bias=smax[:, 3:4], scale=SM_SCALE,
                    accum_out=ssum[:, 2:3]), after=[tnb2])
                t_exp_prev = t_exp
                tsc = p.op("ve", lambda e: e.tensor_tensor(
                    out=ssum[:, 0:1], in0=ssum[:, 0:1], in1=ssum[:, 2:3],
                    op=OP.add), after=[t_expA, t_exp])
                trc = p.op("ve", lambda e: e.reciprocal(
                    out=ssum[:, 1:2], in_=ssum[:, 0:1]), after=[tsc])
                tpnA = p.op("se", lambda e, ea=ea: e.activation(
                    out=pbuf[:, 0:ea], in_=pbuf[:, 0:ea], func=AF.Identity,
                    scale=ssum[:, 1:2]), after=[trc])
                tpn = p.op("se", lambda e: e.activation(
                    out=pbuf[:, 1024:2048], in_=pbuf[:, 1024:2048],
                    func=AF.Identity, scale=ssum[:, 1:2]), after=[tpnA])
                t_soft_prev = tpn
                vis_groups = []
                for lo0 in range(0, nv1, 4):
                    vis_groups.append((lo0, min(4, nv1 - lo0)))
                vis_groups += [(8, 4), (12, 4)]
                tptr = None
                for gi, (lo0, n4) in enumerate(vis_groups):
                    pb = 6 + (gi % 2)
                    tts = None
                    for k4 in range(n4):
                        kt = lo0 + k4
                        tts = p.op("te", lambda e, kt=kt, k4=k4, pb=pb: e.transpose(
                            out=bank_bf(pb)[:, k4 * 128:(k4 + 1) * 128],
                            in_=pbuf[:, kt * 128:(kt + 1) * 128],
                            identity=ident[:]),
                            after=([tpnA if lo0 < 8 else tpn,
                                    bank_free3[pb]] if k4 == 0 else []),
                            inc="T" if k4 == n4 - 1 else None)
                    dst3 = pT3[:, lo0:lo0 + n4, q4 * 128:(q4 + 1) * 128]
                    if gi % 2 == 0:
                        tev = p.op("ve", lambda e, d=dst3, pb=pb, n4=n4: e.tensor_copy(
                            out=d, in_=bank_bf(pb)[:, 0:n4 * 128]),
                            after=[tts, t_pv_guard] if gi == 0 else [tts])
                    else:
                        tev = p.op("se", lambda e, d=dst3, pb=pb, n4=n4: e.activation(
                            out=d, in_=bank_bf(pb)[:, 0:n4 * 128],
                            func=AF.Identity),
                            after=[tts, t_pv_guard] if gi == 1 else [tts])
                    bank_free3[pb] = tev
                    t_ptr_evacs.append(tev)
                    tptr = tts
                t_ptr_last_prev = tptr
            pv_list = [(kt, max(0, kt - qb * 4) * 128)
                       for kt in range(qb * 4 + 4)]
            pv_list += [(kt, 0) for kt in range(8, 16)]
            n_pv = len(pv_list)
            tpv = None
            for i, (kt, off) in enumerate(pv_list):
                tpv = p.op("te", lambda e, kt=kt, qb=qb, off=off, i=i, n_pv=n_pv: e.matmul(
                    out=psB[:, qb * 512 + off:(qb + 1) * 512],
                    lhsT=vtok[:, kt * 128:(kt + 1) * 128],
                    rhs=pT[:, kt * 512 + off:(kt + 1) * 512],
                    start=(i == 0), stop=(i == n_pv - 1)),
                    after=([t_vtok_evac, t_oevac_prev] + t_ptr_evacs)
                    if i == 0 else [],
                    inc="T" if i == n_pv - 1 else None)
            t_pv[qb] = tpv
            t_pv_guard = tpv
            if h == 0 and "vtok_dbg" in io["debug"]:
                p.op("sy", lambda e, qb=qb: e.dma_start(
                    out=io["dbg"]["pT_dbg"][:, qb * 8192:(qb + 1) * 8192],
                    in_=pT[:]), after=t_ptr_evacs[-1:], inc="S", key="dbg")
                p.op("sy", lambda e, qb=qb: e.dma_start(
                    out=io["dbg"]["vtok2_dbg"], in_=vtok[:]),
                    after=[tpv], inc="S", key="dbg")
        toe = p.op("ve", lambda e, h=h: e.tensor_copy(
            out=attnO(h), in_=psB[:]), after=[t_pv[0], t_pv[1]])
        t_oevac_prev = toe
        t_kh_used[par] = t_qk_last
        t_qh_used[par] = t_qk_last

    p.barrier([t_oevac_prev])

    if "aoT_dbg" in io["debug"]:
        aod = io["dout"]("aoT_dbg", [D, OWN], BF16)
        for k in range(NKC):
            p.op("sy", lambda e, k=k: e.dma_start(
                out=aod[k * 128:(k + 1) * 128, :], in_=attnO(k)), inc="S", key="dbg")

    ph3.close()

    ph4 = contextlib.ExitStack()

    def sb4(name, shape, dt):
        return ph4.enter_context(nc.sbuf_tensor(name, shape, dt))

    worow = sb4("worow", [128, 1024], F32)
    onesf = sb4("onesf", [1, 128], F32)
    st4 = sb4("st4", [128, 16], F32)
    xtile = sb4("xtile", [128, D], F32)
    xnb = sb4("xnb", [128, D], F32)
    ln2b = sb4("ln2b", [128, D], F32)
    vb4 = sb4("vb4", [128, D], F32)
    gb4 = sb4("gb4", [128, D], I32)
    bo_row = worow[0:1, :]
    wob = [biga[:, (16 + 2 * k) * OWN:(18 + 2 * k) * OWN] for k in range(NKC)]
    bo_b = biga[:, 48 * OWN:50 * OWN]
    cb1 = biga[:, 50 * OWN:52 * OWN]
    cb2 = biga[:, 52 * OWN:54 * OWN]
    qxb = biga[:, 54 * OWN:56 * OWN]
    sqd = biga[:, 56 * OWN:58 * OWN]

    qx2T3 = qx2T[:].rearrange("p (k t) -> p k t", k=NKC)

    t_of = p.op("ve", lambda e: e.memset(onesf[:], 1.0))
    tmmb = None
    for hb in range(2):
        t_bol = p.op("sy", lambda e, hb=hb: e.dma_start(
            out=bo_row, in_=io["bo"][None, hb * 1024:(hb + 1) * 1024]),
            after=[tmmb], inc="L", key="worow")
        for i in range(2):
            tmmb = p.op("te", lambda e, hb=hb, i=i: e.matmul(
                out=psA[:, hb * 1024 + i * 512:hb * 1024 + (i + 1) * 512],
                lhsT=onesf[:], rhs=bo_row[:, i * 512:(i + 1) * 512],
                start=True, stop=True),
                after=[t_bol, t_of] if i == 0 else [],
                inc="T" if i == 1 else None)
    t_bob = p.op("ve", lambda e: e.tensor_copy(out=bo_b, in_=psA[:]),
                 after=[tmmb])

    t_qx2ev = []
    bank_free4 = {6: None, 7: None}
    t_psA_free = t_bob
    t_worow_free = tmmb
    t_xst = None
    t_xtile_used = None
    t_stat_used = None
    t_xnb_used = None
    t_st4sq_used = None
    t_qxb_free = None
    tcache = [None] * NKC

    for qt in range(NQ):
        tmm_last = None
        for k in range(NKC):
            if qt == 0:
                for hb in range(2):
                    twl = p.op("sy", lambda e, k=k, hb=hb: e.dma_start(
                        out=worow[:],
                        in_=io["wo"][k * 128:(k + 1) * 128,
                                     hb * 1024:(hb + 1) * 1024]),
                        after=[t_worow_free], inc="L", key="worow")
                    twc = p.op("ge", lambda e, k=k, hb=hb: e.tensor_copy(
                        out=wob[k][:, hb * 1024:(hb + 1) * 1024],
                        in_=worow[:]), after=[twl])
                    t_worow_free = twc
                tcache[k] = t_worow_free
            for tb in range(4):
                deps = []
                if tb == 0 and qt == 0:
                    deps = [tcache[k]] + ([t_psA_free] if k == 0 else [])
                elif tb == 0 and k == 0:
                    deps = [t_psA_free]
                tmm_last = p.op("te", lambda e, k=k, qt=qt, tb=tb: e.matmul(
                    out=bank(tb),
                    lhsT=attnO(k)[:, qt * 128:(qt + 1) * 128],
                    rhs=wob[k][:, tb * 512:(tb + 1) * 512],
                    start=(k == 0), stop=(k == NKC - 1)),
                    after=deps,
                    inc="T" if (k == NKC - 1 and tb == 3) else None)
        txl = p.op("sy", lambda e, qt=qt: e.dma_start(
            out=xtile[:], in_=x[qt * 128:(qt + 1) * 128, :]),
            after=[t_xtile_used], inc="L", key="xtile")
        txb = p.op("ve", lambda e: e.tensor_tensor(
            out=xtile[:], in0=xtile[:], in1=bo_b, op=OP.add),
            after=[txl, t_bob])
        p.op("ve", lambda e: e.tensor_tensor(
            out=xnb[:, 0:1024], in0=psA[:, 0:1024],
            in1=xtile[:, 0:1024], op=OP.add),
            after=[tmm_last, txb, t_xnb_used, t_xst])
        tev = p.op("ve", lambda e: e.tensor_tensor(
            out=xnb[:, 1024:2048], in0=psA[:, 1024:2048],
            in1=xtile[:, 1024:2048], op=OP.add))
        t_psA_free = tev
        t_xtile_used = tev
        t_xst = p.op("sy", lambda e, qt=qt: e.dma_start(
            out=io["xnew_hbm"][qt * 128:(qt + 1) * 128, :], in_=xnb[:]),
            after=[tev], inc="S", key="xnb")
        tv1 = p.op("ve", lambda e: e.reduce_sum(
            out=st4[:, 0:1], in_=xnb[:], axis=AX), after=[tev, t_stat_used])
        ta1 = p.op("se", lambda e: e.activation(
            out=sqd[:, 0:1024], in_=xnb[:, 0:1024], func=AF.Square,
            accum_out=st4[:, 1:2]), after=[tev, t_st4sq_used])
        ta1b = p.op("se", lambda e: e.activation(
            out=sqd[:, 1024:2048], in_=xnb[:, 1024:2048], func=AF.Square,
            accum_out=st4[:, 2:3]), after=[ta1])
        tmean = p.op("ve", lambda e: e.tensor_scalar(
            out=st4[:, 3:4], in0=st4[:, 0:1], scalar1=1.0 / D,
            scalar2=None, op0=OP.mult), after=[tv1])
        tm2 = p.op("ve", lambda e: e.tensor_tensor(
            out=st4[:, 4:5], in0=st4[:, 3:4], in1=st4[:, 3:4], op=OP.mult),
            after=[tmean])
        tsq = p.op("ve", lambda e: e.tensor_tensor(
            out=st4[:, 5:6], in0=st4[:, 1:2], in1=st4[:, 2:3], op=OP.add),
            after=[ta1b, tm2])
        t_st4sq_used = tsq
        tvar = p.op("ve", lambda e: e.scalar_tensor_tensor(
            out=st4[:, 6:7], in0=st4[:, 5:6], scalar=1.0 / D,
            in1=st4[:, 4:5], op0=OP.mult, op1=OP.subtract), after=[tsq])
        teps = p.op("ve", lambda e: e.tensor_scalar(
            out=st4[:, 7:8], in0=st4[:, 6:7], scalar1=LN_EPS,
            scalar2=None, op0=OP.add), after=[tvar])
        trc2 = p.op("ve", lambda e: e.reciprocal(
            out=st4[:, 8:9], in_=st4[:, 7:8]), after=[teps])
        tsqr = p.op("se", lambda e: e.activation(
            out=st4[:, 9:10], in_=st4[:, 8:9], func=AF.Sqrt), after=[trc2])
        tln = p.op("ve", lambda e: e.tensor_scalar(
            out=ln2b[:], in0=xnb[:], scalar1=st4[:, 3:4],
            scalar2=st4[:, 9:10], op0=OP.subtract, op1=OP.mult),
            after=[tsqr])
        t_xnb_used = tln
        tam = p.op("ve", lambda e: e.reduce_max(
            out=st4[:, 10:11], in_=ln2b[:], axis=AX,
            apply_absolute_value=True), after=[tln])
        tt1 = p.op("ve", lambda e: e.tensor_scalar(
            out=st4[:, 11:12], in0=st4[:, 10:11], scalar1=QEPS,
            scalar2=None, op0=OP.max), after=[tam])
        tqr = p.op("ve", lambda e: e.reciprocal(
            out=st4[:, 12:13], in_=st4[:, 11:12]), after=[tt1])
        tdq = p.op("ve", lambda e: e.tensor_scalar(
            out=st4[:, 13:14], in0=st4[:, 11:12], scalar1=1.0 / 127.0,
            scalar2=None, op0=OP.mult), after=[tt1])
        t_stat_used = tdq
        tvv = p.op("ve", lambda e: e.tensor_scalar(
            out=vb4[:], in0=ln2b[:], scalar1=st4[:, 12:13], scalar2=127.0,
            op0=OP.mult, op1=OP.mult), after=[tqr, tln])
        tvc = p.op("ve", lambda e: e.tensor_scalar(
            out=ln2b[:], in0=vb4[:], scalar1=-128.0, scalar2=127.0,
            op0=OP.max, op1=OP.min), after=[tvv])
        tg = p.op("ve", lambda e: e.tensor_copy(
            out=gb4[:], in_=ln2b[:]), after=[tvc])
        tc1 = p.op("ve", lambda e: e.scalar_tensor_tensor(
            out=cb1, in0=gb4[:], scalar=0.5, in1=ln2b[:],
            op0=OP.add, op1=OP.is_le), after=[tg])
        tc2 = p.op("ve", lambda e: e.scalar_tensor_tensor(
            out=cb2, in0=gb4[:], scalar=-0.5, in1=ln2b[:],
            op0=OP.add, op1=OP.is_ge), after=[tc1])
        tu1 = p.op("ve", lambda e: e.scalar_tensor_tensor(
            out=vb4[:], in0=cb1, scalar=1.0, in1=gb4[:],
            op0=OP.bypass, op1=OP.add), after=[tc1])
        tu2 = p.op("ve", lambda e: e.scalar_tensor_tensor(
            out=ln2b[:], in0=cb2, scalar=-1.0, in1=vb4[:],
            op0=OP.mult, op1=OP.add), after=[tu1, tc2])
        tqx = p.op("ve", lambda e: e.tensor_scalar(
            out=qxb, in0=ln2b[:], scalar1=st4[:, 13:14], scalar2=None,
            op0=OP.mult), after=[tu2, tdq, t_qxb_free])
        for kg in range(4):
            pb = 6 + (kg % 2)
            tts = None
            for k4 in range(4):
                kk = kg * 4 + k4
                tts = p.op("te", lambda e, kk=kk, k4=k4, pb=pb: e.transpose(
                    out=bank_bf(pb)[:, k4 * 128:(k4 + 1) * 128],
                    in_=qxb[:, kk * 128:(kk + 1) * 128],
                    identity=ident[:]),
                    after=([tqx, bank_free4[pb]] if k4 == 0 else []),
                    inc="T" if k4 == 3 else None)
            dst3 = qx2T3[:, kg * 4:(kg + 1) * 4, qt * 128:(qt + 1) * 128]
            tev2 = p.op("ve", lambda e, d=dst3, pb=pb: e.tensor_copy(
                out=d, in_=bank_bf(pb)[:, 0:512]), after=[tts])
            bank_free4[pb] = tev2
            t_qx2ev.append(tev2)
        t_qxb_free = tts

    p.barrier(t_qx2ev[-2:] + [t_xst])

    if "qx2T_dbg" in io["debug"]:
        qxd = io["dout"]("qx2T_dbg", [D, OWN], BF16)
        for k in range(NKC):
            p.op("sy", lambda e, k=k: e.dma_start(
                out=qxd[k * 128:(k + 1) * 128, :], in_=qx2Tc(k)), inc="S", key="dbg")

    ph4.close()

    ph45 = contextlib.ExitStack()

    def sb45(name, shape, dt):
        return ph45.enter_context(nc.sbuf_tensor(name, shape, dt))

    wpb = [sb45(f"wpb{i}", [128, 2048], F32) for i in range(2)]
    wstat16 = sb45("wstat16", [128, 192 * 16], F32)
    wstat = sb45("wstat", [128, 192], F32)
    onesf128 = sb45("onesf128", [128, 128], F32)
    wsum = sb45("wsum", [128, 192], F32)

    blocks = []
    for mi, (wdram, nrc) in enumerate(((io["wg"], 64), (io["wu"], 64),
                                       (io["wd"], 16))):
        if mi < 2:
            for rr in range(nrc):
                blocks.append((mi, wdram[rr * 128:(rr + 1) * 128, :], mi * 64 + rr))
        else:
            for rr in range(nrc):
                for qq2 in range(4):
                    blocks.append((mi, wdram[rr * 128:(rr + 1) * 128,
                                             qq2 * 2048:(qq2 + 1) * 2048],
                                   128 + rr * 4 + qq2))
    t_wp_used = [None, None]
    t_abs = []
    for bi2, (mi, src, col) in enumerate(blocks):
        par = bi2 % 2
        twl = p.op("sy", lambda e, par=par, src=src: e.dma_start(
            out=wpb[par][:], in_=src), after=[t_wp_used[par]], inc="L", key=f"wpb{par}")
        tr = p.op("ve", lambda e, par=par, col=col: e.tensor_reduce(
            out=wstat16[:, col * 16:(col + 1) * 16],
            in_=wpb[par][:].rearrange("p (a b) -> p a b", a=16),
            axis=AX, op=OP.add, apply_absolute_value=True), after=[twl])
        t_wp_used[par] = tr
        t_abs.append(tr)
    tst2 = p.op("ve", lambda e: e.tensor_reduce(
        out=wstat[:], in_=wstat16[:].rearrange("p (a b) -> p a b", a=192),
        axis=AX, op=OP.add), after=[t_abs[-1]])
    tcast = p.op("ve", lambda e: e.memset(onesf128[:], 1.0), after=[tst2])
    tsum = p.op("te", lambda e: e.matmul(
        out=psB[:, 0:192], lhsT=onesf128[:], rhs=wstat[:],
        start=True, stop=True), after=[tcast, tst2])
    tsev = p.op("ve", lambda e: e.tensor_copy(out=wsum[:], in_=psB[:, 0:192]),
                after=[tsum])
    tprev = tsev
    for mi, (c0, ncols, numel) in enumerate(
            ((0, 64, II * D), (64, 64, II * D), (128, 64, D * II))):
        t1_ = p.op("ve", lambda e, mi=mi, c0=c0, ncols=ncols: e.reduce_sum(
            out=cmean[:, mi + 4:mi + 5], in_=wsum[:, c0:c0 + ncols], axis=AX),
            after=[tprev])
        t2_ = p.op("ve", lambda e, mi=mi, numel=numel: e.tensor_scalar(
            out=cmean[:, mi:mi + 1], in0=cmean[:, mi + 4:mi + 5],
            scalar1=1.0 / numel, scalar2=QEPS, op0=OP.mult, op1=OP.max),
            after=[t1_])
        t3_ = p.op("ve", lambda e, mi=mi: e.tensor_scalar(
            out=thc[:, mi:mi + 1], in0=cmean[:, mi:mi + 1], scalar1=0.5,
            scalar2=None, op0=OP.mult), after=[t2_])
        tprev = p.op("ve", lambda e, mi=mi: e.tensor_scalar(
            out=thc[:, mi + 4:mi + 5], in0=thc[:, mi:mi + 1], scalar1=-1.0,
            scalar2=None, op0=OP.mult), after=[t3_])
    t_scales = tprev
    p.barrier([t_scales])
    ph45.close()

    ph5 = contextlib.ExitStack()

    def sb5(name, shape, dt):
        return ph5.enter_context(nc.sbuf_tensor(name, shape, dt))

    wr5 = [[sb5(f"wr5_{m}_{i}", [128, 1024], F32) for i in range(2)]
           for m in range(2)]
    gslab = sb5("gslab", [128, 2048], BF16)
    uslab = sb5("uslab", [128, 2048], BF16)
    qu_b = sb5("qu_b", [128, 2048], BF16)
    qv_b = sb5("qv_b", [128, 2048], BF16)
    qr_b = [sb5(f"qr_b{i}", [128, 2048], BF16) for i in range(2)]
    gsilb = sb5("gsilb", [128, 1024], BF16)
    h2b = sb5("h2b", [128, 1024], BF16)

    t_wr_used = [[None, None], [None, None]]
    t_slab_used = [None, None]
    t_gu_ev = None
    t_h2 = None
    t_hfm = [None] * NIC
    bank_free5 = {6: None, 7: None}
    t_gu_mm = None
    t_ss_mm = None
    t_ss_acc = None

    t_qr_used = [None, None]
    t_sil_prev = None
    pend_ss = None

    def emit_ss(th2_tok):
        nonlocal t_ss_mm, t_ss_acc
        for tb in range(2):
            t_ss_mm = p.op("te", lambda e, tb=tb: e.matmul(
                out=psB[:, tb * 512:(tb + 1) * 512], lhsT=ones[:],
                rhs=h2b[:, tb * 512:(tb + 1) * 512],
                start=True, stop=True),
                after=[th2_tok, t_ss_acc] if tb == 0 else [],
                inc="T" if tb == 1 else None)
        if t_ss_acc is None:
            t_ss_acc = p.op("ve", lambda e: e.tensor_copy(
                out=ssacc[:], in_=psB[:]), after=[t_ss_mm])
        else:
            t_ss_acc = p.op("ve", lambda e: e.tensor_tensor(
                out=ssacc[:], in0=ssacc[:], in1=psB[:], op=OP.add),
                after=[t_ss_mm, t_ss_acc])

    def emit_quant5(j):
        toks = [None, None]
        for m, wdram in enumerate((io["wg"], io["wu"])):
            tls = []
            for hh in range(2):
                par = (2 * j + hh) % 2
                tls.append(p.op("sy", lambda e, m=m, j=j, hh=hh, par=par, w=wdram: e.dma_start(
                    out=wr5[m][par][:],
                    in_=w[j * 128:(j + 1) * 128, hh * 1024:(hh + 1) * 1024]),
                    after=[t_wr_used[m][par]], inc="L", key=f"wr5_{m}_{par}"))
            tq = None
            for hh in range(2):
                par = (2 * j + hh) % 2
                tu = p.op("ve", lambda e, m=m, hh=hh, par=par: e.tensor_scalar(
                    out=qu_b[:, hh * 1024:(hh + 1) * 1024], in0=wr5[m][par][:],
                    scalar1=thc[:, m:m + 1], scalar2=None, op0=OP.is_ge),
                    after=[tls[hh]] + ([t_qr_used[m]] if hh == 0 else []))
                tv = p.op("ve", lambda e, m=m, hh=hh, par=par: e.tensor_scalar(
                    out=qv_b[:, hh * 1024:(hh + 1) * 1024], in0=wr5[m][par][:],
                    scalar1=thc[:, m + 4:m + 5], scalar2=None, op0=OP.is_le),
                    after=[tu])
                tq = p.op("ve", lambda e, m=m, hh=hh: e.tensor_tensor(
                    out=qr_b[m][:, hh * 1024:(hh + 1) * 1024],
                    in0=qu_b[:, hh * 1024:(hh + 1) * 1024],
                    in1=qv_b[:, hh * 1024:(hh + 1) * 1024], op=OP.subtract),
                    after=[tv])
                t_wr_used[m][par] = tq
            toks[m] = tq
        return toks

    pend_quant = emit_quant5(0)
    for j in range(NIC):
        slabs = (gslab, uslab)
        tquant = [None, None]
        quant_toks = pend_quant
        if j + 1 < NIC:
            pend_quant = emit_quant5(j + 1)
        for m in range(2):
            tq = quant_toks[m]
            for kg in range(4):
                pb = 6 + (kg % 2)
                tts = None
                for k4 in range(4):
                    kk = kg * 4 + k4
                    tts = p.op("te", lambda e, m=m, kk=kk, k4=k4, pb=pb: e.transpose(
                        out=bank_bf(pb)[:, k4 * 128:(k4 + 1) * 128],
                        in_=qr_b[m][:, kk * 128:(kk + 1) * 128],
                        identity=ident[:]),
                        after=([tq, bank_free5[pb], t_slab_used[m]]
                               if k4 == 0 else []),
                        inc="T" if k4 == 3 else None)
                tev = p.op("se", lambda e, m=m, kg=kg, pb=pb, slab=slabs[m]: e.activation(
                    out=slab[:, kg * 512:(kg + 1) * 512],
                    in_=bank_bf(pb)[:, 0:512], func=AF.Identity),
                    after=[tts])
                bank_free5[pb] = tev
            t_qr_used[m] = tts
            t_slab_used[m] = None
            tquant[m] = (tq, tev)
        tmmg = None
        for k in range(NKC):
            for m in range(2):
                slab = (gslab, uslab)[m]
                for tb in range(2):
                    deps = []
                    if k == 0 and tb == 0 and m == 0:
                        deps = [tquant[0][1], t_sil_prev]
                    elif k == 0 and tb == 0 and m == 1:
                        deps = [tquant[1][1], t_gu_ev]
                    tmmg = p.op("te", lambda e, m=m, k=k, tb=tb, slab=slab: e.matmul(
                        out=psA[:, m * 1024 + tb * 512: m * 1024 + (tb + 1) * 512],
                        lhsT=slab[:, k * 128:(k + 1) * 128],
                        rhs=qx2Tc(k)[:, tb * 512:(tb + 1) * 512],
                        start=(k == 0), stop=(k == NKC - 1)),
                        after=deps,
                        inc="T" if (k == NKC - 1 and m == 1 and tb == 1) else None)
        t_slab_used[0] = tmmg
        t_slab_used[1] = tmmg
        if pend_ss is not None:
            emit_ss(pend_ss)
        tsil = p.op("se", lambda e: e.activation(
            out=gsilb[:], in_=psA[:, 0:1024], func=AF.Silu,
            scale=cmean[:, 0:1]), after=[tmmg, t_h2])
        t_sil_prev = tsil
        thf = p.op("ve", lambda e, j=j: e.scalar_tensor_tensor(
            out=hfm(j), in0=psA[:, 1024:2048], scalar=cmean[:, 1:2],
            in1=gsilb[:], op0=OP.mult, op1=OP.mult), after=[tsil])
        t_gu_ev = thf
        t_hfm[j] = thf
        th2 = p.op("se", lambda e, j=j: e.activation(
            out=h2b[:], in_=hfm(j), func=AF.Square), after=[thf, t_ss_mm])
        t_h2 = th2
        pend_ss = th2
    emit_ss(pend_ss)

    p.barrier([t_ss_acc, t_gu_ev])
    ph5.close()

    ph6 = contextlib.ExitStack()

    def sb6(name, shape, dt):
        return ph6.enter_context(nc.sbuf_tensor(name, shape, dt))

    rmsb = sb6("rmsb", [128, 1024], F32)
    qsb = sb6("qsb", [128, 1024], F32)
    rqb = sb6("rqb", [128, 1024], F32)
    hnb = sb6("hnb", [128, 1024], BF16)
    amax8 = sb6("amax8", [128, 16], F32)
    qsr = sb6("qsr", [8, 256], F32)
    qsrow_qs = ssacc[0:1, 0:1024]
    qsrow_dq = dqb[0:1, 0:1024]
    identf = sb6("identf", [128, 128], F32)
    onesf2 = sb6("onesf2", [1, 128], F32)
    vb = sb6("vb", [128, 1024], F32)
    vcb = sb6("vcb", [128, 1024], F32)
    gb = sb6("gb", [128, 1024], I32)
    u1b = sb6("u1b", [128, 1024], F32)
    c1b = sb6("c1b", [128, 1024], BF16)
    c2b = sb6("c2b", [128, 1024], BF16)

    tid2 = p.op("ve", lambda e: e.tensor_copy(out=identf[:], in_=ident[:]))
    tof2 = p.op("ve", lambda e: e.memset(onesf2[:], 1.0))
    trms1 = p.op("ve", lambda e: e.tensor_scalar(
        out=rmsb[:], in0=ssacc[:], scalar1=1.0 / II, scalar2=RMS_EPS,
        op0=OP.mult, op1=OP.add))
    trms0 = trms1
    trms2 = p.op("ve", lambda e: e.reciprocal(out=rmsb[:], in_=rmsb[:]),
                 after=[trms1])
    trms3 = p.op("se", lambda e: e.activation(
        out=rmsb[:], in_=rmsb[:], func=AF.Sqrt), after=[trms2])

    t_am = None
    t_hn_used = None
    bank_free6 = {6: None, 7: None}
    for j in range(NIC):
        thn = p.op("ve", lambda e, j=j: e.scalar_tensor_tensor(
            out=hnb[:], in0=hfm(j), scalar=lnwc[:, j:j + 1], in1=rmsb[:],
            op0=OP.mult, op1=OP.mult), after=[trms3, t_hn_used])
        pb = 6 + (j % 2)
        tts = None
        for k4 in range(8):
            tts = p.op("te", lambda e, k4=k4, pb=pb: e.transpose(
                out=bank_bf(pb)[:, k4 * 128:(k4 + 1) * 128],
                in_=hnb[:, k4 * 128:(k4 + 1) * 128],
                identity=ident[:]),
                after=([thn, bank_free6[pb]] if k4 == 0 else []),
                inc="T" if k4 == 7 else None)
        t_hn_used = tts
        trd = p.op("ve", lambda e, pb=pb: e.tensor_reduce(
            out=amax8[:, 8:16], in_=bank_bf(pb)[:, 0:1024].rearrange(
                "p (q c) -> p q c", q=8),
            axis=AX, op=OP.max, apply_absolute_value=True),
            after=[tts, t_am])
        bank_free6[pb] = trd
        if j == 0:
            t_am = p.op("ve", lambda e: e.tensor_copy(
                out=amax8[:, 0:8], in_=amax8[:, 8:16]), after=[trd])
        else:
            t_am = p.op("ve", lambda e: e.tensor_tensor(
                out=amax8[:, 0:8], in0=amax8[:, 0:8], in1=amax8[:, 8:16],
                op=OP.max), after=[trd, t_am])
    tq1 = p.op("ve", lambda e: e.tensor_scalar(
        out=amax8[:, 8:16], in0=amax8[:, 0:8], scalar1=QEPS, scalar2=None,
        op0=OP.max), after=[t_am])
    tq2 = p.op("ve", lambda e: e.reciprocal(
        out=amax8[:, 0:8], in_=amax8[:, 8:16]), after=[tq1])
    tq3 = p.op("ve", lambda e: e.tensor_scalar(
        out=amax8[:, 0:8], in0=amax8[:, 0:8], scalar1=127.0, scalar2=None,
        op0=OP.mult), after=[tq2])
    tq4 = p.op("ve", lambda e: e.tensor_scalar(
        out=amax8[:, 8:16], in0=amax8[:, 8:16], scalar1=1.0 / 127.0,
        scalar2=cmean[:, 2:3], op0=OP.mult, op1=OP.mult), after=[tq3])
    ttr1 = p.op("te", lambda e: e.transpose(
        out=psA[0:8, 0:128], in_=amax8[:, 0:8], identity=identf[:]),
        after=[tq3, tq4, tid2, t_hn_used, bank_free6[6], bank_free6[7]],
        inc="T")
    tev1 = p.op("ve", lambda e: e.tensor_copy(out=qsr[0:8, 0:128],
                                              in_=psA[0:8, 0:128]),
                after=[ttr1])
    ttr2 = p.op("te", lambda e: e.transpose(
        out=psA[0:8, 128:256], in_=amax8[:, 8:16], identity=identf[:]),
        after=[tev1], inc="T")
    tev2 = p.op("ve", lambda e: e.tensor_copy(out=qsr[0:8, 128:256],
                                              in_=psA[0:8, 128:256]),
                after=[ttr2])
    tgr1 = p.op("sy", lambda e: e.dma_start(
        out=qsrow_qs.rearrange("o (q c) -> o q c", q=8),
        in_=qsr[0:8, 0:128]), after=[tev1, trms1], inc="L", key="qsrow1")
    tgr2 = p.op("sy", lambda e: e.dma_start(
        out=qsrow_dq.rearrange("o (q c) -> o q c", q=8),
        in_=qsr[0:8, 128:256]), after=[tev2, trms1], inc="L", key="qsrow2")
    tb1 = None
    for tt in range(8):
        tb1 = p.op("te", lambda e, tt=tt: e.matmul(
            out=psB[:, tt * 128:(tt + 1) * 128], lhsT=onesf2[:],
            rhs=qsrow_qs[:, tt * 128:(tt + 1) * 128], start=True, stop=True),
            after=[tgr1, tof2, trms0] if tt == 0 else [],
            inc="T" if tt == 7 else None)
    tqsb = p.op("ve", lambda e: e.tensor_copy(out=qsb[:], in_=psB[:]),
                after=[tb1])
    tb2 = None
    for tt in range(8):
        tb2 = p.op("te", lambda e, tt=tt: e.matmul(
            out=psB[:, tt * 128:(tt + 1) * 128], lhsT=onesf2[:],
            rhs=qsrow_dq[:, tt * 128:(tt + 1) * 128],
            start=True, stop=True),
            after=[tgr2, tqsb] if tt == 0 else [],
            inc="T" if tt == 7 else None)
    tdqb = p.op("ve", lambda e: e.tensor_copy(out=dqb[:], in_=psB[:]),
                after=[tb2])
    trqb = p.op("ve", lambda e: e.tensor_tensor(
        out=rqb[:], in0=rmsb[:], in1=qsb[:], op=OP.mult), after=[tqsb, trms3])

    t_ch = None
    for j in range(NIC):
        tvv = p.op("ve", lambda e, j=j: e.scalar_tensor_tensor(
            out=vb[:], in0=hfm(j), scalar=lnwc[:, j:j + 1], in1=rqb[:],
            op0=OP.mult, op1=OP.mult), after=[trqb, t_ch])
        tvc = p.op("ge", lambda e: e.tensor_scalar(
            out=vcb[:], in0=vb[:], scalar1=-128.0, scalar2=127.0,
            op0=OP.max, op1=OP.min), after=[tvv])
        tg = p.op("ge", lambda e: e.tensor_copy(out=gb[:], in_=vcb[:]),
                  after=[tvc])
        tc1 = p.op("ve", lambda e: e.scalar_tensor_tensor(
            out=c1b[:], in0=gb[:], scalar=0.5, in1=vcb[:], op0=OP.add,
            op1=OP.is_le), after=[tg])
        tc2 = p.op("ve", lambda e: e.scalar_tensor_tensor(
            out=c2b[:], in0=gb[:], scalar=-0.5, in1=vcb[:], op0=OP.add,
            op1=OP.is_ge), after=[tc1])
        tu1 = p.op("ve", lambda e: e.scalar_tensor_tensor(
            out=u1b[:], in0=c1b[:], scalar=1.0, in1=gb[:], op0=OP.bypass,
            op1=OP.add), after=[tc1])
        t_ch = p.op("ve", lambda e, j=j: e.scalar_tensor_tensor(
            out=hfm(j), in0=c2b[:], scalar=-1.0, in1=u1b[:], op0=OP.mult,
            op1=OP.add), after=[tu1, tc2])

    if "ffn_dbg" in io["debug"]:
        fd = io["dout"]("ffn_dbg", [128, 4224], F32)
        p.op("sy", lambda e: e.dma_start(out=fd[:, 0:1024], in_=rmsb[:]),
             after=[t_ch], inc="S", key="dbg")
        p.op("sy", lambda e: e.dma_start(out=fd[:, 1024:2048], in_=qsb[:]),
             after=[t_ch], inc="S", key="dbg")
        p.op("sy", lambda e: e.dma_start(out=fd[:, 2048:3072], in_=dqb[:]),
             after=[t_ch], inc="S", key="dbg")
        p.op("sy", lambda e: e.dma_start(out=fd[:, 3072:3088],
                                         in_=amax8[:, 0:16]),
             after=[t_ch], inc="S", key="dbg")
        p.op("sy", lambda e: e.dma_start(out=fd[:, 3088:3096],
                                         in_=cmean[:, 0:8]),
             after=[t_ch], inc="S", key="dbg")
        p.op("sy", lambda e: e.dma_start(out=fd[:, 3096:3104],
                                         in_=thc[:, 0:8]),
             after=[t_ch], inc="S", key="dbg")
        p.op("sy", lambda e: e.dma_start(out=fd[:, 3104:4128],
                                         in_=ssacc[:]), after=[t_ch], inc="S", key="dbg")

    if "qh_dbg" in io["debug"]:
        qhd = io["dout"]("qh_dbg", [II, OWN], BF16)
        for j in range(NIC):
            p.op("sy", lambda e, j=j: e.dma_start(
                out=qhd[j * 128:(j + 1) * 128, :], in_=hfm(j)),
                after=[t_ch], inc="S", key="dbg")

    p.barrier([t_ch])
    ph6.close()

    ph7 = contextlib.ExitStack()

    def sb7(name, shape, dt):
        return ph7.enter_context(nc.sbuf_tensor(name, shape, dt))

    wdq = [sb7(f"wdq{i}", [128, 2048], F32) for i in range(2)]
    du_b = sb7("du_b", [128, 1024], BF16)
    dv_b = sb7("dv_b", [128, 1024], BF16)
    dr_b = sb7("dr_b", [128, 2048], BF16)
    dslab = sb7("dslab", [128, 8192], BF16)
    y2b = sb7("y2b", [128, 1024], BF16)
    ystage = sb7("ystage", [128, 1024], BF16)

    t_wd_used = [None, None]
    t_dq_used = None
    t_y2mm = [None, None]
    t_yst = None
    t_y2ev = None
    bank_free7 = {6: None, 7: None}
    t_slab7_used = None
    y2_hbm = io["y2_hbm"]

    for j in range(NKC):
        jp = j % 2
        tlast_ev = None
        for qq2 in range(4):
            par = (j * 4 + qq2) % 2
            tld = p.op("sy", lambda e, j=j, qq2=qq2, par=par: e.dma_start(
                out=wdq[par][:],
                in_=io["wd"][j * 128:(j + 1) * 128,
                             qq2 * 2048:(qq2 + 1) * 2048]),
                after=[t_wd_used[par]], inc="L", key=f"wdq{par}")
            tdr = None
            for hh in range(2):
                tdu = p.op("ve", lambda e, par=par, hh=hh: e.tensor_scalar(
                    out=du_b[:], in0=wdq[par][:, hh * 1024:(hh + 1) * 1024],
                    scalar1=thc[:, 2:3], scalar2=None, op0=OP.is_ge),
                    after=[tld, t_dq_used] if hh == 0 else [])
                tdv = p.op("ve", lambda e, par=par, hh=hh: e.tensor_scalar(
                    out=dv_b[:], in0=wdq[par][:, hh * 1024:(hh + 1) * 1024],
                    scalar1=thc[:, 6:7], scalar2=None, op0=OP.is_le),
                    after=[tdu])
                tdr = p.op("ve", lambda e, hh=hh: e.tensor_tensor(
                    out=dr_b[:, hh * 1024:(hh + 1) * 1024], in0=du_b[:],
                    in1=dv_b[:], op=OP.subtract), after=[tdv])
            t_wd_used[par] = tdr
            t_dq_used = tdr
            for kg in range(4):
                pb = 6 + (kg % 2)
                tts = None
                for k4 in range(4):
                    kk = kg * 4 + k4
                    tts = p.op("te", lambda e, kk=kk, k4=k4, pb=pb: e.transpose(
                        out=bank_bf(pb)[:, k4 * 128:(k4 + 1) * 128],
                        in_=dr_b[:, kk * 128:(kk + 1) * 128],
                        identity=ident[:]),
                        after=([tdr, bank_free7[pb], t_slab7_used]
                               if k4 == 0 else []),
                        inc="T" if k4 == 3 else None)
                tlast_ev = p.op("se", lambda e, qq2=qq2, kg=kg, pb=pb: e.activation(
                    out=dslab[:, qq2 * 2048 + kg * 512:qq2 * 2048 + (kg + 1) * 512],
                    in_=bank_bf(pb)[:, 0:512], func=AF.Identity), after=[tts])
                bank_free7[pb] = tlast_ev
        tdm = None
        for k in range(NIC):
            for tb in range(2):
                tdm = p.op("te", lambda e, j=j, jp=jp, k=k, tb=tb: e.matmul(
                    out=psA[:, jp * 1024 + tb * 512: jp * 1024 + (tb + 1) * 512],
                    lhsT=dslab[:, k * 128:(k + 1) * 128],
                    rhs=hfm(k)[:, tb * 512:(tb + 1) * 512],
                    start=(k == 0), stop=(k == NIC - 1)),
                    after=[tlast_ev, t_y2mm[jp]] if (k == 0 and tb == 0) else [],
                    inc="T" if (k == NIC - 1 and tb == 1) else None)
        t_slab7_used = tdm
        tyv = p.op("ve", lambda e, jp=jp: e.tensor_tensor(
            out=y2b[:], in0=psA[:, jp * 1024:(jp + 1) * 1024], in1=dqb[:],
            op=OP.mult), after=[tdm, t_yst])
        t_y2mm[jp] = tyv
        pb = 6 + (j % 2)
        tts = None
        for k4 in range(8):
            tts = p.op("te", lambda e, k4=k4, pb=pb: e.transpose(
                out=bank_bf(pb)[:, k4 * 128:(k4 + 1) * 128],
                in_=y2b[:, k4 * 128:(k4 + 1) * 128],
                identity=ident[:]),
                after=([tyv, bank_free7[pb]] if k4 == 0 else []),
                inc="T" if k4 == 7 else None)
        tyev = p.op("se", lambda e, pb=pb: e.activation(
            out=ystage[:], in_=bank_bf(pb)[:, 0:1024], func=AF.Identity),
            after=[tts, t_yst])
        bank_free7[pb] = tyev
        t_y2ev = tyev
        t_yst = p.op("sy", lambda e, j=j: e.dma_start(
            out=y2_hbm[j].rearrange("q r c -> r q c"),
            in_=ystage[:].rearrange("p (q c) -> p q c", q=8)),
            after=[tyev], inc="S", key="ystage")

    p.barrier([t_yst])
    ph7.close()

    ph8 = contextlib.ExitStack()

    def sb8(name, shape, dt):
        return ph8.enter_context(nc.sbuf_tensor(name, shape, dt))

    y2g = [sb8(f"y2g{i}", [128, 2048], BF16) for i in range(2)]
    xr = [sb8(f"xr{i}", [128, 2048], F32) for i in range(2)]
    outb = [sb8(f"outb{i}", [128, 2048], F32) for i in range(2)]
    t_fin = []
    t_outst = [None, None]
    for qt in range(NQ):
        par = qt % 2
        tg1 = p.op("sy", lambda e, qt=qt, par=par: e.dma_start(
            out=y2g[par][:].rearrange("p (jj c) -> p jj c", jj=NKC),
            in_=y2_hbm[:, qt].rearrange("jj r c -> r jj c")),
            after=[t_outst[par]], inc="L", key=f"y2g{par}")
        tg2 = p.op("sy", lambda e, qt=qt, par=par: e.dma_start(
            out=xr[par][:], in_=io["xnew_hbm"][qt * 128:(qt + 1) * 128, :]),
            after=[t_xst], inc="L", key=f"xr{par}")
        tadd = p.op("ve", lambda e, par=par: e.tensor_tensor(
            out=outb[par][:], in0=xr[par][:], in1=y2g[par][:], op=OP.add),
            after=[tg1, tg2])
        tst = p.op("sy", lambda e, qt=qt, par=par: e.dma_start(
            out=io["out"][qt * 128:(qt + 1) * 128, :], in_=outb[par][:]),
            after=[tadd], inc="S", key=f"outb{par}")
        t_outst[par] = tst
        t_fin.append(tst)
    ph8.close()

    p.op("sy", None, after=p.store_totals(), inc=None)

    p.play(ctx.enter_context(nc.Block()))
    return p



def _shard_inputs(inputs):
    xq = np.asarray(inputs["L0_attn_query"], dtype=np.float32)
    eye = np.eye(128, dtype=BF)
    onesm = np.ones((128, 128), dtype=BF)
    ii = np.arange(128)
    mtri = np.where(ii[None, :] <= ii[:, None], 0.0, MASK_RAW).astype(BF)
    shared = dict(
        wq=np.asarray(inputs["L0_attn_w_q"], np.float32),
        wk=np.asarray(inputs["L0_attn_w_k"], np.float32),
        wv=np.asarray(inputs["L0_attn_w_v"], np.float32),
        bq=np.asarray(inputs["L0_attn_b_q"], np.float32),
        bk=np.asarray(inputs["L0_attn_b_k"], np.float32),
        bv=np.asarray(inputs["L0_attn_b_v"], np.float32),
        wo=np.asarray(inputs["L0_attn_w_o"], np.float32),
        bo=np.asarray(inputs["L0_attn_b_o"], np.float32),
        wg=np.asarray(inputs["L0_ffn_w_gate"], np.float32),
        wu=np.asarray(inputs["L0_ffn_w_up"], np.float32),
        wd=np.asarray(inputs["L0_ffn_w_down"], np.float32),
        lnw=np.asarray(inputs["L0_ffn_ln_weight"], np.float32),
        ident_in=eye, ones_in=onesm,
    )
    in_maps = []
    for c in range(8):
        b, half = c // 2, c % 2
        own = slice(half * OWN, (half + 1) * OWN)
        oth = slice((1 - half) * OWN, (2 - half) * OWN)
        xs = np.concatenate([xq[b, own], xq[b, oth]], axis=0)
        im = dict(shared)
        im["x_seq"] = np.ascontiguousarray(xs)
        im["mtri_in"] = mtri
        im["m2c_in"] = np.full((128, 1), 0.0 if half == 1 else MASK_RAW,
                               np.float32)
        in_maps.append(im)
    return in_maps


_NC_CACHE = {}


def kernel(**inputs):
    key = "main"
    if key not in _NC_CACHE:
        _NC_CACHE[key] = build_program()
    nc = _NC_CACHE[key]
    in_maps = _shard_inputs(inputs)
    res = run_bass_kernel_spmd(nc, in_maps, core_ids=list(range(8)))
    full = np.empty((4, T, D), dtype=np.float32)
    for c in range(8):
        b, half = c // 2, c % 2
        full[b, half * OWN:(half + 1) * OWN] = res.results[c]["out"]
    return full


# version 10
# speedup vs baseline: 1.1318x; 1.1318x over previous
import numpy as np
import ml_dtypes
import contextlib
import concourse.bass as bass
import concourse.mybir as mybir
from concourse.bass_utils import run_bass_kernel_spmd

F32 = mybir.dt.float32
BF16 = mybir.dt.bfloat16
I32 = mybir.dt.int32
AX = mybir.AxisListType.X
OP = mybir.AluOpType
AF = mybir.ActivationFunctionType
BF = ml_dtypes.bfloat16

D = 2048
T = 2048
OWN = 1024
NH = 16
DH = 128
II = 8192
NKC = D // 128
NIC = II // 128
NT = T // 128
NQ = OWN // 128
SM_SCALE = float(1.0 / np.sqrt(DH))
MASK_RAW = float(-1e10 / SM_SCALE)
LN_EPS = 1e-5
RMS_EPS = 1e-6
QEPS = 1e-5

ENGINES = ("te", "ve", "se", "ge", "sy")


class Prog:

    def __init__(self, nc):
        self.nc = nc
        self.stream = {e: [] for e in ENGINES}
        self.sem_ctr = {}
        self.sem_handle = {}
        self.phase_wait = {e: [] for e in ENGINES}

    def make_sems(self, stack, names):
        self._stack = stack
        for n in names:
            self.sem_handle[n] = stack.enter_context(self.nc.semaphore(n))
            self.sem_ctr[n] = 0

    def _sem(self, name):
        if name not in self.sem_handle:
            self.sem_handle[name] = self._stack.enter_context(
                self.nc.semaphore(name))
            self.sem_ctr[name] = 0
        return name

    ENG_SEM = {"te": "T", "ve": "V", "se": "A", "ge": "G"}

    def op(self, eng, fn, after=(), inc="auto", amt=1, key=None):
        waits = list(self.phase_wait[eng]) + [t for t in after if t is not None]
        self.phase_wait[eng] = []
        tok = None
        if inc == "auto":
            inc = self.ENG_SEM.get(eng)
            assert inc is not None, "sync ops must pass inc='L'/'S' + key"
        if inc is not None:
            if inc in ("L", "S"):
                assert key is not None, "DMA op needs a buffer key"
                inc = self._sem(f"{inc}_{key}")
                amt = 16
            self.sem_ctr[inc] += amt
            tok = (inc, self.sem_ctr[inc],
                   "dma" if inc.startswith(("L_", "S_")) else eng)
        self.stream[eng].append((waits, fn, inc, amt))
        return tok

    def store_totals(self):
        return [(n, self.sem_ctr[n], "dma") for n in self.sem_ctr
                if n.startswith("S_")]

    def barrier(self, tokens):
        toks = [t for t in tokens if t is not None]
        for e in ENGINES:
            self.phase_wait[e].extend(toks)

    def play(self, block):
        secs = {"te": block.tensor, "ve": block.vector, "se": block.scalar,
                "ge": block.gpsimd, "sy": block.sync}
        for ename in ENGINES:
            ops = self.stream[ename]
            if not ops:
                continue

            def body(e, ops=ops, ename=ename):
                waited = {}
                for waits, fn, inc, amt in ops:
                    for (sname, val, peng) in waits:
                        if waited.get(sname, 0) >= val:
                            continue
                        e.wait_ge(self.sem_handle[sname], val)
                        waited[sname] = val
                    if fn is None:
                        continue
                    inst = fn(e)
                    if inc is not None:
                        inst.then_inc(self.sem_handle[inc], amt)
            secs[ename](body)


def build_program(debug=()):
    nc = bass.Bass("TRN2", target_bir_lowering=False,
                   detect_race_conditions=False,
                   dynamic_dma_scratch_size=8192)

    def din(name, shape, dt):
        return nc.dram_tensor(name, shape, dt, kind="ExternalInput").ap()

    def dout(name, shape, dt):
        return nc.dram_tensor(name, shape, dt, kind="ExternalOutput").ap()

    dbgset = set(debug)

    def dint(name, shape, dt):
        kind = "ExternalOutput" if name in dbgset else "Internal"
        return nc.dram_tensor(name, shape, dt, kind=kind).ap()

    io = dict(
        x_seq=din("x_seq", [T, D], F32),
        mtri=din("mtri_in", [128, 128], BF16),
        m2c=din("m2c_in", [128, 1], F32),
        wq=din("wq", [D, D], F32), wk=din("wk", [D, D], F32),
        wv=din("wv", [D, D], F32),
        bq=din("bq", [D], F32), bk=din("bk", [D], F32), bv=din("bv", [D], F32),
        wo=din("wo", [D, D], F32), bo=din("bo", [D], F32),
        wg=din("wg", [II, D], F32), wu=din("wu", [II, D], F32),
        wd=din("wd", [D, II], F32), lnw=din("lnw", [II], F32),
        identd=din("ident_in", [128, 128], BF16),
        onesd=din("ones_in", [128, 128], BF16),
        out=dout("out", [OWN, D], F32),
        kT_hbm=dint("kT_hbm", [D, T], BF16),
        vT_hbm=dint("vT_hbm", [D, T], BF16),
        qT_hbm=dint("qT_hbm", [D, OWN], BF16),
        aoT_hbm=dint("aoT_hbm", [D, OWN], BF16),
        y1_hbm=dint("y1_hbm", [NKC, NQ, 128, 128], BF16),
        xnew_hbm=dint("xnew_hbm", [OWN, D], F32),
        y2_hbm=dint("y2_hbm", [NKC, NQ, 128, 128], BF16),
    )
    io["debug"] = set(debug)
    io["dbg"] = {}
    io["dout"] = dout

    with contextlib.ExitStack() as ctx:
        _build(nc, ctx, io)
    return nc


def _build(nc, ctx, io):
    p = Prog(nc)
    p.make_sems(ctx, ["L", "S", "T", "V", "A", "G"])

    def sb(name, shape, dt):
        return ctx.enter_context(nc.sbuf_tensor(name, shape, dt))

    def psum(name, shape, dt):
        return ctx.enter_context(nc.psum_tensor(name, shape, dt))

    ident = sb("ident", [128, 128], BF16)
    ones = sb("ones", [128, 128], BF16)
    mtri_b = sb("mtri_b", [128, 128], BF16)
    m2b = sb("m2b", [128, 1], F32)
    bqc = sb("bqc", [128, NKC], F32)
    bkc = sb("bkc", [128, NKC], F32)
    bvc = sb("bvc", [128, NKC], F32)
    lnwc = sb("lnwc", [128, NIC], F32)
    ssacc = sb("ssacc", [128, 1024], F32)
    biga = sb("biga", [128, NIC * OWN], BF16)
    qx2T = sb("qx2T", [128, NKC * OWN], BF16)
    cmean = sb("cmean", [128, 8], F32)
    thc = sb("thc", [128, 8], F32)
    dqb = sb("dqb", [128, 1024], F32)

    def lnT(k):
        return biga[:, k * T:(k + 1) * T]

    def attnO(k):
        return biga[:, k * OWN:(k + 1) * OWN]

    def hfm(j):
        return biga[:, j * OWN:(j + 1) * OWN]

    def qx2Tc(k):
        return qx2T[:, k * OWN:(k + 1) * OWN]

    psA = psum("psA", [128, 2048], F32)
    psB = psum("psB", [128, 1024], F32)
    psC = psum("psC", [128, 1024], BF16)
    psD = psum("psD", [128, 1024], BF16)

    def bank(i):
        if i < 4:
            return psA[:, i * 512:(i + 1) * 512]
        assert i < 6
        return psB[:, (i - 4) * 512:(i - 3) * 512]

    def bank_bf(i):
        return (psC if i == 6 else psD)[:]

    x = io["x_seq"]

    t_id = p.op("sy", lambda e: e.dma_start(out=ident[:], in_=io["identd"]),
                inc="L", amt=16, key="ident")
    p.op("sy", lambda e: e.dma_start(out=ones[:], in_=io["onesd"]),
         inc="L", amt=16, key="ones")
    t_consts = [t_id]
    for nm2, (dst, src) in (("bqc", (bqc, io["bq"])), ("bkc", (bkc, io["bk"])),
                            ("bvc", (bvc, io["bv"]))):
        t_consts.append(p.op("sy", lambda e, d=dst, s=src: e.dma_start(
            out=d[:], in_=s.rearrange("(c p) -> p c", p=128),
            allow_slow_non_contiguous=True), inc="L", amt=16, key=nm2))
    t_consts.append(p.op("sy", lambda e: e.dma_start(
        out=lnwc[:], in_=io["lnw"].rearrange("(c p) -> p c", p=128),
        allow_slow_non_contiguous=True), inc="L", amt=16, key="lnwc"))
    t_mtri = p.op("sy", lambda e: e.dma_start(out=mtri_b[:], in_=io["mtri"]),
                  inc="L", amt=16, key="mtri")
    t_m2 = p.op("sy", lambda e: e.dma_start(out=m2b[:], in_=io["m2c"]),
                inc="L", amt=16, key="m2b")

    ph1 = contextlib.ExitStack()

    def sb1(name, shape, dt):
        return ph1.enter_context(nc.sbuf_tensor(name, shape, dt))

    xbuf = [sb1(f"xbuf{i}", [128, D], F32) for i in range(2)]
    xsq = sb1("xsq", [128, D], BF16)
    lntile = [sb1(f"lntile{i}", [128, D], BF16) for i in range(2)]
    st1 = sb1("st1", [128, 16], F32)
    xload = [None, None]
    lnfree = [None, None]
    stat_free = [None, None]
    sqrt_free = [None, None]
    bank_free = {6: None, 7: None}
    t_lnT_done = None

    for tt in range(NT):
        bi = tt % 2
        c0 = 8 * bi
        tl = p.op("sy", lambda e, b=bi, t=tt: e.dma_start(
            out=xbuf[b][:], in_=x[t * 128:(t + 1) * 128, :]),
            after=[xload[bi]] + t_consts, inc="L", amt=16, key=f"xbuf{bi}")
        tv1 = p.op("ve", lambda e, b=bi, c=c0: e.reduce_sum(
            out=st1[:, c:c + 1], in_=xbuf[b][:], axis=AX), after=[tl])
        ta1 = p.op("se", lambda e, b=bi, c=c0: e.activation(
            out=xsq[:], in_=xbuf[b][:], func=AF.Square,
            accum_out=st1[:, c + 1:c + 2]),
            after=[tl, stat_free[bi]])
        tmean = p.op("ve", lambda e, c=c0: e.tensor_scalar(
            out=st1[:, c + 2:c + 3], in0=st1[:, c:c + 1], scalar1=1.0 / D,
            scalar2=None, op0=OP.mult), after=[tv1])
        tm2 = p.op("ve", lambda e, c=c0: e.tensor_tensor(
            out=st1[:, c + 3:c + 4], in0=st1[:, c + 2:c + 3],
            in1=st1[:, c + 2:c + 3], op=OP.mult), after=[tmean])
        tvar = p.op("ve", lambda e, c=c0: e.scalar_tensor_tensor(
            out=st1[:, c + 4:c + 5], in0=st1[:, c + 1:c + 2], scalar=1.0 / D,
            in1=st1[:, c + 3:c + 4], op0=OP.mult, op1=OP.subtract),
            after=[ta1, tm2])
        stat_free[bi] = tvar
        teps = p.op("ve", lambda e, c=c0: e.tensor_scalar(
            out=st1[:, c + 5:c + 6], in0=st1[:, c + 4:c + 5], scalar1=LN_EPS,
            scalar2=None, op0=OP.add), after=[tvar])
        tv2 = p.op("ve", lambda e, c=c0: e.reciprocal(
            out=st1[:, c + 6:c + 7], in_=st1[:, c + 5:c + 6]), after=[teps])
        ta2 = p.op("se", lambda e, c=c0: e.activation(
            out=st1[:, c + 7:c + 8], in_=st1[:, c + 6:c + 7], func=AF.Sqrt),
            after=[tv2, sqrt_free[bi]])
        tv3 = p.op("ve", lambda e, b=bi, c=c0: e.tensor_scalar(
            out=lntile[b][:], in0=xbuf[b][:], scalar1=st1[:, c + 2:c + 3],
            scalar2=st1[:, c + 7:c + 8], op0=OP.subtract, op1=OP.mult),
            after=[ta2, lnfree[bi]])
        xload[bi] = tv3
        sqrt_free[bi] = tv3
        last_tr = None
        for kg in range(4):
            pb = 6 + (kg % 2)
            tts = None
            for k4 in range(4):
                k = kg * 4 + k4
                tts = p.op("te", lambda e, b=bi, kk=k, k4=k4, pb=pb: e.transpose(
                    out=bank_bf(pb)[:, k4 * 128:(k4 + 1) * 128],
                    in_=lntile[b][:, kk * 128:(kk + 1) * 128],
                    identity=ident[:]),
                    after=([tv3, t_id, bank_free[pb]] if k4 == 0 else []),
                    inc="T" if k4 == 3 else None)
            base = kg * 4 * T + tt * 128
            dst3 = biga[:, base:base + 4 * T].rearrange(
                "p (c q) -> p c q", c=4)[:, :, 0:128]
            tev = p.op("ve", lambda e, d=dst3, pb=pb: e.tensor_copy(
                out=d, in_=bank_bf(pb)[:, 0:512]), after=[tts], inc="V")
            bank_free[pb] = tev
            last_tr = tts
            t_lnT_done = tev
        lnfree[bi] = last_tr

    p.barrier([t_lnT_done, bank_free[6], bank_free[7]])

    if "lnT_dbg" in io["debug"]:
        lnd = io["dout"]("lnT_dbg", [D, T], BF16)
        for k in range(NKC):
            p.op("sy", lambda e, k=k: e.dma_start(
                out=lnd[k * 128:(k + 1) * 128, :], in_=lnT(k)), inc="S", amt=16, key="dbg")

    ph1.close()

    ph2 = contextlib.ExitStack()

    def sb2(name, shape, dt):
        return ph2.enter_context(nc.sbuf_tensor(name, shape, dt))

    wcol = [sb2(f"wcol{i}", [128, D], F32) for i in range(2)]
    wbf = [sb2(f"wbf{i}", [128, D], BF16) for i in range(2)]
    stage = [sb2(f"stage{i}", [128, T], BF16) for i in range(2)]
    wload = [None, None]
    wbf_free = [None, None]
    stage_free = [None, None]
    grp_free = {"A": None, "B": None}
    grp_free2 = {"A": None, "B": None}
    t_p2_done = []

    projs = [
        ("q", io["wq"], bqc, io["qT_hbm"], OWN),
        ("k", io["wk"], bkc, io["kT_hbm"], T),
        ("v", io["wv"], bvc, io["vT_hbm"], T),
    ]
    ji = 0
    for (pname, wdram, bcol, ohbm, tspan) in projs:
        nb = tspan // 512
        for j in range(NKC):
            par = ji % 2
            ji += 1
            wsrc = wdram[:, j * 128:(j + 1) * 128].rearrange(
                "(kc p) c -> p kc c", p=128)
            tl = p.op("sy", lambda e, par=par, s=wsrc: e.dma_start(
                out=wcol[par][:].rearrange("p (kc c) -> p kc c", kc=NKC),
                in_=s), after=[wload[par]], inc="L", amt=16, key=f"wcol{par}")
            tc = p.op("ge", lambda e, par=par: e.tensor_copy(
                out=wbf[par][:], in_=wcol[par][:]),
                after=[tl, wbf_free[par]], inc="G")
            wload[par] = tc
            if nb == 4:
                bset, gk = [0, 1, 2, 3], "A"
            else:
                bset, gk = ([0, 1], "A") if par == 0 else ([4, 5], "B")
            tmm = None
            for k in range(NKC):
                for tb in range(nb):
                    tmm = p.op("te", lambda e, par=par, k=k, tb=tb, bset=bset: e.matmul(
                        out=bank(bset[tb]),
                        lhsT=wbf[par][:, k * 128:(k + 1) * 128],
                        rhs=lnT(k)[:, tb * 512:(tb + 1) * 512],
                        start=(k == 0), stop=(k == NKC - 1)),
                        after=([tc, grp_free[gk], grp_free2[gk]]
                               if k == 0 and tb == 0 else
                               [tc] if k == 0 else []),
                        inc="T" if (k == NKC - 1 and tb == nb - 1) else None)
            wbf_free[par] = tmm
            tevs2 = []
            for tb in range(nb):
                eng = "ve" if (tb % 2 == 0) else "se"
                if eng == "ve":
                    tevs2.append(p.op("ve", lambda e, par=par, tb=tb, bset=bset, j=j, bcol=bcol: e.tensor_scalar(
                        out=stage[par][:, tb * 512:(tb + 1) * 512],
                        in0=bank(bset[tb]), scalar1=bcol[:, j:j + 1],
                        scalar2=None, op0=OP.add),
                        after=[tmm, stage_free[par]], inc="V"))
                else:
                    tevs2.append(p.op("se", lambda e, par=par, tb=tb, bset=bset, j=j, bcol=bcol: e.activation(
                        out=stage[par][:, tb * 512:(tb + 1) * 512],
                        in_=bank(bset[tb]), func=AF.Identity,
                        bias=bcol[:, j:j + 1]),
                        after=[tmm, stage_free[par]], inc="A"))
            grp_free[gk] = tevs2[-2] if len(tevs2) >= 2 else tevs2[-1]
            grp_free2[gk] = tevs2[-1]
            tst = p.op("sy", lambda e, par=par, j=j, ohbm=ohbm, tspan=tspan: e.dma_start(
                out=ohbm[j * 128:(j + 1) * 128, :],
                in_=stage[par][:, 0:tspan]),
                after=tevs2[-2:], inc="S", amt=16, key=f"stage{par}")
            stage_free[par] = tst
            t_p2_done.append(tst)

    p.barrier(t_p2_done[-2:])

    ph2.close()

    ph3 = contextlib.ExitStack()

    def sb3(name, shape, dt):
        return ph3.enter_context(nc.sbuf_tensor(name, shape, dt))

    kh = [sb3(f"kh{i}", [128, T], BF16) for i in range(2)]
    vTh0 = sb3("vTh0", [128, T], BF16)
    vTh = [vTh0, vTh0]
    qh = [sb3(f"qh{i}", [128, OWN], BF16) for i in range(2)]
    vtok = sb3("vtok", [128, T], BF16)
    pbuf = sb3("pbuf", [128, T], BF16)
    pT = sb3("pT", [128, NT * 512], BF16)
    smax = sb3("smax", [128, 6], F32)
    ssum = sb3("ssum", [128, 4], F32)

    pT3 = pT[:].rearrange("p (kt q) -> p kt q", kt=NT)
    if "vtok_dbg" in io["debug"]:
        io["dbg"]["vtok_dbg"] = io["dout"]("vtok_dbg", [128, T], BF16)
        io["dbg"]["vtok2_dbg"] = io["dout"]("vtok2_dbg", [128, T], BF16)
        io["dbg"]["pT_dbg"] = io["dout"]("pT_dbg", [128, 2 * NT * 512], BF16)
    t_kh_used = [None, None]
    t_vTh_used = [None, None]
    t_qh_used = [None, None]
    t_vtok_evac = None
    t_pv = [None, None]
    t_pv_guard = None
    t_oevac_prev = None
    t_exp_prev = None
    t_soft_prev = None
    t_ptr_last_prev = None
    bank_free3 = {6: None, 7: None}

    for h in range(NH):
        par = h % 2
        tkl = p.op("sy", lambda e, par=par, h=h: e.dma_start(
            out=kh[par][:], in_=io["kT_hbm"][h * 128:(h + 1) * 128, :]),
            after=[t_kh_used[par]], inc="L", key=f"kh{par}")
        tvl = p.op("sy", lambda e, par=par, h=h: e.dma_start(
            out=vTh[par][:], in_=io["vT_hbm"][h * 128:(h + 1) * 128, :]),
            after=[t_vTh_used[0], t_vTh_used[1]], inc="L", key="vTh")
        tql = p.op("sy", lambda e, par=par, h=h: e.dma_start(
            out=qh[par][:], in_=io["qT_hbm"][h * 128:(h + 1) * 128, :]),
            after=[t_qh_used[par]], inc="L", key=f"qh{par}")
        t_vtr = None
        for kg in range(4):
            pb = 6 + (kg % 2)
            tts = None
            for k4 in range(4):
                kt = kg * 4 + k4
                tts = p.op("te", lambda e, par=par, kt=kt, k4=k4, pb=pb: e.transpose(
                    out=bank_bf(pb)[:, k4 * 128:(k4 + 1) * 128],
                    in_=vTh[par][:, kt * 128:(kt + 1) * 128],
                    identity=ident[:]),
                    after=([tvl, bank_free3[pb]] if k4 == 0 else []),
                    inc="T" if k4 == 3 else None)
            base = kg * 4 * 128
            if kg % 2 == 0:
                tev = p.op("ve", lambda e, base=base, pb=pb: e.tensor_copy(
                    out=vtok[:, base:base + 512], in_=bank_bf(pb)[:, 0:512]),
                    after=[tts, t_pv[0], t_pv[1]] if kg == 0 else [tts])
            else:
                tev = p.op("se", lambda e, base=base, pb=pb: e.activation(
                    out=vtok[:, base:base + 512], in_=bank_bf(pb)[:, 0:512],
                    func=AF.Identity),
                    after=[tts, t_pv[0], t_pv[1]] if kg == 1 else [tts])
            bank_free3[pb] = tev
            t_vtr = tev
        t_vTh_used[0] = t_vtr
        t_vTh_used[1] = t_vtr
        t_vtok_evac = t_vtr
        if h == 0 and "vtok_dbg" in io["debug"]:
            p.op("sy", lambda e: e.dma_start(
                out=io["dbg"]["vtok_dbg"], in_=vtok[:]),
                after=[t_vtr], inc="S", key="dbg")

        t_qk_last = None
        for qb in range(2):
            t_ptr_evacs = []
            for q4 in range(4):
                qt = qb * 4 + q4
                vis1 = list(range(qt + 1))
                nv1 = qt + 1
                tqk = None
                first = True
                for b in range(4):
                    if b < 2:
                        lo = b * 4
                        if lo > qt:
                            continue
                        w = (min(qt, lo + 3) - lo + 1) * 128
                        has_diag = lo <= qt <= lo + 3
                        deps = [tkl, tql, t_exp_prev] if first else []
                        if first and h == 0 and qt == 0:
                            deps = deps + [t_mtri, t_m2]
                        tq_ = p.op("te", lambda e, par=par, qt=qt, b=b, lo=lo, w=w, has_diag=has_diag: e.matmul(
                            out=psA[:, lo * 128:lo * 128 + w],
                            lhsT=qh[par][:, qt * 128:(qt + 1) * 128],
                            rhs=kh[par][:, lo * 128:lo * 128 + w],
                            start=True, stop=not has_diag),
                            after=deps,
                            inc=None if has_diag else "T")
                        if has_diag:
                            tqk = p.op("te", lambda e, qt=qt: e.matmul(
                                out=psA[:, qt * 128:(qt + 1) * 128],
                                lhsT=ident[:], rhs=mtri_b[:],
                                start=False, stop=True), inc="T")
                        else:
                            tqk = tq_
                        first = False
                    else:
                        tqk = p.op("te", lambda e, par=par, qt=qt, b=b: e.matmul(
                            out=bank(b),
                            lhsT=qh[par][:, qt * 128:(qt + 1) * 128],
                            rhs=kh[par][:, b * 512:(b + 1) * 512],
                            start=True, stop=True), inc="T" if b == 3 else None)
                t_qk_last = tqk
                ea = nv1 * 128
                tmxA = p.op("ve", lambda e, ea=ea: e.reduce_max(
                    out=smax[:, 0:1], in_=psA[:, 0:ea], axis=AX),
                    after=[tqk, t_soft_prev])
                tmxB = p.op("ve", lambda e: e.reduce_max(
                    out=smax[:, 2:3], in_=psA[:, 1024:2048], axis=AX))
                tmxC = p.op("ve", lambda e: e.tensor_tensor(
                    out=smax[:, 0:1], in0=smax[:, 0:1], in1=smax[:, 2:3],
                    op=OP.max), after=[tmxA, tmxB])
                tnb = p.op("ve", lambda e: e.tensor_scalar(
                    out=smax[:, 1:2], in0=smax[:, 0:1], scalar1=-SM_SCALE,
                    scalar2=None, op0=OP.mult), after=[tmxC])
                tnb2a = p.op("ve", lambda e: e.tensor_tensor(
                    out=smax[:, 4:5], in0=m2b[:], in1=smax[:, 0:1],
                    op=OP.subtract), after=[tmxC])
                tnb2 = p.op("ve", lambda e: e.tensor_scalar(
                    out=smax[:, 3:4], in0=smax[:, 4:5], scalar1=SM_SCALE,
                    scalar2=None, op0=OP.mult), after=[tnb2a])
                t_expA = p.op("se", lambda e, ea=ea: e.activation(
                    out=pbuf[:, 0:ea], in_=psA[:, 0:ea], func=AF.Exp,
                    bias=smax[:, 1:2], scale=SM_SCALE,
                    accum_out=ssum[:, 0:1]),
                    after=[tnb, t_ptr_last_prev, t_soft_prev])
                t_exp = p.op("se", lambda e: e.activation(
                    out=pbuf[:, 1024:2048], in_=psA[:, 1024:2048],
                    func=AF.Exp, bias=smax[:, 3:4], scale=SM_SCALE,
                    accum_out=ssum[:, 2:3]), after=[tnb2])
                t_exp_prev = t_exp
                tsc = p.op("ve", lambda e: e.tensor_tensor(
                    out=ssum[:, 0:1], in0=ssum[:, 0:1], in1=ssum[:, 2:3],
                    op=OP.add), after=[t_expA, t_exp])
                trc = p.op("ve", lambda e: e.reciprocal(
                    out=ssum[:, 1:2], in_=ssum[:, 0:1]), after=[tsc])
                tpnA = p.op("se", lambda e, ea=ea: e.activation(
                    out=pbuf[:, 0:ea], in_=pbuf[:, 0:ea], func=AF.Identity,
                    scale=ssum[:, 1:2]), after=[trc])
                tpn = p.op("se", lambda e: e.activation(
                    out=pbuf[:, 1024:2048], in_=pbuf[:, 1024:2048],
                    func=AF.Identity, scale=ssum[:, 1:2]), after=[tpnA])
                t_soft_prev = tpn
                vis_groups = []
                for lo0 in range(0, nv1, 4):
                    vis_groups.append((lo0, min(4, nv1 - lo0)))
                vis_groups += [(8, 4), (12, 4)]
                tptr = None
                for gi, (lo0, n4) in enumerate(vis_groups):
                    pb = 6 + (gi % 2)
                    tts = None
                    for k4 in range(n4):
                        kt = lo0 + k4
                        tts = p.op("te", lambda e, kt=kt, k4=k4, pb=pb: e.transpose(
                            out=bank_bf(pb)[:, k4 * 128:(k4 + 1) * 128],
                            in_=pbuf[:, kt * 128:(kt + 1) * 128],
                            identity=ident[:]),
                            after=([tpnA if lo0 < 8 else tpn,
                                    bank_free3[pb]] if k4 == 0 else []),
                            inc="T" if k4 == n4 - 1 else None)
                    dst3 = pT3[:, lo0:lo0 + n4, q4 * 128:(q4 + 1) * 128]
                    if gi % 2 == 0:
                        tev = p.op("ve", lambda e, d=dst3, pb=pb, n4=n4: e.tensor_copy(
                            out=d, in_=bank_bf(pb)[:, 0:n4 * 128]),
                            after=[tts, t_pv_guard] if gi == 0 else [tts])
                    else:
                        tev = p.op("se", lambda e, d=dst3, pb=pb, n4=n4: e.activation(
                            out=d, in_=bank_bf(pb)[:, 0:n4 * 128],
                            func=AF.Identity),
                            after=[tts, t_pv_guard] if gi == 1 else [tts])
                    bank_free3[pb] = tev
                    t_ptr_evacs.append(tev)
                    tptr = tts
                t_ptr_last_prev = tptr
            pv_list = [(kt, max(0, kt - qb * 4) * 128)
                       for kt in range(qb * 4 + 4)]
            pv_list += [(kt, 0) for kt in range(8, 16)]
            n_pv = len(pv_list)
            tpv = None
            for i, (kt, off) in enumerate(pv_list):
                tpv = p.op("te", lambda e, kt=kt, qb=qb, off=off, i=i, n_pv=n_pv: e.matmul(
                    out=psB[:, qb * 512 + off:(qb + 1) * 512],
                    lhsT=vtok[:, kt * 128:(kt + 1) * 128],
                    rhs=pT[:, kt * 512 + off:(kt + 1) * 512],
                    start=(i == 0), stop=(i == n_pv - 1)),
                    after=([t_vtok_evac, t_oevac_prev] + t_ptr_evacs)
                    if i == 0 else [],
                    inc="T" if i == n_pv - 1 else None)
            t_pv[qb] = tpv
            t_pv_guard = tpv
            if h == 0 and "vtok_dbg" in io["debug"]:
                p.op("sy", lambda e, qb=qb: e.dma_start(
                    out=io["dbg"]["pT_dbg"][:, qb * 8192:(qb + 1) * 8192],
                    in_=pT[:]), after=t_ptr_evacs[-1:], inc="S", key="dbg")
                p.op("sy", lambda e, qb=qb: e.dma_start(
                    out=io["dbg"]["vtok2_dbg"], in_=vtok[:]),
                    after=[tpv], inc="S", key="dbg")
        toe = p.op("ve", lambda e, h=h: e.tensor_copy(
            out=attnO(h), in_=psB[:]), after=[t_pv[0], t_pv[1]])
        t_oevac_prev = toe
        t_kh_used[par] = t_qk_last
        t_qh_used[par] = t_qk_last

    p.barrier([t_oevac_prev])

    if "aoT_dbg" in io["debug"]:
        aod = io["dout"]("aoT_dbg", [D, OWN], BF16)
        for k in range(NKC):
            p.op("sy", lambda e, k=k: e.dma_start(
                out=aod[k * 128:(k + 1) * 128, :], in_=attnO(k)), inc="S", key="dbg")

    ph3.close()

    ph4 = contextlib.ExitStack()

    def sb4(name, shape, dt):
        return ph4.enter_context(nc.sbuf_tensor(name, shape, dt))

    worow = sb4("worow", [128, 1024], F32)
    onesf = sb4("onesf", [1, 128], F32)
    st4 = sb4("st4", [128, 16], F32)
    xtile = sb4("xtile", [128, D], F32)
    xnb = sb4("xnb", [128, D], F32)
    ln2b = sb4("ln2b", [128, D], F32)
    vb4 = sb4("vb4", [128, D], F32)
    gb4 = sb4("gb4", [128, D], I32)
    bo_row = worow[0:1, :]
    wob = [biga[:, (16 + 2 * k) * OWN:(18 + 2 * k) * OWN] for k in range(NKC)]
    bo_b = biga[:, 48 * OWN:50 * OWN]
    cb1 = biga[:, 50 * OWN:52 * OWN]
    cb2 = biga[:, 52 * OWN:54 * OWN]
    qxb = biga[:, 54 * OWN:56 * OWN]
    sqd = biga[:, 56 * OWN:58 * OWN]

    qx2T3 = qx2T[:].rearrange("p (k t) -> p k t", k=NKC)

    t_of = p.op("ve", lambda e: e.memset(onesf[:], 1.0))
    tmmb = None
    for hb in range(2):
        t_bol = p.op("sy", lambda e, hb=hb: e.dma_start(
            out=bo_row, in_=io["bo"][None, hb * 1024:(hb + 1) * 1024]),
            after=[tmmb], inc="L", key="worow")
        for i in range(2):
            tmmb = p.op("te", lambda e, hb=hb, i=i: e.matmul(
                out=psA[:, hb * 1024 + i * 512:hb * 1024 + (i + 1) * 512],
                lhsT=onesf[:], rhs=bo_row[:, i * 512:(i + 1) * 512],
                start=True, stop=True),
                after=[t_bol, t_of] if i == 0 else [],
                inc="T" if i == 1 else None)
    t_bob = p.op("ve", lambda e: e.tensor_copy(out=bo_b, in_=psA[:]),
                 after=[tmmb])

    t_qx2ev = []
    bank_free4 = {6: None, 7: None}
    t_psA_free = t_bob
    t_worow_free = tmmb
    t_xst = None
    t_xtile_used = None
    t_stat_used = None
    t_xnb_used = None
    t_st4sq_used = None
    t_qxb_free = None
    tcache = [None] * NKC

    for qt in range(NQ):
        tmm_last = None
        for k in range(NKC):
            if qt == 0:
                for hb in range(2):
                    twl = p.op("sy", lambda e, k=k, hb=hb: e.dma_start(
                        out=worow[:],
                        in_=io["wo"][k * 128:(k + 1) * 128,
                                     hb * 1024:(hb + 1) * 1024]),
                        after=[t_worow_free], inc="L", key="worow")
                    twc = p.op("ge", lambda e, k=k, hb=hb: e.tensor_copy(
                        out=wob[k][:, hb * 1024:(hb + 1) * 1024],
                        in_=worow[:]), after=[twl])
                    t_worow_free = twc
                tcache[k] = t_worow_free
            for tb in range(4):
                deps = []
                if tb == 0 and qt == 0:
                    deps = [tcache[k]] + ([t_psA_free] if k == 0 else [])
                elif tb == 0 and k == 0:
                    deps = [t_psA_free]
                tmm_last = p.op("te", lambda e, k=k, qt=qt, tb=tb: e.matmul(
                    out=bank(tb),
                    lhsT=attnO(k)[:, qt * 128:(qt + 1) * 128],
                    rhs=wob[k][:, tb * 512:(tb + 1) * 512],
                    start=(k == 0), stop=(k == NKC - 1)),
                    after=deps,
                    inc="T" if (k == NKC - 1 and tb == 3) else None)
        txl = p.op("sy", lambda e, qt=qt: e.dma_start(
            out=xtile[:], in_=x[qt * 128:(qt + 1) * 128, :]),
            after=[t_xtile_used], inc="L", key="xtile")
        txb = p.op("ve", lambda e: e.tensor_tensor(
            out=xtile[:], in0=xtile[:], in1=bo_b, op=OP.add),
            after=[txl, t_bob])
        p.op("ve", lambda e: e.tensor_tensor(
            out=xnb[:, 0:1024], in0=psA[:, 0:1024],
            in1=xtile[:, 0:1024], op=OP.add),
            after=[tmm_last, txb, t_xnb_used, t_xst])
        tev = p.op("ve", lambda e: e.tensor_tensor(
            out=xnb[:, 1024:2048], in0=psA[:, 1024:2048],
            in1=xtile[:, 1024:2048], op=OP.add))
        t_psA_free = tev
        t_xtile_used = tev
        t_xst = p.op("sy", lambda e, qt=qt: e.dma_start(
            out=io["xnew_hbm"][qt * 128:(qt + 1) * 128, :], in_=xnb[:]),
            after=[tev], inc="S", key="xnb")
        tv1 = p.op("ve", lambda e: e.reduce_sum(
            out=st4[:, 0:1], in_=xnb[:], axis=AX), after=[tev, t_stat_used])
        ta1 = p.op("se", lambda e: e.activation(
            out=sqd[:, 0:1024], in_=xnb[:, 0:1024], func=AF.Square,
            accum_out=st4[:, 1:2]), after=[tev, t_st4sq_used])
        ta1b = p.op("se", lambda e: e.activation(
            out=sqd[:, 1024:2048], in_=xnb[:, 1024:2048], func=AF.Square,
            accum_out=st4[:, 2:3]), after=[ta1])
        tmean = p.op("ve", lambda e: e.tensor_scalar(
            out=st4[:, 3:4], in0=st4[:, 0:1], scalar1=1.0 / D,
            scalar2=None, op0=OP.mult), after=[tv1])
        tm2 = p.op("ve", lambda e: e.tensor_tensor(
            out=st4[:, 4:5], in0=st4[:, 3:4], in1=st4[:, 3:4], op=OP.mult),
            after=[tmean])
        tsq = p.op("ve", lambda e: e.tensor_tensor(
            out=st4[:, 5:6], in0=st4[:, 1:2], in1=st4[:, 2:3], op=OP.add),
            after=[ta1b, tm2])
        t_st4sq_used = tsq
        tvar = p.op("ve", lambda e: e.scalar_tensor_tensor(
            out=st4[:, 6:7], in0=st4[:, 5:6], scalar=1.0 / D,
            in1=st4[:, 4:5], op0=OP.mult, op1=OP.subtract), after=[tsq])
        teps = p.op("ve", lambda e: e.tensor_scalar(
            out=st4[:, 7:8], in0=st4[:, 6:7], scalar1=LN_EPS,
            scalar2=None, op0=OP.add), after=[tvar])
        trc2 = p.op("ve", lambda e: e.reciprocal(
            out=st4[:, 8:9], in_=st4[:, 7:8]), after=[teps])
        tsqr = p.op("se", lambda e: e.activation(
            out=st4[:, 9:10], in_=st4[:, 8:9], func=AF.Sqrt), after=[trc2])
        tln = p.op("ve", lambda e: e.tensor_scalar(
            out=ln2b[:], in0=xnb[:], scalar1=st4[:, 3:4],
            scalar2=st4[:, 9:10], op0=OP.subtract, op1=OP.mult),
            after=[tsqr])
        t_xnb_used = tln
        tam = p.op("ve", lambda e: e.reduce_max(
            out=st4[:, 10:11], in_=ln2b[:], axis=AX,
            apply_absolute_value=True), after=[tln])
        tt1 = p.op("ve", lambda e: e.tensor_scalar(
            out=st4[:, 11:12], in0=st4[:, 10:11], scalar1=QEPS,
            scalar2=None, op0=OP.max), after=[tam])
        tqr = p.op("ve", lambda e: e.reciprocal(
            out=st4[:, 12:13], in_=st4[:, 11:12]), after=[tt1])
        tdq = p.op("ve", lambda e: e.tensor_scalar(
            out=st4[:, 13:14], in0=st4[:, 11:12], scalar1=1.0 / 127.0,
            scalar2=None, op0=OP.mult), after=[tt1])
        t_stat_used = tdq
        tvv = p.op("ve", lambda e: e.tensor_scalar(
            out=vb4[:], in0=ln2b[:], scalar1=st4[:, 12:13], scalar2=127.0,
            op0=OP.mult, op1=OP.mult), after=[tqr, tln])
        tvc = p.op("ve", lambda e: e.tensor_scalar(
            out=ln2b[:], in0=vb4[:], scalar1=-128.0, scalar2=127.0,
            op0=OP.max, op1=OP.min), after=[tvv])
        tg = p.op("ve", lambda e: e.tensor_copy(
            out=gb4[:], in_=ln2b[:]), after=[tvc])
        tc1 = p.op("ve", lambda e: e.scalar_tensor_tensor(
            out=cb1, in0=gb4[:], scalar=0.5, in1=ln2b[:],
            op0=OP.add, op1=OP.is_le), after=[tg])
        tc2 = p.op("ve", lambda e: e.scalar_tensor_tensor(
            out=cb2, in0=gb4[:], scalar=-0.5, in1=ln2b[:],
            op0=OP.add, op1=OP.is_ge), after=[tc1])
        tu1 = p.op("ve", lambda e: e.scalar_tensor_tensor(
            out=vb4[:], in0=cb1, scalar=1.0, in1=gb4[:],
            op0=OP.bypass, op1=OP.add), after=[tc1])
        tu2 = p.op("ve", lambda e: e.scalar_tensor_tensor(
            out=ln2b[:], in0=cb2, scalar=-1.0, in1=vb4[:],
            op0=OP.mult, op1=OP.add), after=[tu1, tc2])
        tqx = p.op("ve", lambda e: e.tensor_scalar(
            out=qxb, in0=ln2b[:], scalar1=st4[:, 13:14], scalar2=None,
            op0=OP.mult), after=[tu2, tdq, t_qxb_free])
        for kg in range(4):
            pb = 6 + (kg % 2)
            tts = None
            for k4 in range(4):
                kk = kg * 4 + k4
                tts = p.op("te", lambda e, kk=kk, k4=k4, pb=pb: e.transpose(
                    out=bank_bf(pb)[:, k4 * 128:(k4 + 1) * 128],
                    in_=qxb[:, kk * 128:(kk + 1) * 128],
                    identity=ident[:]),
                    after=([tqx, bank_free4[pb]] if k4 == 0 else []),
                    inc="T" if k4 == 3 else None)
            dst3 = qx2T3[:, kg * 4:(kg + 1) * 4, qt * 128:(qt + 1) * 128]
            tev2 = p.op("ve", lambda e, d=dst3, pb=pb: e.tensor_copy(
                out=d, in_=bank_bf(pb)[:, 0:512]), after=[tts])
            bank_free4[pb] = tev2
            t_qx2ev.append(tev2)
        t_qxb_free = tts

    p.barrier(t_qx2ev[-2:] + [t_xst])

    if "qx2T_dbg" in io["debug"]:
        qxd = io["dout"]("qx2T_dbg", [D, OWN], BF16)
        for k in range(NKC):
            p.op("sy", lambda e, k=k: e.dma_start(
                out=qxd[k * 128:(k + 1) * 128, :], in_=qx2Tc(k)), inc="S", key="dbg")

    ph4.close()

    ph45 = contextlib.ExitStack()

    def sb45(name, shape, dt):
        return ph45.enter_context(nc.sbuf_tensor(name, shape, dt))

    wpb = [sb45(f"wpb{i}", [128, 2048], F32) for i in range(2)]
    wstat16 = sb45("wstat16", [128, 192 * 16], F32)
    wstat = sb45("wstat", [128, 192], F32)
    onesf128 = sb45("onesf128", [128, 128], F32)
    wsum = sb45("wsum", [128, 192], F32)

    blocks = []
    for mi, (wdram, nrc) in enumerate(((io["wg"], 64), (io["wu"], 64),
                                       (io["wd"], 16))):
        if mi < 2:
            for rr in range(nrc):
                blocks.append((mi, wdram[rr * 128:(rr + 1) * 128, :], mi * 64 + rr))
        else:
            for rr in range(nrc):
                for qq2 in range(4):
                    blocks.append((mi, wdram[rr * 128:(rr + 1) * 128,
                                             qq2 * 2048:(qq2 + 1) * 2048],
                                   128 + rr * 4 + qq2))
    t_wp_used = [None, None]
    t_abs = []
    for bi2, (mi, src, col) in enumerate(blocks):
        par = bi2 % 2
        twl = p.op("sy", lambda e, par=par, src=src: e.dma_start(
            out=wpb[par][:], in_=src), after=[t_wp_used[par]], inc="L", key=f"wpb{par}")
        tr = p.op("ve", lambda e, par=par, col=col: e.tensor_reduce(
            out=wstat16[:, col * 16:(col + 1) * 16],
            in_=wpb[par][:].rearrange("p (a b) -> p a b", a=16),
            axis=AX, op=OP.add, apply_absolute_value=True), after=[twl])
        t_wp_used[par] = tr
        t_abs.append(tr)
    tst2 = p.op("ve", lambda e: e.tensor_reduce(
        out=wstat[:], in_=wstat16[:].rearrange("p (a b) -> p a b", a=192),
        axis=AX, op=OP.add), after=[t_abs[-1]])
    tcast = p.op("ve", lambda e: e.memset(onesf128[:], 1.0), after=[tst2])
    tsum = p.op("te", lambda e: e.matmul(
        out=psB[:, 0:192], lhsT=onesf128[:], rhs=wstat[:],
        start=True, stop=True), after=[tcast, tst2])
    tsev = p.op("ve", lambda e: e.tensor_copy(out=wsum[:], in_=psB[:, 0:192]),
                after=[tsum])
    tprev = tsev
    for mi, (c0, ncols, numel) in enumerate(
            ((0, 64, II * D), (64, 64, II * D), (128, 64, D * II))):
        t1_ = p.op("ve", lambda e, mi=mi, c0=c0, ncols=ncols: e.reduce_sum(
            out=cmean[:, mi + 4:mi + 5], in_=wsum[:, c0:c0 + ncols], axis=AX),
            after=[tprev])
        t2_ = p.op("ve", lambda e, mi=mi, numel=numel: e.tensor_scalar(
            out=cmean[:, mi:mi + 1], in0=cmean[:, mi + 4:mi + 5],
            scalar1=1.0 / numel, scalar2=QEPS, op0=OP.mult, op1=OP.max),
            after=[t1_])
        t3_ = p.op("ve", lambda e, mi=mi: e.tensor_scalar(
            out=thc[:, mi:mi + 1], in0=cmean[:, mi:mi + 1], scalar1=0.5,
            scalar2=None, op0=OP.mult), after=[t2_])
        tprev = p.op("ve", lambda e, mi=mi: e.tensor_scalar(
            out=thc[:, mi + 4:mi + 5], in0=thc[:, mi:mi + 1], scalar1=-1.0,
            scalar2=None, op0=OP.mult), after=[t3_])
    t_scales = tprev
    p.barrier([t_scales])
    ph45.close()

    ph5 = contextlib.ExitStack()

    def sb5(name, shape, dt):
        return ph5.enter_context(nc.sbuf_tensor(name, shape, dt))

    wr5 = [[sb5(f"wr5_{m}_{i}", [128, 1024], F32) for i in range(2)]
           for m in range(2)]
    gslab = sb5("gslab", [128, 2048], BF16)
    uslab = sb5("uslab", [128, 2048], BF16)
    qu_b = sb5("qu_b", [128, 2048], BF16)
    qv_b = sb5("qv_b", [128, 2048], BF16)
    qr_b = [sb5(f"qr_b{i}", [128, 2048], BF16) for i in range(2)]
    gsilb = sb5("gsilb", [128, 1024], BF16)
    h2b = sb5("h2b", [128, 1024], BF16)

    t_wr_used = [[None, None], [None, None]]
    t_slab_used = [None, None]
    t_gu_ev = None
    t_h2 = None
    t_hfm = [None] * NIC
    bank_free5 = {6: None, 7: None}
    t_gu_mm = None
    t_ss_mm = None
    t_ss_acc = None

    t_qr_used = [None, None]
    t_sil_prev = None
    pend_ss = None

    def emit_ss(th2_tok):
        nonlocal t_ss_mm, t_ss_acc
        for tb in range(2):
            t_ss_mm = p.op("te", lambda e, tb=tb: e.matmul(
                out=psB[:, tb * 512:(tb + 1) * 512], lhsT=ones[:],
                rhs=h2b[:, tb * 512:(tb + 1) * 512],
                start=True, stop=True),
                after=[th2_tok, t_ss_acc] if tb == 0 else [],
                inc="T" if tb == 1 else None)
        if t_ss_acc is None:
            t_ss_acc = p.op("ve", lambda e: e.tensor_copy(
                out=ssacc[:], in_=psB[:]), after=[t_ss_mm])
        else:
            t_ss_acc = p.op("ve", lambda e: e.tensor_tensor(
                out=ssacc[:], in0=ssacc[:], in1=psB[:], op=OP.add),
                after=[t_ss_mm, t_ss_acc])

    def emit_quant5(j):
        toks = [None, None]
        for m, wdram in enumerate((io["wg"], io["wu"])):
            tls = []
            for hh in range(2):
                par = (2 * j + hh) % 2
                tls.append(p.op("sy", lambda e, m=m, j=j, hh=hh, par=par, w=wdram: e.dma_start(
                    out=wr5[m][par][:],
                    in_=w[j * 128:(j + 1) * 128, hh * 1024:(hh + 1) * 1024]),
                    after=[t_wr_used[m][par]], inc="L", key=f"wr5_{m}_{par}"))
            tq = None
            for hh in range(2):
                par = (2 * j + hh) % 2
                tu = p.op("ve", lambda e, m=m, hh=hh, par=par: e.tensor_scalar(
                    out=qu_b[:, hh * 1024:(hh + 1) * 1024], in0=wr5[m][par][:],
                    scalar1=thc[:, m:m + 1], scalar2=None, op0=OP.is_ge),
                    after=[tls[hh]] + ([t_qr_used[m]] if hh == 0 else []))
                tv = p.op("ve", lambda e, m=m, hh=hh, par=par: e.tensor_scalar(
                    out=qv_b[:, hh * 1024:(hh + 1) * 1024], in0=wr5[m][par][:],
                    scalar1=thc[:, m + 4:m + 5], scalar2=None, op0=OP.is_le),
                    after=[tu])
                tq = p.op("ve", lambda e, m=m, hh=hh: e.tensor_tensor(
                    out=qr_b[m][:, hh * 1024:(hh + 1) * 1024],
                    in0=qu_b[:, hh * 1024:(hh + 1) * 1024],
                    in1=qv_b[:, hh * 1024:(hh + 1) * 1024], op=OP.subtract),
                    after=[tv])
                t_wr_used[m][par] = tq
            toks[m] = tq
        return toks

    pend_quant = emit_quant5(0)
    for j in range(NIC):
        slabs = (gslab, uslab)
        tquant = [None, None]
        quant_toks = pend_quant
        if j + 1 < NIC:
            pend_quant = emit_quant5(j + 1)
        for m in range(2):
            tq = quant_toks[m]
            for kg in range(4):
                pb = 6 + (kg % 2)
                tts = None
                for k4 in range(4):
                    kk = kg * 4 + k4
                    tts = p.op("te", lambda e, m=m, kk=kk, k4=k4, pb=pb: e.transpose(
                        out=bank_bf(pb)[:, k4 * 128:(k4 + 1) * 128],
                        in_=qr_b[m][:, kk * 128:(kk + 1) * 128],
                        identity=ident[:]),
                        after=([tq, bank_free5[pb], t_slab_used[m]]
                               if k4 == 0 else []),
                        inc="T" if k4 == 3 else None)
                tev = p.op("se", lambda e, m=m, kg=kg, pb=pb, slab=slabs[m]: e.activation(
                    out=slab[:, kg * 512:(kg + 1) * 512],
                    in_=bank_bf(pb)[:, 0:512], func=AF.Identity),
                    after=[tts])
                bank_free5[pb] = tev
            t_qr_used[m] = tts
            t_slab_used[m] = None
            tquant[m] = (tq, tev)
        tmmg = None
        for k in range(NKC):
            for m in range(2):
                slab = (gslab, uslab)[m]
                for tb in range(2):
                    deps = []
                    if k == 0 and tb == 0 and m == 0:
                        deps = [tquant[0][1], t_sil_prev]
                    elif k == 0 and tb == 0 and m == 1:
                        deps = [tquant[1][1], t_gu_ev]
                    tmmg = p.op("te", lambda e, m=m, k=k, tb=tb, slab=slab: e.matmul(
                        out=psA[:, m * 1024 + tb * 512: m * 1024 + (tb + 1) * 512],
                        lhsT=slab[:, k * 128:(k + 1) * 128],
                        rhs=qx2Tc(k)[:, tb * 512:(tb + 1) * 512],
                        start=(k == 0), stop=(k == NKC - 1)),
                        after=deps,
                        inc="T" if (k == NKC - 1 and m == 1 and tb == 1) else None)
        t_slab_used[0] = tmmg
        t_slab_used[1] = tmmg
        if pend_ss is not None:
            emit_ss(pend_ss)
        tsil = p.op("se", lambda e: e.activation(
            out=gsilb[:], in_=psA[:, 0:1024], func=AF.Silu,
            scale=cmean[:, 0:1]), after=[tmmg, t_h2])
        t_sil_prev = tsil
        thf = p.op("ve", lambda e, j=j: e.scalar_tensor_tensor(
            out=hfm(j), in0=psA[:, 1024:2048], scalar=cmean[:, 1:2],
            in1=gsilb[:], op0=OP.mult, op1=OP.mult), after=[tsil])
        t_gu_ev = thf
        t_hfm[j] = thf
        th2 = p.op("se", lambda e, j=j: e.activation(
            out=h2b[:], in_=hfm(j), func=AF.Square), after=[thf, t_ss_mm])
        t_h2 = th2
        pend_ss = th2
    emit_ss(pend_ss)

    p.barrier([t_ss_acc, t_gu_ev])
    ph5.close()

    ph6 = contextlib.ExitStack()

    def sb6(name, shape, dt):
        return ph6.enter_context(nc.sbuf_tensor(name, shape, dt))

    rmsb = sb6("rmsb", [128, 1024], F32)
    qsb = sb6("qsb", [128, 1024], F32)
    rqb = sb6("rqb", [128, 1024], F32)
    hnb = sb6("hnb", [128, 1024], BF16)
    amax8 = sb6("amax8", [128, 16], F32)
    qsr = sb6("qsr", [8, 256], F32)
    qsrow_qs = ssacc[0:1, 0:1024]
    qsrow_dq = dqb[0:1, 0:1024]
    identf = sb6("identf", [128, 128], F32)
    onesf2 = sb6("onesf2", [1, 128], F32)
    vb = sb6("vb", [128, 1024], F32)
    vcb = sb6("vcb", [128, 1024], F32)
    gb = sb6("gb", [128, 1024], I32)
    u1b = sb6("u1b", [128, 1024], F32)
    c1b = sb6("c1b", [128, 1024], BF16)
    c2b = sb6("c2b", [128, 1024], BF16)

    tid2 = p.op("ve", lambda e: e.tensor_copy(out=identf[:], in_=ident[:]))
    tof2 = p.op("ve", lambda e: e.memset(onesf2[:], 1.0))
    trms1 = p.op("ve", lambda e: e.tensor_scalar(
        out=rmsb[:], in0=ssacc[:], scalar1=1.0 / II, scalar2=RMS_EPS,
        op0=OP.mult, op1=OP.add))
    trms0 = trms1
    trms2 = p.op("ve", lambda e: e.reciprocal(out=rmsb[:], in_=rmsb[:]),
                 after=[trms1])
    trms3 = p.op("se", lambda e: e.activation(
        out=rmsb[:], in_=rmsb[:], func=AF.Sqrt), after=[trms2])

    t_am = None
    t_hn_used = None
    bank_free6 = {6: None, 7: None}
    for j in range(NIC):
        thn = p.op("ve", lambda e, j=j: e.scalar_tensor_tensor(
            out=hnb[:], in0=hfm(j), scalar=lnwc[:, j:j + 1], in1=rmsb[:],
            op0=OP.mult, op1=OP.mult), after=[trms3, t_hn_used])
        pb = 6 + (j % 2)
        tts = None
        for k4 in range(8):
            tts = p.op("te", lambda e, k4=k4, pb=pb: e.transpose(
                out=bank_bf(pb)[:, k4 * 128:(k4 + 1) * 128],
                in_=hnb[:, k4 * 128:(k4 + 1) * 128],
                identity=ident[:]),
                after=([thn, bank_free6[pb]] if k4 == 0 else []),
                inc="T" if k4 == 7 else None)
        t_hn_used = tts
        trd = p.op("ve", lambda e, pb=pb: e.tensor_reduce(
            out=amax8[:, 8:16], in_=bank_bf(pb)[:, 0:1024].rearrange(
                "p (q c) -> p q c", q=8),
            axis=AX, op=OP.max, apply_absolute_value=True),
            after=[tts, t_am])
        bank_free6[pb] = trd
        if j == 0:
            t_am = p.op("ve", lambda e: e.tensor_copy(
                out=amax8[:, 0:8], in_=amax8[:, 8:16]), after=[trd])
        else:
            t_am = p.op("ve", lambda e: e.tensor_tensor(
                out=amax8[:, 0:8], in0=amax8[:, 0:8], in1=amax8[:, 8:16],
                op=OP.max), after=[trd, t_am])
    tq1 = p.op("ve", lambda e: e.tensor_scalar(
        out=amax8[:, 8:16], in0=amax8[:, 0:8], scalar1=QEPS, scalar2=None,
        op0=OP.max), after=[t_am])
    tq2 = p.op("ve", lambda e: e.reciprocal(
        out=amax8[:, 0:8], in_=amax8[:, 8:16]), after=[tq1])
    tq3 = p.op("ve", lambda e: e.tensor_scalar(
        out=amax8[:, 0:8], in0=amax8[:, 0:8], scalar1=127.0, scalar2=None,
        op0=OP.mult), after=[tq2])
    tq4 = p.op("ve", lambda e: e.tensor_scalar(
        out=amax8[:, 8:16], in0=amax8[:, 8:16], scalar1=1.0 / 127.0,
        scalar2=cmean[:, 2:3], op0=OP.mult, op1=OP.mult), after=[tq3])
    ttr1 = p.op("te", lambda e: e.transpose(
        out=psA[0:8, 0:128], in_=amax8[:, 0:8], identity=identf[:]),
        after=[tq3, tq4, tid2, t_hn_used, bank_free6[6], bank_free6[7]],
        inc="T")
    tev1 = p.op("ve", lambda e: e.tensor_copy(out=qsr[0:8, 0:128],
                                              in_=psA[0:8, 0:128]),
                after=[ttr1])
    ttr2 = p.op("te", lambda e: e.transpose(
        out=psA[0:8, 128:256], in_=amax8[:, 8:16], identity=identf[:]),
        after=[tev1], inc="T")
    tev2 = p.op("ve", lambda e: e.tensor_copy(out=qsr[0:8, 128:256],
                                              in_=psA[0:8, 128:256]),
                after=[ttr2])
    tgr1 = p.op("sy", lambda e: e.dma_start(
        out=qsrow_qs.rearrange("o (q c) -> o q c", q=8),
        in_=qsr[0:8, 0:128]), after=[tev1, trms1], inc="L", key="qsrow1")
    tgr2 = p.op("sy", lambda e: e.dma_start(
        out=qsrow_dq.rearrange("o (q c) -> o q c", q=8),
        in_=qsr[0:8, 128:256]), after=[tev2, trms1], inc="L", key="qsrow2")
    tb1 = None
    for tt in range(8):
        tb1 = p.op("te", lambda e, tt=tt: e.matmul(
            out=psB[:, tt * 128:(tt + 1) * 128], lhsT=onesf2[:],
            rhs=qsrow_qs[:, tt * 128:(tt + 1) * 128], start=True, stop=True),
            after=[tgr1, tof2, trms0] if tt == 0 else [],
            inc="T" if tt == 7 else None)
    tqsb = p.op("ve", lambda e: e.tensor_copy(out=qsb[:], in_=psB[:]),
                after=[tb1])
    tb2 = None
    for tt in range(8):
        tb2 = p.op("te", lambda e, tt=tt: e.matmul(
            out=psB[:, tt * 128:(tt + 1) * 128], lhsT=onesf2[:],
            rhs=qsrow_dq[:, tt * 128:(tt + 1) * 128],
            start=True, stop=True),
            after=[tgr2, tqsb] if tt == 0 else [],
            inc="T" if tt == 7 else None)
    tdqb = p.op("ve", lambda e: e.tensor_copy(out=dqb[:], in_=psB[:]),
                after=[tb2])
    trqb = p.op("ve", lambda e: e.tensor_tensor(
        out=rqb[:], in0=rmsb[:], in1=qsb[:], op=OP.mult), after=[tqsb, trms3])

    t_ch = None
    for j in range(NIC):
        tvv = p.op("ve", lambda e, j=j: e.scalar_tensor_tensor(
            out=vb[:], in0=hfm(j), scalar=lnwc[:, j:j + 1], in1=rqb[:],
            op0=OP.mult, op1=OP.mult), after=[trqb, t_ch])
        tvc = p.op("ve", lambda e: e.tensor_scalar(
            out=vcb[:], in0=vb[:], scalar1=-128.0, scalar2=127.0,
            op0=OP.max, op1=OP.min), after=[tvv])
        tg = p.op("ve", lambda e: e.tensor_copy(out=gb[:], in_=vcb[:]),
                  after=[tvc])
        tc1 = p.op("ve", lambda e: e.scalar_tensor_tensor(
            out=c1b[:], in0=gb[:], scalar=0.5, in1=vcb[:], op0=OP.add,
            op1=OP.is_le), after=[tg])
        tc2 = p.op("ve", lambda e: e.scalar_tensor_tensor(
            out=c2b[:], in0=gb[:], scalar=-0.5, in1=vcb[:], op0=OP.add,
            op1=OP.is_ge), after=[tc1])
        tu1 = p.op("ve", lambda e: e.scalar_tensor_tensor(
            out=u1b[:], in0=c1b[:], scalar=1.0, in1=gb[:], op0=OP.bypass,
            op1=OP.add), after=[tc1])
        t_ch = p.op("ve", lambda e, j=j: e.scalar_tensor_tensor(
            out=hfm(j), in0=c2b[:], scalar=-1.0, in1=u1b[:], op0=OP.mult,
            op1=OP.add), after=[tu1, tc2])

    if "ffn_dbg" in io["debug"]:
        fd = io["dout"]("ffn_dbg", [128, 4224], F32)
        p.op("sy", lambda e: e.dma_start(out=fd[:, 0:1024], in_=rmsb[:]),
             after=[t_ch], inc="S", key="dbg")
        p.op("sy", lambda e: e.dma_start(out=fd[:, 1024:2048], in_=qsb[:]),
             after=[t_ch], inc="S", key="dbg")
        p.op("sy", lambda e: e.dma_start(out=fd[:, 2048:3072], in_=dqb[:]),
             after=[t_ch], inc="S", key="dbg")
        p.op("sy", lambda e: e.dma_start(out=fd[:, 3072:3088],
                                         in_=amax8[:, 0:16]),
             after=[t_ch], inc="S", key="dbg")
        p.op("sy", lambda e: e.dma_start(out=fd[:, 3088:3096],
                                         in_=cmean[:, 0:8]),
             after=[t_ch], inc="S", key="dbg")
        p.op("sy", lambda e: e.dma_start(out=fd[:, 3096:3104],
                                         in_=thc[:, 0:8]),
             after=[t_ch], inc="S", key="dbg")
        p.op("sy", lambda e: e.dma_start(out=fd[:, 3104:4128],
                                         in_=ssacc[:]), after=[t_ch], inc="S", key="dbg")

    if "qh_dbg" in io["debug"]:
        qhd = io["dout"]("qh_dbg", [II, OWN], BF16)
        for j in range(NIC):
            p.op("sy", lambda e, j=j: e.dma_start(
                out=qhd[j * 128:(j + 1) * 128, :], in_=hfm(j)),
                after=[t_ch], inc="S", key="dbg")

    p.barrier([t_ch])
    ph6.close()

    ph7 = contextlib.ExitStack()

    def sb7(name, shape, dt):
        return ph7.enter_context(nc.sbuf_tensor(name, shape, dt))

    wdq = [sb7(f"wdq{i}", [128, 2048], F32) for i in range(2)]
    du_b = sb7("du_b", [128, 1024], BF16)
    dv_b = sb7("dv_b", [128, 1024], BF16)
    dr_b = sb7("dr_b", [128, 2048], BF16)
    dslab = sb7("dslab", [128, 8192], BF16)
    y2b = sb7("y2b", [128, 1024], BF16)
    ystage = sb7("ystage", [128, 1024], BF16)

    t_wd_used = [None, None]
    t_dq_used = None
    t_y2mm = [None, None]
    t_yst = None
    t_y2ev = None
    bank_free7 = {6: None, 7: None}
    t_slab7_used = None
    y2_hbm = io["y2_hbm"]

    for j in range(NKC):
        jp = j % 2
        tlast_ev = None
        for qq2 in range(4):
            par = (j * 4 + qq2) % 2
            tld = p.op("sy", lambda e, j=j, qq2=qq2, par=par: e.dma_start(
                out=wdq[par][:],
                in_=io["wd"][j * 128:(j + 1) * 128,
                             qq2 * 2048:(qq2 + 1) * 2048]),
                after=[t_wd_used[par]], inc="L", key=f"wdq{par}")
            tdr = None
            for hh in range(2):
                tdu = p.op("ve", lambda e, par=par, hh=hh: e.tensor_scalar(
                    out=du_b[:], in0=wdq[par][:, hh * 1024:(hh + 1) * 1024],
                    scalar1=thc[:, 2:3], scalar2=None, op0=OP.is_ge),
                    after=[tld, t_dq_used] if hh == 0 else [])
                tdv = p.op("ve", lambda e, par=par, hh=hh: e.tensor_scalar(
                    out=dv_b[:], in0=wdq[par][:, hh * 1024:(hh + 1) * 1024],
                    scalar1=thc[:, 6:7], scalar2=None, op0=OP.is_le),
                    after=[tdu])
                tdr = p.op("ve", lambda e, hh=hh: e.tensor_tensor(
                    out=dr_b[:, hh * 1024:(hh + 1) * 1024], in0=du_b[:],
                    in1=dv_b[:], op=OP.subtract), after=[tdv])
            t_wd_used[par] = tdr
            t_dq_used = tdr
            for kg in range(4):
                pb = 6 + (kg % 2)
                tts = None
                for k4 in range(4):
                    kk = kg * 4 + k4
                    tts = p.op("te", lambda e, kk=kk, k4=k4, pb=pb: e.transpose(
                        out=bank_bf(pb)[:, k4 * 128:(k4 + 1) * 128],
                        in_=dr_b[:, kk * 128:(kk + 1) * 128],
                        identity=ident[:]),
                        after=([tdr, bank_free7[pb], t_slab7_used]
                               if k4 == 0 else []),
                        inc="T" if k4 == 3 else None)
                tlast_ev = p.op("se", lambda e, qq2=qq2, kg=kg, pb=pb: e.activation(
                    out=dslab[:, qq2 * 2048 + kg * 512:qq2 * 2048 + (kg + 1) * 512],
                    in_=bank_bf(pb)[:, 0:512], func=AF.Identity), after=[tts])
                bank_free7[pb] = tlast_ev
        tdm = None
        for k in range(NIC):
            for tb in range(2):
                tdm = p.op("te", lambda e, j=j, jp=jp, k=k, tb=tb: e.matmul(
                    out=psA[:, jp * 1024 + tb * 512: jp * 1024 + (tb + 1) * 512],
                    lhsT=dslab[:, k * 128:(k + 1) * 128],
                    rhs=hfm(k)[:, tb * 512:(tb + 1) * 512],
                    start=(k == 0), stop=(k == NIC - 1)),
                    after=[tlast_ev, t_y2mm[jp]] if (k == 0 and tb == 0) else [],
                    inc="T" if (k == NIC - 1 and tb == 1) else None)
        t_slab7_used = tdm
        tyv = p.op("ve", lambda e, jp=jp: e.tensor_tensor(
            out=y2b[:], in0=psA[:, jp * 1024:(jp + 1) * 1024], in1=dqb[:],
            op=OP.mult), after=[tdm, t_yst])
        t_y2mm[jp] = tyv
        pb = 6 + (j % 2)
        tts = None
        for k4 in range(8):
            tts = p.op("te", lambda e, k4=k4, pb=pb: e.transpose(
                out=bank_bf(pb)[:, k4 * 128:(k4 + 1) * 128],
                in_=y2b[:, k4 * 128:(k4 + 1) * 128],
                identity=ident[:]),
                after=([tyv, bank_free7[pb]] if k4 == 0 else []),
                inc="T" if k4 == 7 else None)
        tyev = p.op("se", lambda e, pb=pb: e.activation(
            out=ystage[:], in_=bank_bf(pb)[:, 0:1024], func=AF.Identity),
            after=[tts, t_yst])
        bank_free7[pb] = tyev
        t_y2ev = tyev
        t_yst = p.op("sy", lambda e, j=j: e.dma_start(
            out=y2_hbm[j].rearrange("q r c -> r q c"),
            in_=ystage[:].rearrange("p (q c) -> p q c", q=8)),
            after=[tyev], inc="S", key="ystage")

    p.barrier([t_yst])
    ph7.close()

    ph8 = contextlib.ExitStack()

    def sb8(name, shape, dt):
        return ph8.enter_context(nc.sbuf_tensor(name, shape, dt))

    y2g = [sb8(f"y2g{i}", [128, 2048], BF16) for i in range(2)]
    xr = [sb8(f"xr{i}", [128, 2048], F32) for i in range(2)]
    outb = [sb8(f"outb{i}", [128, 2048], F32) for i in range(2)]
    t_fin = []
    t_outst = [None, None]
    for qt in range(NQ):
        par = qt % 2
        tg1 = p.op("sy", lambda e, qt=qt, par=par: e.dma_start(
            out=y2g[par][:].rearrange("p (jj c) -> p jj c", jj=NKC),
            in_=y2_hbm[:, qt].rearrange("jj r c -> r jj c")),
            after=[t_outst[par]], inc="L", key=f"y2g{par}")
        tg2 = p.op("sy", lambda e, qt=qt, par=par: e.dma_start(
            out=xr[par][:], in_=io["xnew_hbm"][qt * 128:(qt + 1) * 128, :]),
            after=[t_xst], inc="L", key=f"xr{par}")
        tadd = p.op("ve", lambda e, par=par: e.tensor_tensor(
            out=outb[par][:], in0=xr[par][:], in1=y2g[par][:], op=OP.add),
            after=[tg1, tg2])
        tst = p.op("sy", lambda e, qt=qt, par=par: e.dma_start(
            out=io["out"][qt * 128:(qt + 1) * 128, :], in_=outb[par][:]),
            after=[tadd], inc="S", key=f"outb{par}")
        t_outst[par] = tst
        t_fin.append(tst)
    ph8.close()

    p.op("sy", None, after=p.store_totals(), inc=None)

    p.play(ctx.enter_context(nc.Block()))
    return p



def _shard_inputs(inputs):
    xq = np.asarray(inputs["L0_attn_query"], dtype=np.float32)
    eye = np.eye(128, dtype=BF)
    onesm = np.ones((128, 128), dtype=BF)
    ii = np.arange(128)
    mtri = np.where(ii[None, :] <= ii[:, None], 0.0, MASK_RAW).astype(BF)
    shared = dict(
        wq=np.asarray(inputs["L0_attn_w_q"], np.float32),
        wk=np.asarray(inputs["L0_attn_w_k"], np.float32),
        wv=np.asarray(inputs["L0_attn_w_v"], np.float32),
        bq=np.asarray(inputs["L0_attn_b_q"], np.float32),
        bk=np.asarray(inputs["L0_attn_b_k"], np.float32),
        bv=np.asarray(inputs["L0_attn_b_v"], np.float32),
        wo=np.asarray(inputs["L0_attn_w_o"], np.float32),
        bo=np.asarray(inputs["L0_attn_b_o"], np.float32),
        wg=np.asarray(inputs["L0_ffn_w_gate"], np.float32),
        wu=np.asarray(inputs["L0_ffn_w_up"], np.float32),
        wd=np.asarray(inputs["L0_ffn_w_down"], np.float32),
        lnw=np.asarray(inputs["L0_ffn_ln_weight"], np.float32),
        ident_in=eye, ones_in=onesm,
    )
    in_maps = []
    for c in range(8):
        b, half = c // 2, c % 2
        own = slice(half * OWN, (half + 1) * OWN)
        oth = slice((1 - half) * OWN, (2 - half) * OWN)
        xs = np.concatenate([xq[b, own], xq[b, oth]], axis=0)
        im = dict(shared)
        im["x_seq"] = np.ascontiguousarray(xs)
        im["mtri_in"] = mtri
        im["m2c_in"] = np.full((128, 1), 0.0 if half == 1 else MASK_RAW,
                               np.float32)
        in_maps.append(im)
    return in_maps


_NC_CACHE = {}


def kernel(**inputs):
    key = "main"
    if key not in _NC_CACHE:
        _NC_CACHE[key] = build_program()
    nc = _NC_CACHE[key]
    in_maps = _shard_inputs(inputs)
    res = run_bass_kernel_spmd(nc, in_maps, core_ids=list(range(8)))
    full = np.empty((4, T, D), dtype=np.float32)
    for c in range(8):
        b, half = c // 2, c % 2
        full[b, half * OWN:(half + 1) * OWN] = res.results[c]["out"]
    return full


# version 12 (final)
# speedup vs baseline: 1.2669x; 1.1194x over previous
import numpy as np
import ml_dtypes
import contextlib
import concourse.bass as bass
import concourse.mybir as mybir
from concourse.bass_utils import run_bass_kernel_spmd

F32 = mybir.dt.float32
BF16 = mybir.dt.bfloat16
I32 = mybir.dt.int32
AX = mybir.AxisListType.X
OP = mybir.AluOpType
AF = mybir.ActivationFunctionType
BF = ml_dtypes.bfloat16

D = 2048
T = 2048
OWN = 1024
NH = 16
DH = 128
II = 8192
NKC = D // 128
NIC = II // 128
NT = T // 128
NQ = OWN // 128
SM_SCALE = float(1.0 / np.sqrt(DH))
MASK_RAW = float(-1e10 / SM_SCALE)
LN_EPS = 1e-5
RMS_EPS = 1e-6
QEPS = 1e-5

ENGINES = ("te", "ve", "se", "ge", "sy")


class Prog:

    def __init__(self, nc):
        self.nc = nc
        self.stream = {e: [] for e in ENGINES}
        self.sem_ctr = {}
        self.sem_handle = {}
        self.phase_wait = {e: [] for e in ENGINES}

    def make_sems(self, stack, names):
        self._stack = stack
        for n in names:
            self.sem_handle[n] = stack.enter_context(self.nc.semaphore(n))
            self.sem_ctr[n] = 0

    def _sem(self, name):
        if name not in self.sem_handle:
            self.sem_handle[name] = self._stack.enter_context(
                self.nc.semaphore(name))
            self.sem_ctr[name] = 0
        return name

    ENG_SEM = {"te": "T", "ve": "V", "se": "A", "ge": "G"}

    def op(self, eng, fn, after=(), inc="auto", amt=1, key=None):
        waits = list(self.phase_wait[eng]) + [t for t in after if t is not None]
        self.phase_wait[eng] = []
        tok = None
        if inc == "auto":
            inc = self.ENG_SEM.get(eng)
            assert inc is not None, "sync ops must pass inc='L'/'S' + key"
        if inc is not None:
            if inc in ("L", "S"):
                assert key is not None, "DMA op needs a buffer key"
                inc = self._sem(f"{inc}_{key}")
                amt = 16
            self.sem_ctr[inc] += amt
            tok = (inc, self.sem_ctr[inc],
                   "dma" if inc.startswith(("L_", "S_")) else eng)
        self.stream[eng].append((waits, fn, inc, amt))
        return tok

    def store_totals(self):
        return [(n, self.sem_ctr[n], "dma") for n in self.sem_ctr
                if n.startswith("S_")]

    def barrier(self, tokens):
        toks = [t for t in tokens if t is not None]
        for e in ENGINES:
            self.phase_wait[e].extend(toks)

    def play(self, block):
        secs = {"te": block.tensor, "ve": block.vector, "se": block.scalar,
                "ge": block.gpsimd, "sy": block.sync}
        for ename in ENGINES:
            ops = self.stream[ename]
            if not ops:
                continue

            def body(e, ops=ops, ename=ename):
                waited = {}
                for waits, fn, inc, amt in ops:
                    for (sname, val, peng) in waits:
                        if waited.get(sname, 0) >= val:
                            continue
                        e.wait_ge(self.sem_handle[sname], val)
                        waited[sname] = val
                    if fn is None:
                        continue
                    inst = fn(e)
                    if inc is not None:
                        inst.then_inc(self.sem_handle[inc], amt)
            secs[ename](body)


def build_program(debug=()):
    nc = bass.Bass("TRN2", target_bir_lowering=False,
                   detect_race_conditions=False,
                   dynamic_dma_scratch_size=8192)

    def din(name, shape, dt):
        return nc.dram_tensor(name, shape, dt, kind="ExternalInput").ap()

    def dout(name, shape, dt):
        return nc.dram_tensor(name, shape, dt, kind="ExternalOutput").ap()

    dbgset = set(debug)

    def dint(name, shape, dt):
        kind = "ExternalOutput" if name in dbgset else "Internal"
        return nc.dram_tensor(name, shape, dt, kind=kind).ap()

    io = dict(
        x_seq=din("x_seq", [T, D], F32),
        mtri=din("mtri_in", [128, 128], BF16),
        m2c=din("m2c_in", [128, 1], F32),
        wq=din("wq", [D, D], F32), wk=din("wk", [D, D], F32),
        wv=din("wv", [D, D], F32),
        bq=din("bq", [D], F32), bk=din("bk", [D], F32), bv=din("bv", [D], F32),
        wo=din("wo", [D, D], F32), bo=din("bo", [D], F32),
        wg=din("wg", [II, D], F32), wu=din("wu", [II, D], F32),
        wd=din("wd", [D, II], F32), lnw=din("lnw", [II], F32),
        identd=din("ident_in", [128, 128], BF16),
        onesd=din("ones_in", [128, 128], BF16),
        out=dout("out", [OWN, D], F32),
        kT_hbm=dint("kT_hbm", [D, T], BF16),
        vT_hbm=dint("vT_hbm", [D, T], BF16),
        qT_hbm=dint("qT_hbm", [D, OWN], BF16),
        aoT_hbm=dint("aoT_hbm", [D, OWN], BF16),
        y1_hbm=dint("y1_hbm", [NKC, NQ, 128, 128], BF16),
        xnew_hbm=dint("xnew_hbm", [OWN, D], F32),
        y2_hbm=dint("y2_hbm", [NKC, NQ, 128, 128], BF16),
    )
    io["debug"] = set(debug)
    io["dbg"] = {}
    io["dout"] = dout

    with contextlib.ExitStack() as ctx:
        _build(nc, ctx, io)
    return nc


def _build(nc, ctx, io):
    p = Prog(nc)
    p.make_sems(ctx, ["L", "S", "T", "V", "A", "G"])

    def sb(name, shape, dt):
        return ctx.enter_context(nc.sbuf_tensor(name, shape, dt))

    def psum(name, shape, dt):
        return ctx.enter_context(nc.psum_tensor(name, shape, dt))

    ident = sb("ident", [128, 128], BF16)
    ones = sb("ones", [128, 128], BF16)
    mtri_b = sb("mtri_b", [128, 128], BF16)
    m2b = sb("m2b", [128, 1], F32)
    bqc = sb("bqc", [128, NKC], F32)
    bkc = sb("bkc", [128, NKC], F32)
    bvc = sb("bvc", [128, NKC], F32)
    lnwc = sb("lnwc", [128, NIC], F32)
    ssacc = sb("ssacc", [128, 1024], F32)
    biga = sb("biga", [128, NIC * OWN], BF16)
    qx2T = sb("qx2T", [128, NKC * OWN], BF16)
    cmean = sb("cmean", [128, 8], F32)
    thc = sb("thc", [128, 8], F32)
    dqb = sb("dqb", [128, 1024], F32)

    def lnT(k):
        return biga[:, k * T:(k + 1) * T]

    def attnO(k):
        return biga[:, k * OWN:(k + 1) * OWN]

    def hfm(j):
        return biga[:, j * OWN:(j + 1) * OWN]

    def qx2Tc(k):
        return qx2T[:, k * OWN:(k + 1) * OWN]

    psA = psum("psA", [128, 2048], F32)
    psB = psum("psB", [128, 1024], F32)
    psC = psum("psC", [128, 1024], BF16)
    psD = psum("psD", [128, 1024], BF16)

    def bank(i):
        if i < 4:
            return psA[:, i * 512:(i + 1) * 512]
        assert i < 6
        return psB[:, (i - 4) * 512:(i - 3) * 512]

    def bank_bf(i):
        return (psC if i == 6 else psD)[:]

    x = io["x_seq"]

    t_id = p.op("sy", lambda e: e.dma_start(out=ident[:], in_=io["identd"]),
                inc="L", amt=16, key="ident")
    p.op("sy", lambda e: e.dma_start(out=ones[:], in_=io["onesd"]),
         inc="L", amt=16, key="ones")
    t_consts = [t_id]
    for nm2, (dst, src) in (("bqc", (bqc, io["bq"])), ("bkc", (bkc, io["bk"])),
                            ("bvc", (bvc, io["bv"]))):
        t_consts.append(p.op("sy", lambda e, d=dst, s=src: e.dma_start(
            out=d[:], in_=s.rearrange("(c p) -> p c", p=128),
            allow_slow_non_contiguous=True), inc="L", amt=16, key=nm2))
    t_consts.append(p.op("sy", lambda e: e.dma_start(
        out=lnwc[:], in_=io["lnw"].rearrange("(c p) -> p c", p=128),
        allow_slow_non_contiguous=True), inc="L", amt=16, key="lnwc"))
    t_mtri = p.op("sy", lambda e: e.dma_start(out=mtri_b[:], in_=io["mtri"]),
                  inc="L", amt=16, key="mtri")
    t_m2 = p.op("sy", lambda e: e.dma_start(out=m2b[:], in_=io["m2c"]),
                inc="L", amt=16, key="m2b")

    ph1 = contextlib.ExitStack()

    def sb1(name, shape, dt):
        return ph1.enter_context(nc.sbuf_tensor(name, shape, dt))

    xbuf = [sb1(f"xbuf{i}", [128, D], F32) for i in range(2)]
    xsq = sb1("xsq", [128, D], BF16)
    lntile = [sb1(f"lntile{i}", [128, D], BF16) for i in range(2)]
    st1 = sb1("st1", [128, 16], F32)
    xload = [None, None]
    lnfree = [None, None]
    stat_free = [None, None]
    sqrt_free = [None, None]
    bank_free = {6: None, 7: None}
    t_lnT_done = None

    for tt in range(NT):
        bi = tt % 2
        c0 = 8 * bi
        tl = p.op("sy", lambda e, b=bi, t=tt: e.dma_start(
            out=xbuf[b][:], in_=x[t * 128:(t + 1) * 128, :]),
            after=[xload[bi]] + t_consts, inc="L", amt=16, key=f"xbuf{bi}")
        tv1 = p.op("ve", lambda e, b=bi, c=c0: e.reduce_sum(
            out=st1[:, c:c + 1], in_=xbuf[b][:], axis=AX), after=[tl])
        ta1 = p.op("se", lambda e, b=bi, c=c0: e.activation(
            out=xsq[:], in_=xbuf[b][:], func=AF.Square,
            accum_out=st1[:, c + 1:c + 2]),
            after=[tl, stat_free[bi]])
        tmean = p.op("ve", lambda e, c=c0: e.tensor_scalar(
            out=st1[:, c + 2:c + 3], in0=st1[:, c:c + 1], scalar1=1.0 / D,
            scalar2=None, op0=OP.mult), after=[tv1])
        tm2 = p.op("ve", lambda e, c=c0: e.tensor_tensor(
            out=st1[:, c + 3:c + 4], in0=st1[:, c + 2:c + 3],
            in1=st1[:, c + 2:c + 3], op=OP.mult), after=[tmean])
        tvar = p.op("ve", lambda e, c=c0: e.scalar_tensor_tensor(
            out=st1[:, c + 4:c + 5], in0=st1[:, c + 1:c + 2], scalar=1.0 / D,
            in1=st1[:, c + 3:c + 4], op0=OP.mult, op1=OP.subtract),
            after=[ta1, tm2])
        stat_free[bi] = tvar
        teps = p.op("ve", lambda e, c=c0: e.tensor_scalar(
            out=st1[:, c + 5:c + 6], in0=st1[:, c + 4:c + 5], scalar1=LN_EPS,
            scalar2=None, op0=OP.add), after=[tvar])
        tv2 = p.op("ve", lambda e, c=c0: e.reciprocal(
            out=st1[:, c + 6:c + 7], in_=st1[:, c + 5:c + 6]), after=[teps])
        ta2 = p.op("se", lambda e, c=c0: e.activation(
            out=st1[:, c + 7:c + 8], in_=st1[:, c + 6:c + 7], func=AF.Sqrt),
            after=[tv2, sqrt_free[bi]])
        tv3 = p.op("ve", lambda e, b=bi, c=c0: e.tensor_scalar(
            out=lntile[b][:], in0=xbuf[b][:], scalar1=st1[:, c + 2:c + 3],
            scalar2=st1[:, c + 7:c + 8], op0=OP.subtract, op1=OP.mult),
            after=[ta2, lnfree[bi]])
        xload[bi] = tv3
        sqrt_free[bi] = tv3
        last_tr = None
        for kg in range(4):
            pb = 6 + (kg % 2)
            tts = None
            for k4 in range(4):
                k = kg * 4 + k4
                tts = p.op("te", lambda e, b=bi, kk=k, k4=k4, pb=pb: e.transpose(
                    out=bank_bf(pb)[:, k4 * 128:(k4 + 1) * 128],
                    in_=lntile[b][:, kk * 128:(kk + 1) * 128],
                    identity=ident[:]),
                    after=([tv3, t_id, bank_free[pb]] if k4 == 0 else []),
                    inc="T" if k4 == 3 else None)
            base = kg * 4 * T + tt * 128
            dst3 = biga[:, base:base + 4 * T].rearrange(
                "p (c q) -> p c q", c=4)[:, :, 0:128]
            tev = p.op("ve", lambda e, d=dst3, pb=pb: e.tensor_copy(
                out=d, in_=bank_bf(pb)[:, 0:512]), after=[tts], inc="V")
            bank_free[pb] = tev
            last_tr = tts
            t_lnT_done = tev
        lnfree[bi] = last_tr

    p.barrier([t_lnT_done, bank_free[6], bank_free[7]])

    if "lnT_dbg" in io["debug"]:
        lnd = io["dout"]("lnT_dbg", [D, T], BF16)
        for k in range(NKC):
            p.op("sy", lambda e, k=k: e.dma_start(
                out=lnd[k * 128:(k + 1) * 128, :], in_=lnT(k)), inc="S", amt=16, key="dbg")

    ph1.close()

    ph2 = contextlib.ExitStack()

    def sb2(name, shape, dt):
        return ph2.enter_context(nc.sbuf_tensor(name, shape, dt))

    wcol = [sb2(f"wcol{i}", [128, D], F32) for i in range(2)]
    wbf = [sb2(f"wbf{i}", [128, D], BF16) for i in range(2)]
    stage = [sb2(f"stage{i}", [128, T], BF16) for i in range(2)]
    wload = [None, None]
    wbf_free = [None, None]
    stage_free = [None, None]
    grp_free = {"A": None, "B": None}
    grp_free2 = {"A": None, "B": None}
    t_p2_done = []

    projs = [
        ("q", io["wq"], bqc, io["qT_hbm"], OWN),
        ("k", io["wk"], bkc, io["kT_hbm"], T),
        ("v", io["wv"], bvc, io["vT_hbm"], T),
    ]
    ji = 0
    for (pname, wdram, bcol, ohbm, tspan) in projs:
        nb = tspan // 512
        for j in range(NKC):
            par = ji % 2
            ji += 1
            wsrc = wdram[:, j * 128:(j + 1) * 128].rearrange(
                "(kc p) c -> p kc c", p=128)
            tl = p.op("sy", lambda e, par=par, s=wsrc: e.dma_start(
                out=wcol[par][:].rearrange("p (kc c) -> p kc c", kc=NKC),
                in_=s), after=[wload[par]], inc="L", amt=16, key=f"wcol{par}")
            tc = p.op("ge", lambda e, par=par: e.tensor_copy(
                out=wbf[par][:], in_=wcol[par][:]),
                after=[tl, wbf_free[par]], inc="G")
            wload[par] = tc
            if nb == 4:
                bset, gk = [0, 1, 2, 3], "A"
            else:
                bset, gk = ([0, 1], "A") if par == 0 else ([4, 5], "B")
            tmm = None
            for k in range(NKC):
                for tb in range(nb):
                    tmm = p.op("te", lambda e, par=par, k=k, tb=tb, bset=bset: e.matmul(
                        out=bank(bset[tb]),
                        lhsT=wbf[par][:, k * 128:(k + 1) * 128],
                        rhs=lnT(k)[:, tb * 512:(tb + 1) * 512],
                        start=(k == 0), stop=(k == NKC - 1)),
                        after=([tc, grp_free[gk], grp_free2[gk]]
                               if k == 0 and tb == 0 else
                               [tc] if k == 0 else []),
                        inc="T" if (k == NKC - 1 and tb == nb - 1) else None)
            wbf_free[par] = tmm
            tevs2 = []
            for tb in range(nb):
                eng = "ve" if (tb % 2 == 0) else "se"
                if eng == "ve":
                    tevs2.append(p.op("ve", lambda e, par=par, tb=tb, bset=bset, j=j, bcol=bcol: e.tensor_scalar(
                        out=stage[par][:, tb * 512:(tb + 1) * 512],
                        in0=bank(bset[tb]), scalar1=bcol[:, j:j + 1],
                        scalar2=None, op0=OP.add),
                        after=[tmm, stage_free[par]], inc="V"))
                else:
                    tevs2.append(p.op("se", lambda e, par=par, tb=tb, bset=bset, j=j, bcol=bcol: e.activation(
                        out=stage[par][:, tb * 512:(tb + 1) * 512],
                        in_=bank(bset[tb]), func=AF.Identity,
                        bias=bcol[:, j:j + 1]),
                        after=[tmm, stage_free[par]], inc="A"))
            grp_free[gk] = tevs2[-2] if len(tevs2) >= 2 else tevs2[-1]
            grp_free2[gk] = tevs2[-1]
            tst = p.op("se", lambda e, par=par, j=j, ohbm=ohbm, tspan=tspan: e.dma_start(
                out=ohbm[j * 128:(j + 1) * 128, :],
                in_=stage[par][:, 0:tspan]),
                after=tevs2[-2:], inc="S", amt=16, key=f"stage{par}")
            stage_free[par] = tst
            t_p2_done.append(tst)

    p.barrier(t_p2_done[-2:])

    ph2.close()

    ph3 = contextlib.ExitStack()

    def sb3(name, shape, dt):
        return ph3.enter_context(nc.sbuf_tensor(name, shape, dt))

    kh = [sb3(f"kh{i}", [128, T], BF16) for i in range(2)]
    vTh0 = sb3("vTh0", [128, T], BF16)
    vTh = [vTh0, vTh0]
    qh = [sb3(f"qh{i}", [128, OWN], BF16) for i in range(2)]
    vtok = sb3("vtok", [128, T], BF16)
    pbuf = sb3("pbuf", [128, T], BF16)
    pT = sb3("pT", [128, NT * 512], BF16)
    smax = sb3("smax", [128, 6], F32)
    ssum = sb3("ssum", [128, 4], F32)

    pT3 = pT[:].rearrange("p (kt q) -> p kt q", kt=NT)
    if "vtok_dbg" in io["debug"]:
        io["dbg"]["vtok_dbg"] = io["dout"]("vtok_dbg", [128, T], BF16)
        io["dbg"]["vtok2_dbg"] = io["dout"]("vtok2_dbg", [128, T], BF16)
        io["dbg"]["pT_dbg"] = io["dout"]("pT_dbg", [128, 2 * NT * 512], BF16)
    t_kh_used = [None, None]
    t_vTh_used = [None, None]
    t_qh_used = [None, None]
    t_vtok_evac = None
    t_pv = [None, None]
    t_pv_guard = None
    t_oevac_prev = None
    t_exp_prev = None
    t_soft_prev = None
    t_ptr_last_prev = None
    bank_free3 = {6: None, 7: None}

    for h in range(NH):
        par = h % 2
        tkl = p.op("sy", lambda e, par=par, h=h: e.dma_start(
            out=kh[par][:], in_=io["kT_hbm"][h * 128:(h + 1) * 128, :]),
            after=[t_kh_used[par]], inc="L", key=f"kh{par}")
        tvl = p.op("sy", lambda e, par=par, h=h: e.dma_start(
            out=vTh[par][:], in_=io["vT_hbm"][h * 128:(h + 1) * 128, :]),
            after=[t_vTh_used[0], t_vTh_used[1]], inc="L", key="vTh")
        tql = p.op("sy", lambda e, par=par, h=h: e.dma_start(
            out=qh[par][:], in_=io["qT_hbm"][h * 128:(h + 1) * 128, :]),
            after=[t_qh_used[par]], inc="L", key=f"qh{par}")
        t_vtr = None
        for kg in range(4):
            pb = 6 + (kg % 2)
            tts = None
            for k4 in range(4):
                kt = kg * 4 + k4
                tts = p.op("te", lambda e, par=par, kt=kt, k4=k4, pb=pb: e.transpose(
                    out=bank_bf(pb)[:, k4 * 128:(k4 + 1) * 128],
                    in_=vTh[par][:, kt * 128:(kt + 1) * 128],
                    identity=ident[:]),
                    after=([tvl, bank_free3[pb]] if k4 == 0 else []),
                    inc="T" if k4 == 3 else None)
            base = kg * 4 * 128
            if kg % 2 == 0:
                tev = p.op("ve", lambda e, base=base, pb=pb: e.tensor_copy(
                    out=vtok[:, base:base + 512], in_=bank_bf(pb)[:, 0:512]),
                    after=[tts, t_pv[0], t_pv[1]] if kg == 0 else [tts])
            else:
                tev = p.op("se", lambda e, base=base, pb=pb: e.activation(
                    out=vtok[:, base:base + 512], in_=bank_bf(pb)[:, 0:512],
                    func=AF.Identity),
                    after=[tts, t_pv[0], t_pv[1]] if kg == 1 else [tts])
            bank_free3[pb] = tev
            t_vtr = tev
        t_vTh_used[0] = t_vtr
        t_vTh_used[1] = t_vtr
        t_vtok_evac = t_vtr
        if h == 0 and "vtok_dbg" in io["debug"]:
            p.op("sy", lambda e: e.dma_start(
                out=io["dbg"]["vtok_dbg"], in_=vtok[:]),
                after=[t_vtr], inc="S", key="dbg")

        t_qk_last = None
        for qb in range(2):
            t_ptr_evacs = []
            for q4 in range(4):
                qt = qb * 4 + q4
                vis1 = list(range(qt + 1))
                nv1 = qt + 1
                tqk = None
                first = True
                for b in range(4):
                    if b < 2:
                        lo = b * 4
                        if lo > qt:
                            continue
                        w = (min(qt, lo + 3) - lo + 1) * 128
                        has_diag = lo <= qt <= lo + 3
                        deps = [tkl, tql, t_exp_prev] if first else []
                        if first and h == 0 and qt == 0:
                            deps = deps + [t_mtri, t_m2]
                        tq_ = p.op("te", lambda e, par=par, qt=qt, b=b, lo=lo, w=w, has_diag=has_diag: e.matmul(
                            out=psA[:, lo * 128:lo * 128 + w],
                            lhsT=qh[par][:, qt * 128:(qt + 1) * 128],
                            rhs=kh[par][:, lo * 128:lo * 128 + w],
                            start=True, stop=not has_diag),
                            after=deps,
                            inc=None if has_diag else "T")
                        if has_diag:
                            tqk = p.op("te", lambda e, qt=qt: e.matmul(
                                out=psA[:, qt * 128:(qt + 1) * 128],
                                lhsT=ident[:], rhs=mtri_b[:],
                                start=False, stop=True), inc="T")
                        else:
                            tqk = tq_
                        first = False
                    else:
                        tqk = p.op("te", lambda e, par=par, qt=qt, b=b: e.matmul(
                            out=bank(b),
                            lhsT=qh[par][:, qt * 128:(qt + 1) * 128],
                            rhs=kh[par][:, b * 512:(b + 1) * 512],
                            start=True, stop=True), inc="T" if b == 3 else None)
                t_qk_last = tqk
                ea = nv1 * 128
                tmxA = p.op("ve", lambda e, ea=ea: e.reduce_max(
                    out=smax[:, 0:1], in_=psA[:, 0:ea], axis=AX),
                    after=[tqk, t_soft_prev])
                tmxB = p.op("ve", lambda e: e.reduce_max(
                    out=smax[:, 2:3], in_=psA[:, 1024:2048], axis=AX))
                tmxC = p.op("ve", lambda e: e.tensor_tensor(
                    out=smax[:, 0:1], in0=smax[:, 0:1], in1=smax[:, 2:3],
                    op=OP.max), after=[tmxA, tmxB])
                tnb = p.op("ve", lambda e: e.tensor_scalar(
                    out=smax[:, 1:2], in0=smax[:, 0:1], scalar1=-SM_SCALE,
                    scalar2=None, op0=OP.mult), after=[tmxC])
                tnb2a = p.op("ve", lambda e: e.tensor_tensor(
                    out=smax[:, 4:5], in0=m2b[:], in1=smax[:, 0:1],
                    op=OP.subtract), after=[tmxC])
                tnb2 = p.op("ve", lambda e: e.tensor_scalar(
                    out=smax[:, 3:4], in0=smax[:, 4:5], scalar1=SM_SCALE,
                    scalar2=None, op0=OP.mult), after=[tnb2a])
                t_expA = p.op("se", lambda e, ea=ea: e.activation(
                    out=pbuf[:, 0:ea], in_=psA[:, 0:ea], func=AF.Exp,
                    bias=smax[:, 1:2], scale=SM_SCALE,
                    accum_out=ssum[:, 0:1]),
                    after=[tnb, t_ptr_last_prev, t_soft_prev])
                t_exp = p.op("se", lambda e: e.activation(
                    out=pbuf[:, 1024:2048], in_=psA[:, 1024:2048],
                    func=AF.Exp, bias=smax[:, 3:4], scale=SM_SCALE,
                    accum_out=ssum[:, 2:3]), after=[tnb2])
                t_exp_prev = t_exp
                tsc = p.op("ve", lambda e: e.tensor_tensor(
                    out=ssum[:, 0:1], in0=ssum[:, 0:1], in1=ssum[:, 2:3],
                    op=OP.add), after=[t_expA, t_exp])
                trc = p.op("ve", lambda e: e.reciprocal(
                    out=ssum[:, 1:2], in_=ssum[:, 0:1]), after=[tsc])
                tpnA = p.op("se", lambda e, ea=ea: e.activation(
                    out=pbuf[:, 0:ea], in_=pbuf[:, 0:ea], func=AF.Identity,
                    scale=ssum[:, 1:2]), after=[trc])
                tpn = p.op("se", lambda e: e.activation(
                    out=pbuf[:, 1024:2048], in_=pbuf[:, 1024:2048],
                    func=AF.Identity, scale=ssum[:, 1:2]), after=[tpnA])
                t_soft_prev = tpn
                vis_groups = []
                for lo0 in range(0, nv1, 4):
                    vis_groups.append((lo0, min(4, nv1 - lo0)))
                vis_groups += [(8, 4), (12, 4)]
                tptr = None
                for gi, (lo0, n4) in enumerate(vis_groups):
                    pb = 6 + (gi % 2)
                    tts = None
                    for k4 in range(n4):
                        kt = lo0 + k4
                        tts = p.op("te", lambda e, kt=kt, k4=k4, pb=pb: e.transpose(
                            out=bank_bf(pb)[:, k4 * 128:(k4 + 1) * 128],
                            in_=pbuf[:, kt * 128:(kt + 1) * 128],
                            identity=ident[:]),
                            after=([tpnA if lo0 < 8 else tpn,
                                    bank_free3[pb]] if k4 == 0 else []),
                            inc="T" if k4 == n4 - 1 else None)
                    dst3 = pT3[:, lo0:lo0 + n4, q4 * 128:(q4 + 1) * 128]
                    if gi % 2 == 0:
                        tev = p.op("ve", lambda e, d=dst3, pb=pb, n4=n4: e.tensor_copy(
                            out=d, in_=bank_bf(pb)[:, 0:n4 * 128]),
                            after=[tts, t_pv_guard] if gi == 0 else [tts])
                    else:
                        tev = p.op("se", lambda e, d=dst3, pb=pb, n4=n4: e.activation(
                            out=d, in_=bank_bf(pb)[:, 0:n4 * 128],
                            func=AF.Identity),
                            after=[tts, t_pv_guard] if gi == 1 else [tts])
                    bank_free3[pb] = tev
                    t_ptr_evacs.append(tev)
                    tptr = tts
                t_ptr_last_prev = tptr
            pv_list = [(kt, max(0, kt - qb * 4) * 128)
                       for kt in range(qb * 4 + 4)]
            pv_list += [(kt, 0) for kt in range(8, 16)]
            n_pv = len(pv_list)
            tpv = None
            for i, (kt, off) in enumerate(pv_list):
                tpv = p.op("te", lambda e, kt=kt, qb=qb, off=off, i=i, n_pv=n_pv: e.matmul(
                    out=psB[:, qb * 512 + off:(qb + 1) * 512],
                    lhsT=vtok[:, kt * 128:(kt + 1) * 128],
                    rhs=pT[:, kt * 512 + off:(kt + 1) * 512],
                    start=(i == 0), stop=(i == n_pv - 1)),
                    after=([t_vtok_evac, t_oevac_prev] + t_ptr_evacs)
                    if i == 0 else [],
                    inc="T" if i == n_pv - 1 else None)
            t_pv[qb] = tpv
            t_pv_guard = tpv
            if h == 0 and "vtok_dbg" in io["debug"]:
                p.op("sy", lambda e, qb=qb: e.dma_start(
                    out=io["dbg"]["pT_dbg"][:, qb * 8192:(qb + 1) * 8192],
                    in_=pT[:]), after=t_ptr_evacs[-1:], inc="S", key="dbg")
                p.op("sy", lambda e, qb=qb: e.dma_start(
                    out=io["dbg"]["vtok2_dbg"], in_=vtok[:]),
                    after=[tpv], inc="S", key="dbg")
        toe = p.op("ve", lambda e, h=h: e.tensor_copy(
            out=attnO(h), in_=psB[:]), after=[t_pv[0], t_pv[1]])
        t_oevac_prev = toe
        t_kh_used[par] = t_qk_last
        t_qh_used[par] = t_qk_last

    p.barrier([t_oevac_prev])

    if "aoT_dbg" in io["debug"]:
        aod = io["dout"]("aoT_dbg", [D, OWN], BF16)
        for k in range(NKC):
            p.op("sy", lambda e, k=k: e.dma_start(
                out=aod[k * 128:(k + 1) * 128, :], in_=attnO(k)), inc="S", key="dbg")

    ph3.close()

    ph4 = contextlib.ExitStack()

    def sb4(name, shape, dt):
        return ph4.enter_context(nc.sbuf_tensor(name, shape, dt))

    worow = sb4("worow", [128, 1024], F32)
    onesf = sb4("onesf", [1, 128], F32)
    st4 = sb4("st4", [128, 16], F32)
    xtile = sb4("xtile", [128, D], F32)
    xnb = sb4("xnb", [128, D], F32)
    ln2b = sb4("ln2b", [128, D], F32)
    vb4 = sb4("vb4", [128, D], F32)
    gb4 = sb4("gb4", [128, D], I32)
    bo_row = worow[0:1, :]
    wob = [biga[:, (16 + 2 * k) * OWN:(18 + 2 * k) * OWN] for k in range(NKC)]
    bo_b = biga[:, 48 * OWN:50 * OWN]
    cb1 = biga[:, 50 * OWN:52 * OWN]
    cb2 = biga[:, 52 * OWN:54 * OWN]
    qxb = biga[:, 54 * OWN:56 * OWN]
    sqd = biga[:, 56 * OWN:58 * OWN]

    qx2T3 = qx2T[:].rearrange("p (k t) -> p k t", k=NKC)

    t_of = p.op("ve", lambda e: e.memset(onesf[:], 1.0))
    tmmb = None
    for hb in range(2):
        t_bol = p.op("sy", lambda e, hb=hb: e.dma_start(
            out=bo_row, in_=io["bo"][None, hb * 1024:(hb + 1) * 1024]),
            after=[tmmb], inc="L", key="worow")
        for i in range(2):
            tmmb = p.op("te", lambda e, hb=hb, i=i: e.matmul(
                out=psA[:, hb * 1024 + i * 512:hb * 1024 + (i + 1) * 512],
                lhsT=onesf[:], rhs=bo_row[:, i * 512:(i + 1) * 512],
                start=True, stop=True),
                after=[t_bol, t_of] if i == 0 else [],
                inc="T" if i == 1 else None)
    t_bob = p.op("ve", lambda e: e.tensor_copy(out=bo_b, in_=psA[:]),
                 after=[tmmb])

    t_qx2ev = []
    bank_free4 = {6: None, 7: None}
    t_psA_free = t_bob
    t_worow_free = tmmb
    t_xst = None
    t_xtile_used = None
    t_stat_used = None
    t_xnb_used = None
    t_st4sq_used = None
    t_qxb_free = None
    tcache = [None] * NKC

    for qt in range(NQ):
        tmm_last = None
        for k in range(NKC):
            if qt == 0:
                for hb in range(2):
                    twl = p.op("sy", lambda e, k=k, hb=hb: e.dma_start(
                        out=worow[:],
                        in_=io["wo"][k * 128:(k + 1) * 128,
                                     hb * 1024:(hb + 1) * 1024]),
                        after=[t_worow_free], inc="L", key="worow")
                    twc = p.op("ge", lambda e, k=k, hb=hb: e.tensor_copy(
                        out=wob[k][:, hb * 1024:(hb + 1) * 1024],
                        in_=worow[:]), after=[twl])
                    t_worow_free = twc
                tcache[k] = t_worow_free
            for tb in range(4):
                deps = []
                if tb == 0 and qt == 0:
                    deps = [tcache[k]] + ([t_psA_free] if k == 0 else [])
                elif tb == 0 and k == 0:
                    deps = [t_psA_free]
                tmm_last = p.op("te", lambda e, k=k, qt=qt, tb=tb: e.matmul(
                    out=bank(tb),
                    lhsT=attnO(k)[:, qt * 128:(qt + 1) * 128],
                    rhs=wob[k][:, tb * 512:(tb + 1) * 512],
                    start=(k == 0), stop=(k == NKC - 1)),
                    after=deps,
                    inc="T" if (k == NKC - 1 and tb == 3) else None)
        txl = p.op("sy", lambda e, qt=qt: e.dma_start(
            out=xtile[:], in_=x[qt * 128:(qt + 1) * 128, :]),
            after=[t_xtile_used], inc="L", key="xtile")
        txb = p.op("ve", lambda e: e.tensor_tensor(
            out=xtile[:], in0=xtile[:], in1=bo_b, op=OP.add),
            after=[txl, t_bob])
        p.op("ve", lambda e: e.tensor_tensor(
            out=xnb[:, 0:1024], in0=psA[:, 0:1024],
            in1=xtile[:, 0:1024], op=OP.add),
            after=[tmm_last, txb, t_xnb_used, t_xst])
        tev = p.op("ve", lambda e: e.tensor_tensor(
            out=xnb[:, 1024:2048], in0=psA[:, 1024:2048],
            in1=xtile[:, 1024:2048], op=OP.add))
        t_psA_free = tev
        t_xtile_used = tev
        t_xst = p.op("se", lambda e, qt=qt: e.dma_start(
            out=io["xnew_hbm"][qt * 128:(qt + 1) * 128, :], in_=xnb[:]),
            after=[tev], inc="S", key="xnb")
        tv1 = p.op("ve", lambda e: e.reduce_sum(
            out=st4[:, 0:1], in_=xnb[:], axis=AX), after=[tev, t_stat_used])
        ta1 = p.op("se", lambda e: e.activation(
            out=sqd[:, 0:1024], in_=xnb[:, 0:1024], func=AF.Square,
            accum_out=st4[:, 1:2]), after=[tev, t_st4sq_used])
        ta1b = p.op("se", lambda e: e.activation(
            out=sqd[:, 1024:2048], in_=xnb[:, 1024:2048], func=AF.Square,
            accum_out=st4[:, 2:3]), after=[ta1])
        tmean = p.op("ve", lambda e: e.tensor_scalar(
            out=st4[:, 3:4], in0=st4[:, 0:1], scalar1=1.0 / D,
            scalar2=None, op0=OP.mult), after=[tv1])
        tm2 = p.op("ve", lambda e: e.tensor_tensor(
            out=st4[:, 4:5], in0=st4[:, 3:4], in1=st4[:, 3:4], op=OP.mult),
            after=[tmean])
        tsq = p.op("ve", lambda e: e.tensor_tensor(
            out=st4[:, 5:6], in0=st4[:, 1:2], in1=st4[:, 2:3], op=OP.add),
            after=[ta1b, tm2])
        t_st4sq_used = tsq
        tvar = p.op("ve", lambda e: e.scalar_tensor_tensor(
            out=st4[:, 6:7], in0=st4[:, 5:6], scalar=1.0 / D,
            in1=st4[:, 4:5], op0=OP.mult, op1=OP.subtract), after=[tsq])
        teps = p.op("ve", lambda e: e.tensor_scalar(
            out=st4[:, 7:8], in0=st4[:, 6:7], scalar1=LN_EPS,
            scalar2=None, op0=OP.add), after=[tvar])
        trc2 = p.op("ve", lambda e: e.reciprocal(
            out=st4[:, 8:9], in_=st4[:, 7:8]), after=[teps])
        tsqr = p.op("se", lambda e: e.activation(
            out=st4[:, 9:10], in_=st4[:, 8:9], func=AF.Sqrt), after=[trc2])
        tln = p.op("ve", lambda e: e.tensor_scalar(
            out=ln2b[:], in0=xnb[:], scalar1=st4[:, 3:4],
            scalar2=st4[:, 9:10], op0=OP.subtract, op1=OP.mult),
            after=[tsqr])
        t_xnb_used = tln
        tam = p.op("ve", lambda e: e.reduce_max(
            out=st4[:, 10:11], in_=ln2b[:], axis=AX,
            apply_absolute_value=True), after=[tln])
        tt1 = p.op("ve", lambda e: e.tensor_scalar(
            out=st4[:, 11:12], in0=st4[:, 10:11], scalar1=QEPS,
            scalar2=None, op0=OP.max), after=[tam])
        tqr = p.op("ve", lambda e: e.reciprocal(
            out=st4[:, 12:13], in_=st4[:, 11:12]), after=[tt1])
        tdq = p.op("ve", lambda e: e.tensor_scalar(
            out=st4[:, 13:14], in0=st4[:, 11:12], scalar1=1.0 / 127.0,
            scalar2=None, op0=OP.mult), after=[tt1])
        t_stat_used = tdq
        tvv = p.op("ve", lambda e: e.tensor_scalar(
            out=vb4[:], in0=ln2b[:], scalar1=st4[:, 12:13], scalar2=127.0,
            op0=OP.mult, op1=OP.mult), after=[tqr, tln])
        tvc = p.op("ve", lambda e: e.tensor_scalar(
            out=ln2b[:], in0=vb4[:], scalar1=-128.0, scalar2=127.0,
            op0=OP.max, op1=OP.min), after=[tvv])
        tg = p.op("ve", lambda e: e.tensor_copy(
            out=gb4[:], in_=ln2b[:]), after=[tvc])
        tc1 = p.op("ve", lambda e: e.scalar_tensor_tensor(
            out=cb1, in0=gb4[:], scalar=0.5, in1=ln2b[:],
            op0=OP.add, op1=OP.is_le), after=[tg])
        tc2 = p.op("ve", lambda e: e.scalar_tensor_tensor(
            out=cb2, in0=gb4[:], scalar=-0.5, in1=ln2b[:],
            op0=OP.add, op1=OP.is_ge), after=[tc1])
        tu1 = p.op("ve", lambda e: e.scalar_tensor_tensor(
            out=vb4[:], in0=cb1, scalar=1.0, in1=gb4[:],
            op0=OP.bypass, op1=OP.add), after=[tc1])
        tu2 = p.op("ve", lambda e: e.scalar_tensor_tensor(
            out=ln2b[:], in0=cb2, scalar=-1.0, in1=vb4[:],
            op0=OP.mult, op1=OP.add), after=[tu1, tc2])
        tqx = p.op("ve", lambda e: e.tensor_scalar(
            out=qxb, in0=ln2b[:], scalar1=st4[:, 13:14], scalar2=None,
            op0=OP.mult), after=[tu2, tdq, t_qxb_free])
        for kg in range(4):
            pb = 6 + (kg % 2)
            tts = None
            for k4 in range(4):
                kk = kg * 4 + k4
                tts = p.op("te", lambda e, kk=kk, k4=k4, pb=pb: e.transpose(
                    out=bank_bf(pb)[:, k4 * 128:(k4 + 1) * 128],
                    in_=qxb[:, kk * 128:(kk + 1) * 128],
                    identity=ident[:]),
                    after=([tqx, bank_free4[pb]] if k4 == 0 else []),
                    inc="T" if k4 == 3 else None)
            dst3 = qx2T3[:, kg * 4:(kg + 1) * 4, qt * 128:(qt + 1) * 128]
            tev2 = p.op("ve", lambda e, d=dst3, pb=pb: e.tensor_copy(
                out=d, in_=bank_bf(pb)[:, 0:512]), after=[tts])
            bank_free4[pb] = tev2
            t_qx2ev.append(tev2)
        t_qxb_free = tts

    p.barrier(t_qx2ev[-2:] + [t_xst])

    if "qx2T_dbg" in io["debug"]:
        qxd = io["dout"]("qx2T_dbg", [D, OWN], BF16)
        for k in range(NKC):
            p.op("sy", lambda e, k=k: e.dma_start(
                out=qxd[k * 128:(k + 1) * 128, :], in_=qx2Tc(k)), inc="S", key="dbg")

    ph4.close()

    ph45 = contextlib.ExitStack()

    def sb45(name, shape, dt):
        return ph45.enter_context(nc.sbuf_tensor(name, shape, dt))

    wpb = [sb45(f"wpb{i}", [128, 2048], F32) for i in range(2)]
    wstat16 = sb45("wstat16", [128, 192 * 16], F32)
    wstat = sb45("wstat", [128, 192], F32)
    onesf128 = sb45("onesf128", [128, 128], F32)
    wsum = sb45("wsum", [128, 192], F32)

    blocks = []
    for mi, (wdram, nrc) in enumerate(((io["wg"], 64), (io["wu"], 64),
                                       (io["wd"], 16))):
        if mi < 2:
            for rr in range(nrc):
                blocks.append((mi, wdram[rr * 128:(rr + 1) * 128, :], mi * 64 + rr))
        else:
            for rr in range(nrc):
                for qq2 in range(4):
                    blocks.append((mi, wdram[rr * 128:(rr + 1) * 128,
                                             qq2 * 2048:(qq2 + 1) * 2048],
                                   128 + rr * 4 + qq2))
    t_wp_used = [None, None]
    t_abs = []
    for bi2, (mi, src, col) in enumerate(blocks):
        par = bi2 % 2
        twl = p.op("sy", lambda e, par=par, src=src: e.dma_start(
            out=wpb[par][:], in_=src), after=[t_wp_used[par]], inc="L", key=f"wpb{par}")
        tr = p.op("ve", lambda e, par=par, col=col: e.tensor_reduce(
            out=wstat16[:, col * 16:(col + 1) * 16],
            in_=wpb[par][:].rearrange("p (a b) -> p a b", a=16),
            axis=AX, op=OP.add, apply_absolute_value=True), after=[twl])
        t_wp_used[par] = tr
        t_abs.append(tr)
    tst2 = p.op("ve", lambda e: e.tensor_reduce(
        out=wstat[:], in_=wstat16[:].rearrange("p (a b) -> p a b", a=192),
        axis=AX, op=OP.add), after=[t_abs[-1]])
    tcast = p.op("ve", lambda e: e.memset(onesf128[:], 1.0), after=[tst2])
    tsum = p.op("te", lambda e: e.matmul(
        out=psB[:, 0:192], lhsT=onesf128[:], rhs=wstat[:],
        start=True, stop=True), after=[tcast, tst2])
    tsev = p.op("ve", lambda e: e.tensor_copy(out=wsum[:], in_=psB[:, 0:192]),
                after=[tsum])
    tprev = tsev
    for mi, (c0, ncols, numel) in enumerate(
            ((0, 64, II * D), (64, 64, II * D), (128, 64, D * II))):
        t1_ = p.op("ve", lambda e, mi=mi, c0=c0, ncols=ncols: e.reduce_sum(
            out=cmean[:, mi + 4:mi + 5], in_=wsum[:, c0:c0 + ncols], axis=AX),
            after=[tprev])
        t2_ = p.op("ve", lambda e, mi=mi, numel=numel: e.tensor_scalar(
            out=cmean[:, mi:mi + 1], in0=cmean[:, mi + 4:mi + 5],
            scalar1=1.0 / numel, scalar2=QEPS, op0=OP.mult, op1=OP.max),
            after=[t1_])
        t3_ = p.op("ve", lambda e, mi=mi: e.tensor_scalar(
            out=thc[:, mi:mi + 1], in0=cmean[:, mi:mi + 1], scalar1=0.5,
            scalar2=None, op0=OP.mult), after=[t2_])
        tprev = p.op("ve", lambda e, mi=mi: e.tensor_scalar(
            out=thc[:, mi + 4:mi + 5], in0=thc[:, mi:mi + 1], scalar1=-1.0,
            scalar2=None, op0=OP.mult), after=[t3_])
    t_scales = tprev
    p.barrier([t_scales])
    ph45.close()

    ph5 = contextlib.ExitStack()

    def sb5(name, shape, dt):
        return ph5.enter_context(nc.sbuf_tensor(name, shape, dt))

    wr5 = [[sb5(f"wr5_{m}_{i}", [128, 1024], F32) for i in range(2)]
           for m in range(2)]
    gslab = sb5("gslab", [128, 2048], BF16)
    uslab = sb5("uslab", [128, 2048], BF16)
    qu_b = sb5("qu_b", [128, 2048], BF16)
    qv_b = sb5("qv_b", [128, 2048], BF16)
    qr_b = [sb5(f"qr_b{i}", [128, 2048], BF16) for i in range(2)]
    gsilb = sb5("gsilb", [128, 1024], BF16)
    h2b = sb5("h2b", [128, 1024], BF16)

    t_wr_used = [[None, None], [None, None]]
    t_slab_used = [None, None]
    t_gu_ev = None
    t_h2 = None
    t_hfm = [None] * NIC
    bank_free5 = {6: None, 7: None}
    t_gu_mm = None
    t_ss_mm = None
    t_ss_acc = None

    t_qr_used = [None, None]
    t_sil_prev = None
    pend_ss = None

    def emit_ss(th2_tok):
        nonlocal t_ss_mm, t_ss_acc
        for tb in range(2):
            t_ss_mm = p.op("te", lambda e, tb=tb: e.matmul(
                out=psB[:, tb * 512:(tb + 1) * 512], lhsT=ones[:],
                rhs=h2b[:, tb * 512:(tb + 1) * 512],
                start=True, stop=True),
                after=[th2_tok, t_ss_acc] if tb == 0 else [],
                inc="T" if tb == 1 else None)
        if t_ss_acc is None:
            t_ss_acc = p.op("ve", lambda e: e.tensor_copy(
                out=ssacc[:], in_=psB[:]), after=[t_ss_mm])
        else:
            t_ss_acc = p.op("ve", lambda e: e.tensor_tensor(
                out=ssacc[:], in0=ssacc[:], in1=psB[:], op=OP.add),
                after=[t_ss_mm, t_ss_acc])

    def emit_quant5(j):
        toks = [None, None]
        for m, wdram in enumerate((io["wg"], io["wu"])):
            tls = []
            for hh in range(2):
                par = (2 * j + hh) % 2
                tls.append(p.op("sy", lambda e, m=m, j=j, hh=hh, par=par, w=wdram: e.dma_start(
                    out=wr5[m][par][:],
                    in_=w[j * 128:(j + 1) * 128, hh * 1024:(hh + 1) * 1024]),
                    after=[t_wr_used[m][par]], inc="L", key=f"wr5_{m}_{par}"))
            tq = None
            for hh in range(2):
                par = (2 * j + hh) % 2
                tu = p.op("ve", lambda e, m=m, hh=hh, par=par: e.tensor_scalar(
                    out=qu_b[:, hh * 1024:(hh + 1) * 1024], in0=wr5[m][par][:],
                    scalar1=thc[:, m:m + 1], scalar2=None, op0=OP.is_ge),
                    after=[tls[hh]] + ([t_qr_used[m]] if hh == 0 else []))
                tv = p.op("ve", lambda e, m=m, hh=hh, par=par: e.tensor_scalar(
                    out=qv_b[:, hh * 1024:(hh + 1) * 1024], in0=wr5[m][par][:],
                    scalar1=thc[:, m + 4:m + 5], scalar2=None, op0=OP.is_le),
                    after=[tu])
                tq = p.op("ve", lambda e, m=m, hh=hh: e.tensor_tensor(
                    out=qr_b[m][:, hh * 1024:(hh + 1) * 1024],
                    in0=qu_b[:, hh * 1024:(hh + 1) * 1024],
                    in1=qv_b[:, hh * 1024:(hh + 1) * 1024], op=OP.subtract),
                    after=[tv])
                t_wr_used[m][par] = tq
            toks[m] = tq
        return toks

    pend_quant = emit_quant5(0)
    for j in range(NIC):
        slabs = (gslab, uslab)
        tquant = [None, None]
        quant_toks = pend_quant
        if j + 1 < NIC:
            pend_quant = emit_quant5(j + 1)
        for m in range(2):
            tq = quant_toks[m]
            for kg in range(4):
                pb = 6 + (kg % 2)
                tts = None
                for k4 in range(4):
                    kk = kg * 4 + k4
                    tts = p.op("te", lambda e, m=m, kk=kk, k4=k4, pb=pb: e.transpose(
                        out=bank_bf(pb)[:, k4 * 128:(k4 + 1) * 128],
                        in_=qr_b[m][:, kk * 128:(kk + 1) * 128],
                        identity=ident[:]),
                        after=([tq, bank_free5[pb], t_slab_used[m]]
                               if k4 == 0 else []),
                        inc="T" if k4 == 3 else None)
                tev = p.op("se", lambda e, m=m, kg=kg, pb=pb, slab=slabs[m]: e.activation(
                    out=slab[:, kg * 512:(kg + 1) * 512],
                    in_=bank_bf(pb)[:, 0:512], func=AF.Identity),
                    after=[tts])
                bank_free5[pb] = tev
            t_qr_used[m] = tts
            t_slab_used[m] = None
            tquant[m] = (tq, tev)
        tmmg = None
        for k in range(NKC):
            for m in range(2):
                slab = (gslab, uslab)[m]
                for tb in range(2):
                    deps = []
                    if k == 0 and tb == 0 and m == 0:
                        deps = [tquant[0][1], t_sil_prev]
                    elif k == 0 and tb == 0 and m == 1:
                        deps = [tquant[1][1], t_gu_ev]
                    tmmg = p.op("te", lambda e, m=m, k=k, tb=tb, slab=slab: e.matmul(
                        out=psA[:, m * 1024 + tb * 512: m * 1024 + (tb + 1) * 512],
                        lhsT=slab[:, k * 128:(k + 1) * 128],
                        rhs=qx2Tc(k)[:, tb * 512:(tb + 1) * 512],
                        start=(k == 0), stop=(k == NKC - 1)),
                        after=deps,
                        inc="T" if (k == NKC - 1 and m == 1 and tb == 1) else None)
        t_slab_used[0] = tmmg
        t_slab_used[1] = tmmg
        if pend_ss is not None:
            emit_ss(pend_ss)
        tsil = p.op("se", lambda e: e.activation(
            out=gsilb[:], in_=psA[:, 0:1024], func=AF.Silu,
            scale=cmean[:, 0:1]), after=[tmmg, t_h2])
        t_sil_prev = tsil
        thf = p.op("ve", lambda e, j=j: e.scalar_tensor_tensor(
            out=hfm(j), in0=psA[:, 1024:2048], scalar=cmean[:, 1:2],
            in1=gsilb[:], op0=OP.mult, op1=OP.mult), after=[tsil])
        t_gu_ev = thf
        t_hfm[j] = thf
        th2 = p.op("se", lambda e, j=j: e.activation(
            out=h2b[:], in_=hfm(j), func=AF.Square), after=[thf, t_ss_mm])
        t_h2 = th2
        pend_ss = th2
    emit_ss(pend_ss)

    p.barrier([t_ss_acc, t_gu_ev])
    ph5.close()

    ph6 = contextlib.ExitStack()

    def sb6(name, shape, dt):
        return ph6.enter_context(nc.sbuf_tensor(name, shape, dt))

    rmsb = sb6("rmsb", [128, 1024], F32)
    qsb = sb6("qsb", [128, 1024], F32)
    rqb = sb6("rqb", [128, 1024], F32)
    hnb = sb6("hnb", [128, 1024], BF16)
    amax8 = sb6("amax8", [128, 16], F32)
    qsr = sb6("qsr", [8, 256], F32)
    qsrow_qs = ssacc[0:1, 0:1024]
    qsrow_dq = dqb[0:1, 0:1024]
    identf = sb6("identf", [128, 128], F32)
    onesf2 = sb6("onesf2", [1, 128], F32)
    vb = sb6("vb", [128, 1024], F32)
    vcb = sb6("vcb", [128, 1024], F32)
    gb = sb6("gb", [128, 1024], I32)
    u1b = sb6("u1b", [128, 1024], F32)
    c1b = sb6("c1b", [128, 1024], BF16)
    c2b = sb6("c2b", [128, 1024], BF16)

    tid2 = p.op("ve", lambda e: e.tensor_copy(out=identf[:], in_=ident[:]))
    tof2 = p.op("ve", lambda e: e.memset(onesf2[:], 1.0))
    trms1 = p.op("ve", lambda e: e.tensor_scalar(
        out=rmsb[:], in0=ssacc[:], scalar1=1.0 / II, scalar2=RMS_EPS,
        op0=OP.mult, op1=OP.add))
    trms0 = trms1
    trms2 = p.op("ve", lambda e: e.reciprocal(out=rmsb[:], in_=rmsb[:]),
                 after=[trms1])
    trms3 = p.op("se", lambda e: e.activation(
        out=rmsb[:], in_=rmsb[:], func=AF.Sqrt), after=[trms2])

    t_am = None
    t_hn_used = None
    bank_free6 = {6: None, 7: None}
    for j in range(NIC):
        thn = p.op("ve", lambda e, j=j: e.scalar_tensor_tensor(
            out=hnb[:], in0=hfm(j), scalar=lnwc[:, j:j + 1], in1=rmsb[:],
            op0=OP.mult, op1=OP.mult), after=[trms3, t_hn_used])
        pb = 6 + (j % 2)
        tts = None
        for k4 in range(8):
            tts = p.op("te", lambda e, k4=k4, pb=pb: e.transpose(
                out=bank_bf(pb)[:, k4 * 128:(k4 + 1) * 128],
                in_=hnb[:, k4 * 128:(k4 + 1) * 128],
                identity=ident[:]),
                after=([thn, bank_free6[pb]] if k4 == 0 else []),
                inc="T" if k4 == 7 else None)
        t_hn_used = tts
        trd = p.op("ve", lambda e, pb=pb: e.tensor_reduce(
            out=amax8[:, 8:16], in_=bank_bf(pb)[:, 0:1024].rearrange(
                "p (q c) -> p q c", q=8),
            axis=AX, op=OP.max, apply_absolute_value=True),
            after=[tts, t_am])
        bank_free6[pb] = trd
        if j == 0:
            t_am = p.op("ve", lambda e: e.tensor_copy(
                out=amax8[:, 0:8], in_=amax8[:, 8:16]), after=[trd])
        else:
            t_am = p.op("ve", lambda e: e.tensor_tensor(
                out=amax8[:, 0:8], in0=amax8[:, 0:8], in1=amax8[:, 8:16],
                op=OP.max), after=[trd, t_am])
    tq1 = p.op("ve", lambda e: e.tensor_scalar(
        out=amax8[:, 8:16], in0=amax8[:, 0:8], scalar1=QEPS, scalar2=None,
        op0=OP.max), after=[t_am])
    tq2 = p.op("ve", lambda e: e.reciprocal(
        out=amax8[:, 0:8], in_=amax8[:, 8:16]), after=[tq1])
    tq3 = p.op("ve", lambda e: e.tensor_scalar(
        out=amax8[:, 0:8], in0=amax8[:, 0:8], scalar1=127.0, scalar2=None,
        op0=OP.mult), after=[tq2])
    tq4 = p.op("ve", lambda e: e.tensor_scalar(
        out=amax8[:, 8:16], in0=amax8[:, 8:16], scalar1=1.0 / 127.0,
        scalar2=cmean[:, 2:3], op0=OP.mult, op1=OP.mult), after=[tq3])
    ttr1 = p.op("te", lambda e: e.transpose(
        out=psA[0:8, 0:128], in_=amax8[:, 0:8], identity=identf[:]),
        after=[tq3, tq4, tid2, t_hn_used, bank_free6[6], bank_free6[7]],
        inc="T")
    tev1 = p.op("ve", lambda e: e.tensor_copy(out=qsr[0:8, 0:128],
                                              in_=psA[0:8, 0:128]),
                after=[ttr1])
    ttr2 = p.op("te", lambda e: e.transpose(
        out=psA[0:8, 128:256], in_=amax8[:, 8:16], identity=identf[:]),
        after=[tev1], inc="T")
    tev2 = p.op("ve", lambda e: e.tensor_copy(out=qsr[0:8, 128:256],
                                              in_=psA[0:8, 128:256]),
                after=[ttr2])
    tgr1 = p.op("sy", lambda e: e.dma_start(
        out=qsrow_qs.rearrange("o (q c) -> o q c", q=8),
        in_=qsr[0:8, 0:128]), after=[tev1, trms1], inc="L", key="qsrow1")
    tgr2 = p.op("sy", lambda e: e.dma_start(
        out=qsrow_dq.rearrange("o (q c) -> o q c", q=8),
        in_=qsr[0:8, 128:256]), after=[tev2, trms1], inc="L", key="qsrow2")
    tb1 = None
    for tt in range(8):
        tb1 = p.op("te", lambda e, tt=tt: e.matmul(
            out=psB[:, tt * 128:(tt + 1) * 128], lhsT=onesf2[:],
            rhs=qsrow_qs[:, tt * 128:(tt + 1) * 128], start=True, stop=True),
            after=[tgr1, tof2, trms0] if tt == 0 else [],
            inc="T" if tt == 7 else None)
    tqsb = p.op("ve", lambda e: e.tensor_copy(out=qsb[:], in_=psB[:]),
                after=[tb1])
    tb2 = None
    for tt in range(8):
        tb2 = p.op("te", lambda e, tt=tt: e.matmul(
            out=psB[:, tt * 128:(tt + 1) * 128], lhsT=onesf2[:],
            rhs=qsrow_dq[:, tt * 128:(tt + 1) * 128],
            start=True, stop=True),
            after=[tgr2, tqsb] if tt == 0 else [],
            inc="T" if tt == 7 else None)
    tdqb = p.op("ve", lambda e: e.tensor_copy(out=dqb[:], in_=psB[:]),
                after=[tb2])
    trqb = p.op("ve", lambda e: e.tensor_tensor(
        out=rqb[:], in0=rmsb[:], in1=qsb[:], op=OP.mult), after=[tqsb, trms3])

    t_ch = None
    for j in range(NIC):
        tvv = p.op("ve", lambda e, j=j: e.scalar_tensor_tensor(
            out=vb[:], in0=hfm(j), scalar=lnwc[:, j:j + 1], in1=rqb[:],
            op0=OP.mult, op1=OP.mult), after=[trqb, t_ch])
        tvc = p.op("ve", lambda e: e.tensor_scalar(
            out=vcb[:], in0=vb[:], scalar1=-128.0, scalar2=127.0,
            op0=OP.max, op1=OP.min), after=[tvv])
        tg = p.op("ve", lambda e: e.tensor_copy(out=gb[:], in_=vcb[:]),
                  after=[tvc])
        tc1 = p.op("ve", lambda e: e.scalar_tensor_tensor(
            out=c1b[:], in0=gb[:], scalar=0.5, in1=vcb[:], op0=OP.add,
            op1=OP.is_le), after=[tg])
        tc2 = p.op("ve", lambda e: e.scalar_tensor_tensor(
            out=c2b[:], in0=gb[:], scalar=-0.5, in1=vcb[:], op0=OP.add,
            op1=OP.is_ge), after=[tc1])
        tu1 = p.op("ve", lambda e: e.scalar_tensor_tensor(
            out=u1b[:], in0=c1b[:], scalar=1.0, in1=gb[:], op0=OP.bypass,
            op1=OP.add), after=[tc1])
        t_ch = p.op("ve", lambda e, j=j: e.scalar_tensor_tensor(
            out=hfm(j), in0=c2b[:], scalar=-1.0, in1=u1b[:], op0=OP.mult,
            op1=OP.add), after=[tu1, tc2])

    if "ffn_dbg" in io["debug"]:
        fd = io["dout"]("ffn_dbg", [128, 4224], F32)
        p.op("sy", lambda e: e.dma_start(out=fd[:, 0:1024], in_=rmsb[:]),
             after=[t_ch], inc="S", key="dbg")
        p.op("sy", lambda e: e.dma_start(out=fd[:, 1024:2048], in_=qsb[:]),
             after=[t_ch], inc="S", key="dbg")
        p.op("sy", lambda e: e.dma_start(out=fd[:, 2048:3072], in_=dqb[:]),
             after=[t_ch], inc="S", key="dbg")
        p.op("sy", lambda e: e.dma_start(out=fd[:, 3072:3088],
                                         in_=amax8[:, 0:16]),
             after=[t_ch], inc="S", key="dbg")
        p.op("sy", lambda e: e.dma_start(out=fd[:, 3088:3096],
                                         in_=cmean[:, 0:8]),
             after=[t_ch], inc="S", key="dbg")
        p.op("sy", lambda e: e.dma_start(out=fd[:, 3096:3104],
                                         in_=thc[:, 0:8]),
             after=[t_ch], inc="S", key="dbg")
        p.op("sy", lambda e: e.dma_start(out=fd[:, 3104:4128],
                                         in_=ssacc[:]), after=[t_ch], inc="S", key="dbg")

    if "qh_dbg" in io["debug"]:
        qhd = io["dout"]("qh_dbg", [II, OWN], BF16)
        for j in range(NIC):
            p.op("sy", lambda e, j=j: e.dma_start(
                out=qhd[j * 128:(j + 1) * 128, :], in_=hfm(j)),
                after=[t_ch], inc="S", key="dbg")

    p.barrier([t_ch])
    ph6.close()

    ph7 = contextlib.ExitStack()

    def sb7(name, shape, dt):
        return ph7.enter_context(nc.sbuf_tensor(name, shape, dt))

    wdq = [sb7(f"wdq{i}", [128, 2048], F32) for i in range(2)]
    du_b = sb7("du_b", [128, 1024], BF16)
    dv_b = sb7("dv_b", [128, 1024], BF16)
    dr_b = sb7("dr_b", [128, 2048], BF16)
    dslab = sb7("dslab", [128, 8192], BF16)
    y2b = sb7("y2b", [128, 1024], BF16)
    ystage = sb7("ystage", [128, 1024], BF16)

    t_wd_used = [None, None]
    t_dq_used = None
    t_y2mm = [None, None]
    t_yst = None
    t_y2ev = None
    bank_free7 = {6: None, 7: None}
    t_slab7_used = None
    y2_hbm = io["y2_hbm"]

    for j in range(NKC):
        jp = j % 2
        tlast_ev = None
        for qq2 in range(4):
            par = (j * 4 + qq2) % 2
            tld = p.op("sy", lambda e, j=j, qq2=qq2, par=par: e.dma_start(
                out=wdq[par][:],
                in_=io["wd"][j * 128:(j + 1) * 128,
                             qq2 * 2048:(qq2 + 1) * 2048]),
                after=[t_wd_used[par]], inc="L", key=f"wdq{par}")
            tdr = None
            for hh in range(2):
                tdu = p.op("ve", lambda e, par=par, hh=hh: e.tensor_scalar(
                    out=du_b[:], in0=wdq[par][:, hh * 1024:(hh + 1) * 1024],
                    scalar1=thc[:, 2:3], scalar2=None, op0=OP.is_ge),
                    after=[tld, t_dq_used] if hh == 0 else [])
                tdv = p.op("ve", lambda e, par=par, hh=hh: e.tensor_scalar(
                    out=dv_b[:], in0=wdq[par][:, hh * 1024:(hh + 1) * 1024],
                    scalar1=thc[:, 6:7], scalar2=None, op0=OP.is_le),
                    after=[tdu])
                tdr = p.op("ve", lambda e, hh=hh: e.tensor_tensor(
                    out=dr_b[:, hh * 1024:(hh + 1) * 1024], in0=du_b[:],
                    in1=dv_b[:], op=OP.subtract), after=[tdv])
            t_wd_used[par] = tdr
            t_dq_used = tdr
            for kg in range(4):
                pb = 6 + (kg % 2)
                tts = None
                for k4 in range(4):
                    kk = kg * 4 + k4
                    tts = p.op("te", lambda e, kk=kk, k4=k4, pb=pb: e.transpose(
                        out=bank_bf(pb)[:, k4 * 128:(k4 + 1) * 128],
                        in_=dr_b[:, kk * 128:(kk + 1) * 128],
                        identity=ident[:]),
                        after=([tdr, bank_free7[pb], t_slab7_used]
                               if k4 == 0 else []),
                        inc="T" if k4 == 3 else None)
                tlast_ev = p.op("se", lambda e, qq2=qq2, kg=kg, pb=pb: e.activation(
                    out=dslab[:, qq2 * 2048 + kg * 512:qq2 * 2048 + (kg + 1) * 512],
                    in_=bank_bf(pb)[:, 0:512], func=AF.Identity), after=[tts])
                bank_free7[pb] = tlast_ev
        tdm = None
        for k in range(NIC):
            for tb in range(2):
                tdm = p.op("te", lambda e, j=j, jp=jp, k=k, tb=tb: e.matmul(
                    out=psA[:, jp * 1024 + tb * 512: jp * 1024 + (tb + 1) * 512],
                    lhsT=dslab[:, k * 128:(k + 1) * 128],
                    rhs=hfm(k)[:, tb * 512:(tb + 1) * 512],
                    start=(k == 0), stop=(k == NIC - 1)),
                    after=[tlast_ev, t_y2mm[jp]] if (k == 0 and tb == 0) else [],
                    inc="T" if (k == NIC - 1 and tb == 1) else None)
        t_slab7_used = tdm
        tyv = p.op("ve", lambda e, jp=jp: e.tensor_tensor(
            out=y2b[:], in0=psA[:, jp * 1024:(jp + 1) * 1024], in1=dqb[:],
            op=OP.mult), after=[tdm, t_yst])
        t_y2mm[jp] = tyv
        pb = 6 + (j % 2)
        tts = None
        for k4 in range(8):
            tts = p.op("te", lambda e, k4=k4, pb=pb: e.transpose(
                out=bank_bf(pb)[:, k4 * 128:(k4 + 1) * 128],
                in_=y2b[:, k4 * 128:(k4 + 1) * 128],
                identity=ident[:]),
                after=([tyv, bank_free7[pb]] if k4 == 0 else []),
                inc="T" if k4 == 7 else None)
        tyev = p.op("se", lambda e, pb=pb: e.activation(
            out=ystage[:], in_=bank_bf(pb)[:, 0:1024], func=AF.Identity),
            after=[tts, t_yst])
        bank_free7[pb] = tyev
        t_y2ev = tyev
        t_yst = p.op("se", lambda e, j=j: e.dma_start(
            out=y2_hbm[j].rearrange("q r c -> r q c"),
            in_=ystage[:].rearrange("p (q c) -> p q c", q=8)),
            after=[tyev], inc="S", key="ystage")

    p.barrier([t_yst])
    ph7.close()

    ph8 = contextlib.ExitStack()

    def sb8(name, shape, dt):
        return ph8.enter_context(nc.sbuf_tensor(name, shape, dt))

    y2g = [sb8(f"y2g{i}", [128, 2048], BF16) for i in range(2)]
    xr = [sb8(f"xr{i}", [128, 2048], F32) for i in range(2)]
    outb = [sb8(f"outb{i}", [128, 2048], F32) for i in range(2)]
    t_fin = []
    t_outst = [None, None]
    for qt in range(NQ):
        par = qt % 2
        tg1 = p.op("sy", lambda e, qt=qt, par=par: e.dma_start(
            out=y2g[par][:].rearrange("p (jj c) -> p jj c", jj=NKC),
            in_=y2_hbm[:, qt].rearrange("jj r c -> r jj c")),
            after=[t_outst[par]], inc="L", key=f"y2g{par}")
        tg2 = p.op("sy", lambda e, qt=qt, par=par: e.dma_start(
            out=xr[par][:], in_=io["xnew_hbm"][qt * 128:(qt + 1) * 128, :]),
            after=[t_xst], inc="L", key=f"xr{par}")
        tadd = p.op("ve", lambda e, par=par: e.tensor_tensor(
            out=outb[par][:], in0=xr[par][:], in1=y2g[par][:], op=OP.add),
            after=[tg1, tg2])
        tst = p.op("se", lambda e, qt=qt, par=par: e.dma_start(
            out=io["out"][qt * 128:(qt + 1) * 128, :], in_=outb[par][:]),
            after=[tadd], inc="S", key=f"outb{par}")
        t_outst[par] = tst
        t_fin.append(tst)
    ph8.close()

    p.op("sy", None, after=p.store_totals(), inc=None)

    p.play(ctx.enter_context(nc.Block()))
    return p



def _shard_inputs(inputs):
    xq = np.asarray(inputs["L0_attn_query"], dtype=np.float32)
    eye = np.eye(128, dtype=BF)
    onesm = np.ones((128, 128), dtype=BF)
    ii = np.arange(128)
    mtri = np.where(ii[None, :] <= ii[:, None], 0.0, MASK_RAW).astype(BF)
    shared = dict(
        wq=np.asarray(inputs["L0_attn_w_q"], np.float32),
        wk=np.asarray(inputs["L0_attn_w_k"], np.float32),
        wv=np.asarray(inputs["L0_attn_w_v"], np.float32),
        bq=np.asarray(inputs["L0_attn_b_q"], np.float32),
        bk=np.asarray(inputs["L0_attn_b_k"], np.float32),
        bv=np.asarray(inputs["L0_attn_b_v"], np.float32),
        wo=np.asarray(inputs["L0_attn_w_o"], np.float32),
        bo=np.asarray(inputs["L0_attn_b_o"], np.float32),
        wg=np.asarray(inputs["L0_ffn_w_gate"], np.float32),
        wu=np.asarray(inputs["L0_ffn_w_up"], np.float32),
        wd=np.asarray(inputs["L0_ffn_w_down"], np.float32),
        lnw=np.asarray(inputs["L0_ffn_ln_weight"], np.float32),
        ident_in=eye, ones_in=onesm,
    )
    in_maps = []
    for c in range(8):
        b, half = c // 2, c % 2
        own = slice(half * OWN, (half + 1) * OWN)
        oth = slice((1 - half) * OWN, (2 - half) * OWN)
        xs = np.concatenate([xq[b, own], xq[b, oth]], axis=0)
        im = dict(shared)
        im["x_seq"] = np.ascontiguousarray(xs)
        im["mtri_in"] = mtri
        im["m2c_in"] = np.full((128, 1), 0.0 if half == 1 else MASK_RAW,
                               np.float32)
        in_maps.append(im)
    return in_maps


_NC_CACHE = {}


def kernel(**inputs):
    key = "main"
    if key not in _NC_CACHE:
        _NC_CACHE[key] = build_program()
    nc = _NC_CACHE[key]
    in_maps = _shard_inputs(inputs)
    res = run_bass_kernel_spmd(nc, in_maps, core_ids=list(range(8)))
    full = np.empty((4, T, D), dtype=np.float32)
    for c in range(8):
        b, half = c // 2, c % 2
        full[b, half * OWN:(half + 1) * OWN] = res.results[c]["out"]
    return full
